# version 21
# baseline (speedup 1.0000x reference)
import numpy as np
from collections import deque
from contextlib import ExitStack

import concourse.bass as bass
import concourse.mybir as mybir
from concourse.bass_utils import run_bass_kernel_spmd
from concourse.bass_core_rust import IndirectOffsetOnAxis

F32 = mybir.dt.float32
BF16 = mybir.dt.bfloat16
I32 = mybir.dt.int32
OP = mybir.AluOpType
AF = mybir.ActivationFunctionType
AX = mybir.AxisListType

NCORES = 8
B_CORE = 2
T = 2048
NTOK = B_CORE * T
D = 1024
INNER = 2048
DSTATE = 64
NT = 512
NTILES = NTOK // NT
DCH = D // 128
ICH = INNER // 128
HID = 4096
HCH = HID // 128
NEXP = 4
FE = 16
FCAP = 256
FEXP = 2048
FSH = 2048

ENGS = ("pe", "act", "dve", "pool", "sp")
QUEUES = ("sp", "act", "pool")
NQSEM = 4


class R:
    __slots__ = ("name", "w", "readers")

    def __init__(self, name):
        self.name = name
        self.w = None
        self.readers = []


class Em:
    def __init__(self):
        self.steps = []
        self.sem_names = list(ENGS) + [f"q{q}{i}" for q in QUEUES for i in range(NQSEM)]
        self.cnt = {s: 0 for s in self.sem_names}
        self.known = {e: {} for e in ENGS}
        self.qrot = {q: 0 for q in QUEUES}

    def _dep(self, eng, waits, dep):
        if dep is None:
            return
        sem, val = dep
        k = self.known[eng]
        if k.get(sem, 0) >= val:
            return
        k[sem] = val
        waits.append((sem, val))

    def _gather(self, eng, reads, writes, accum):
        waits = []
        for r in reads:
            self._dep(eng, waits, r.w)
        for r in writes:
            if r.w is not None:
                if r.w[0] != eng and not accum:
                    self._dep(eng, waits, r.w)
            for rd in r.readers:
                if rd[0] != eng and not accum:
                    self._dep(eng, waits, rd)
        return waits

    def _commit(self, reads, writes, sem, myval):
        for r in reads:
            r.readers.append((sem, myval))
        for r in writes:
            r.w = (sem, myval)
            r.readers = []

    def step(self, eng, fn, reads=(), writes=(), accum=False, tag=""):
        waits = self._gather(eng, reads, writes, accum)
        self.cnt[eng] += 1
        myval = self.cnt[eng]
        self._commit(reads, writes, eng, myval)
        self.steps.append((eng, waits, fn, eng, 1, tag))

    def dma(self, q, fn, reads=(), writes=(), tag=""):
        waits = self._gather(q, reads, writes, False)
        i = self.qrot[q]
        self.qrot[q] = (i + 1) % NQSEM
        sem = f"q{q}{i}"
        self.cnt[sem] += 16
        myval = self.cnt[sem]
        self._commit(reads, writes, sem, myval)
        self.steps.append((q, waits, fn, sem, 16, tag))

    def barrier(self, tag="bar"):
        snap = {s: v for s, v in self.cnt.items() if v > 0}
        for e in ENGS:
            waits = []
            for s, v in snap.items():
                if s == e:
                    continue
                if self.known[e].get(s, 0) < v:
                    self.known[e][s] = v
                    waits.append((s, v))
            self.steps.append((e, waits, None, None, 0, tag))


def build_nc():
    nc = bass.Bass("TRN2", target_bir_lowering=False)
    em = Em()
    pers = ExitStack()

    def din(name, shape, dtype=F32):
        return nc.dram_tensor(name, shape, dtype, kind="ExternalInput").ap()

    x0fm = din("x0fm", [NTILES, 128, DCH * NT])
    w_inproj = din("w_inproj", [8, 128, 8 * 512])
    b_inproj = din("b_inproj", [1, 2 * INNER])
    convw = din("convw", [128, 48])
    convb = din("convb", [128, 16])
    dtw = din("dtw", [INNER, DSTATE])
    bpw = din("bpw", [INNER, DSTATE])
    cpw = din("cpw", [INNER, DSTATE])
    dtb = din("dtb", [DSTATE, 1])
    bpb = din("bpb", [DSTATE, 1])
    cpb = din("cpb", [DSTATE, 1])
    s2iw = din("s2iw", [DSTATE, INNER])
    s2ib = din("s2ib", [1, INNER])
    dpar = din("dpar", [128, 16])
    outw = din("outw", [4, 128, 4 * D])
    outb = din("outb", [1, D])
    n1w = din("n1w", [128, 8])
    n2w = din("n2w", [128, 8])
    gatew = din("gatew", [D, NEXP])
    gateb = din("gateb", [1, NEXP])
    ew1 = din("ew1", [NEXP, D, HID])
    eb1 = din("eb1", [128, NEXP * 32])
    ew2 = din("ew2", [NEXP, 8, 128, 4 * D])
    eb2 = din("eb2", [128, NEXP * 8])
    fgw = din("fgw", [D, FE])
    wgT = din("wgT", [FE, D, FEXP])
    wuT = din("wuT", [FE, D, FEXP])
    wdT = din("wdT", [FE, 4, 128, 4 * D])
    wsgT = din("wsgT", [D, FSH])
    wsuT = din("wsuT", [D, FSH])
    wsdT = din("wsdT", [4, 128, 4 * D])
    out = nc.dram_tensor("out", [NTOK, D], F32, kind="ExternalOutput").ap()

    def scr(name, shape, dtype):
        return nc.dram_tensor(name, shape, dtype, kind="Internal").ap()

    w1bf = scr("w1bf", [8, 128, 8 * 512], BF16)
    owbf = scr("owbf", [4, 128, 4 * D], BF16)
    gbuf = scr("gbuf", [NTILES, 128, DCH * NT], BF16)
    accbuf = scr("accbuf", [NTILES, 128, DCH * NT], F32)
    x1tm = scr("x1tm", [NTOK, D], F32)
    h2buf = scr("h2buf", [NTILES, 128, DCH * NT], BF16)
    h2tm = [scr(f"h2tm_{s}", [T, D], BF16) for s in range(B_CORE)]
    acc2 = [scr(f"acc2_{s}", [T, D], F32) for s in range(B_CORE)]
    scoreT = [scr(f"scoreT_{s}", [T, FE], F32) for s in range(B_CORE)]
    wrows = scr("wrows", [NEXP, NTOK], F32)

    wgbf = scr("wgbf", [FE, D, FEXP], BF16)
    wubf = scr("wubf", [FE, D, FEXP], BF16)
    wdbf = scr("wdbf", [FE, 4, 128, 4 * D], BF16)
    r_wcast = [R(f"wcast{e}") for e in range(FE)]
    wconv_chunks = []
    for e in range(FE):
        for half in range(2):
            wconv_chunks.append((wgbf[e, half * 512:(half + 1) * 512, :],
                                 wgT[e, half * 512:(half + 1) * 512, :], e))
        for half in range(2):
            wconv_chunks.append((wubf[e, half * 512:(half + 1) * 512, :],
                                 wuT[e, half * 512:(half + 1) * 512, :], e))
        for half in range(2):
            wconv_chunks.append((wdbf[e, half * 2:(half + 1) * 2],
                                 wdT[e, half * 2:(half + 1) * 2], e))
    wconv_pos = [0]

    def emit_wconv(n, tagp):
        for i in range(n):
            if wconv_pos[0] >= len(wconv_chunks):
                return
            dst, src, ce = wconv_chunks[wconv_pos[0]]
            wconv_pos[0] += 1
            em.dma("pool", lambda g, dst=dst, src=src: g.dma_start(out=dst, in_=src),
                   writes=(r_wcast[ce],), tag=f"wcv{tagp}_{i}")

    r_x0fm = [R(f"x0fm{t}") for t in range(NTILES)]
    r_w1bf = R("w1bf"); r_owbf = R("owbf")
    r_gbuf = [R(f"gbuf{t}") for t in range(NTILES)]
    r_accb = [R(f"accb{t}") for t in range(NTILES)]
    r_x1tm = [R(f"x1tm{t}") for t in range(NTILES)]
    r_h2buf = [R(f"h2buf{t}") for t in range(NTILES)]
    r_h2tm = [R(f"h2tm{t}") for t in range(NTILES)]
    r_acc2 = [R(f"acc2_{s}") for s in range(B_CORE)]
    r_scoreT = [R(f"scoreT_{s}") for s in range(B_CORE)]
    r_wrows = R("wrows")

    pA = pers.enter_context(nc.psum_tensor("pA", [128, 1024], F32))
    pB = pers.enter_context(nc.psum_tensor("pB", [128, 1024], F32))
    p4 = pers.enter_context(nc.psum_tensor("p4", [128, 512], F32))
    p5 = pers.enter_context(nc.psum_tensor("p5", [128, 512], F32))
    p6 = pers.enter_context(nc.psum_tensor("p6", [128, 512], F32))
    p7 = pers.enter_context(nc.psum_tensor("p7", [128, 512], F32))
    pbb = p7[:].bitcast(BF16)
    pb = [pA[:, 0:512], pA[:, 512:1024], pB[:, 0:512], pB[:, 512:1024], p4[:], p5[:], p6[:], p7[:]]
    r_pb = [R("b0"), R("b1"), R("b2"), R("b3"), R("p4"), R("p5"), R("p6"), R("p7")]
    r_pbb = r_pb[7]

    def sb(stack, name, shape, dtype):
        return stack.enter_context(nc.sbuf_tensor(name, shape, dtype))

    onescol_b = sb(pers, "onescol_b", [128, 1], BF16)
    onesrow_b = sb(pers, "onesrow_b", [1, 512], BF16)
    onescol_f = sb(pers, "onescol_f", [128, 1], F32)
    onesrow_f = sb(pers, "onesrow_f", [1, 128], F32)
    identf = sb(pers, "identf", [128, 128], F32)
    identb = sb(pers, "identb", [128, 128], BF16)
    pidx_f = sb(pers, "pidx_f", [128, 1], F32)
    iota256 = sb(pers, "iota256", [128, 256], F32)
    iota128_b = sb(pers, "iota128_b", [128, 1], BF16)
    chunkc_b = sb(pers, "chunkc_b", [128, 16], BF16)
    shift1 = sb(pers, "shift1", [128, 1], I32)
    sel_seq = [sb(pers, f"sel{s}", [128, 1], F32) for s in range(B_CORE)]
    r_const = R("const")

    em.step("pool", lambda g: g.iota(pidx_f[:], pattern=[[0, 1]], base=0, channel_multiplier=1,
                                     allow_small_or_imprecise_dtypes=True), writes=(r_const,), tag="c0")
    em.step("pool", lambda g: g.iota(identf[:], pattern=[[1, 128]], base=0, channel_multiplier=0,
                                     allow_small_or_imprecise_dtypes=True), reads=(r_const,), writes=(r_const,), tag="c1")
    em.step("pool", lambda g: g.tensor_scalar(out=identf[:], in0=identf[:], scalar1=pidx_f[:],
                                              scalar2=1.0, op0=OP.is_equal, op1=OP.mult),
            reads=(r_const,), writes=(r_const,), tag="c2")
    em.step("pool", lambda g: g.iota(iota256[:], pattern=[[1, 256]], base=1, channel_multiplier=0,
                                     allow_small_or_imprecise_dtypes=True), reads=(r_const,), writes=(r_const,), tag="c3")
    em.step("pool", lambda g: g.iota(iota128_b[:], pattern=[[0, 1]], base=0, channel_multiplier=1,
                                     allow_small_or_imprecise_dtypes=True), reads=(r_const,), writes=(r_const,), tag="c4")
    em.step("pool", lambda g: g.iota(chunkc_b[:], pattern=[[1, 16]], base=0, channel_multiplier=0,
                                     allow_small_or_imprecise_dtypes=True), reads=(r_const,), writes=(r_const,), tag="c5")
    em.step("dve", lambda v: v.tensor_copy(out=identb[:], in_=identf[:]),
            reads=(r_const,), writes=(r_const,), tag="c6")
    em.step("dve", lambda v: v.memset(onescol_b[:], 1.0), reads=(r_const,), writes=(r_const,), tag="c7")
    em.step("dve", lambda v: v.memset(onescol_f[:], 1.0), reads=(r_const,), writes=(r_const,), tag="c8")
    em.step("dve", lambda v: v.memset(onesrow_f[:], 1.0), reads=(r_const,), writes=(r_const,), tag="c9")
    em.step("dve", lambda v: v.memset(onesrow_b[:], 1.0), reads=(r_const,), writes=(r_const,), tag="c9b")
    em.step("dve", lambda v: v.memset(shift1[:], 1), reads=(r_const,), writes=(r_const,), tag="c10")
    for s in range(B_CORE):
        em.step("dve", lambda v, s=s: v.memset(sel_seq[s][:], 0.0), reads=(r_const,), writes=(r_const,), tag=f"c11{s}")
        em.step("dve", lambda v, s=s: v.memset(sel_seq[s][32 * s:32 * s + 16, :], 1.0),
                reads=(r_const,), writes=(r_const,), tag=f"c12{s}")

    bib_row = sb(pers, "bib_row", [1, 2 * INNER], BF16)
    outb_row = sb(pers, "outb_row", [1, D], BF16)
    s2ib_row = sb(pers, "s2ib_row", [1, INNER], BF16)
    convw_b = sb(pers, "convw_b", [128, 48], BF16)
    convw_f = sb(pers, "convw_f", [128, 48], F32)
    convb_sb = sb(pers, "convb_sb", [128, 16], F32)
    dtb_sb = sb(pers, "dtb_sb", [DSTATE, 1], F32)
    ndtb_sb = sb(pers, "ndtb_sb", [DSTATE, 1], F32)
    bpb_sb = sb(pers, "bpb_sb", [DSTATE, 1], F32)
    cpb_sb = sb(pers, "cpb_sb", [DSTATE, 1], F32)
    dpar_b = sb(pers, "dpar_b", [128, 16], BF16)
    n1w_sb = sb(pers, "n1w_sb", [128, 8], F32)
    n2w_sb = sb(pers, "n2w_sb", [128, 8], F32)
    gatew_sb = [sb(pers, f"gatew_sb{k}", [128, NEXP], BF16) for k in range(DCH)]
    gatewf_sb = [sb(pers, f"gatewf_sb{k}", [128, NEXP], F32) for k in range(DCH)]
    gbrep_sb = sb(pers, "gbrep_sb", [128, NEXP], F32)
    eb1_sb = sb(pers, "eb1_sb", [128, NEXP * 32], F32)
    eb2_sb = sb(pers, "eb2_sb", [128, NEXP * 8], F32)
    fgw_sb = [sb(pers, f"fgw_sb{k}", [128, FE], BF16) for k in range(DCH)]
    fgwf_sb = [sb(pers, f"fgwf_sb{k}", [128, FE], F32) for k in range(DCH)]
    gateb_row = sb(pers, "gateb_row", [1, NEXP], F32)
    mpt_sb = [[sb(pers, f"mpt{s}_{k}", [128, FE], F32) for k in range(16)] for s in range(B_CORE)]
    idxi_all = [sb(pers, f"idxi{e}", [128, 4], I32) for e in range(FE)]
    wcol_all = [sb(pers, f"wcolp{e}", [128, 4], F32) for e in range(FE)]
    r_idxall = [R(f"idxall{e}") for e in range(FE)]
    r_smw = R("smw")
    r_mpt = [R(f"mpt{s}") for s in range(B_CORE)]

    def ldp(dst, src, tag):
        em.dma("sp", lambda e: e.dma_start(out=dst, in_=src), writes=(r_smw,), tag=tag)

    def ldc(dst, src, tag):
        em.dma("pool", lambda e: e.dma_start(out=dst, in_=src), writes=(r_smw,), tag=tag)

    ldc(bib_row[:], b_inproj[:], "ld_bi")
    ldc(outb_row[:], outb[:], "ld_ob")
    ldc(s2ib_row[:], s2ib[:], "ld_sb")
    ldc(convw_b[:], convw[:], "ld_convw")
    ldp(convw_f[:], convw[:], "ld_convwf")
    ldp(convb_sb[:], convb[:], "ld_convb")
    ldp(dtb_sb[:], dtb[:], "ld_dtb")
    ldp(bpb_sb[:], bpb[:], "ld_bpb")
    ldp(cpb_sb[:], cpb[:], "ld_cpb")
    ldc(dpar_b[:], dpar[:], "ld_dpar")
    ldp(n1w_sb[:], n1w[:], "ld_n1w")
    ldp(n2w_sb[:], n2w[:], "ld_n2w")
    for k in range(DCH):
        ldc(gatew_sb[k][:], gatew[k * 128:(k + 1) * 128, :], f"ld_gw{k}")
        ldc(fgw_sb[k][:], fgw[k * 128:(k + 1) * 128, :], f"ld_fgw{k}")
        ldp(gatewf_sb[k][:], gatew[k * 128:(k + 1) * 128, :], f"ld_gwf{k}")
        ldp(fgwf_sb[k][:], fgw[k * 128:(k + 1) * 128, :], f"ld_fgwf{k}")
    ldp(eb1_sb[:], eb1[:], "ld_eb1")
    ldp(eb2_sb[:], eb2[:], "ld_eb2")
    ldp(gateb_row[:], gateb[:], "ld_gb")
    em.step("pe", lambda t: t.matmul(pb[0][0:128, 0:NEXP], onesrow_f[:], gateb_row[:], start=True, stop=True),
            reads=(r_smw, r_const), writes=(r_pb[0],), tag="gbm")
    em.step("act", lambda s: s.copy(out=gbrep_sb[:], in_=pb[0][0:128, 0:NEXP]),
            reads=(r_pb[0],), writes=(r_smw,), tag="gbe")
    em.step("dve", lambda v: v.tensor_scalar(out=ndtb_sb[:], in0=dtb_sb[:], scalar1=-1.0, scalar2=None, op0=OP.mult),
            reads=(r_smw,), writes=(r_smw,), tag="ndtb")

    s01 = ExitStack()
    em.barrier("b0")

    dtw_sb = [sb(s01, f"dtw{k}", [128, DSTATE], BF16) for k in range(ICH)]
    bpw_sb = [sb(s01, f"bpw{k}", [128, DSTATE], BF16) for k in range(ICH)]
    cpw_sb = [sb(s01, f"cpw{k}", [128, DSTATE], BF16) for k in range(ICH)]
    s2iw_sb = sb(s01, "s2iw_sb", [DSTATE, INNER], BF16)
    for k in range(ICH):
        ldc(dtw_sb[k][:], dtw[k * 128:(k + 1) * 128, :], f"ld_dtw{k}")
        ldc(bpw_sb[k][:], bpw[k * 128:(k + 1) * 128, :], f"ld_bpw{k}")
        ldc(cpw_sb[k][:], cpw[k * 128:(k + 1) * 128, :], f"ld_cpw{k}")
    ldc(s2iw_sb[:], s2iw[:], "ld_s2iw")

    w1blk = [sb(s01, f"w1blk{i}", [128, 8 * 512], BF16) for i in range(2)]
    r_w1blk = [R(f"w1blk{i}") for i in range(2)]
    outw_blk = [sb(s01, f"owb{i}", [128, 4 * D], BF16) for i in range(2)]
    r_owb = [R(f"owb{i}") for i in range(2)]

    def w1_load(tl, blk, buf):
        if tl == 0:
            em.dma("pool", lambda e, blk=blk, buf=buf: e.dma_start(out=w1blk[buf][:], in_=w_inproj[blk]),
                   writes=(r_w1blk[buf],), tag=f"wc{blk}")
            em.dma("sp", lambda e, blk=blk, buf=buf: e.dma_start(out=w1bf[blk], in_=w1blk[buf][:]),
                   reads=(r_w1blk[buf],), writes=(r_w1bf,), tag=f"ws{blk}")
        else:
            em.dma("pool", lambda e, blk=blk, buf=buf: e.dma_start(
                out=w1blk[buf][:], in_=w1bf[blk]),
                reads=(r_w1bf,), writes=(r_w1blk[buf],), tag=f"t{tl}lw{blk}")

    def outw_load(tl, half, kb, buf):
        if tl == 0 and half == 0:
            em.dma("pool", lambda e, kb=kb, buf=buf: e.dma_start(out=outw_blk[buf][:], in_=outw[kb]),
                   writes=(r_owb[buf],), tag=f"oc{kb}")
            em.dma("sp", lambda e, kb=kb, buf=buf: e.dma_start(out=owbf[kb], in_=outw_blk[buf][:]),
                   reads=(r_owb[buf],), writes=(r_owbf,), tag=f"os{kb}")
        else:
            em.dma("pool", lambda e, kb=kb, buf=buf: e.dma_start(
                out=outw_blk[buf][:], in_=owbf[kb]),
                reads=(r_owbf,), writes=(r_owb[buf],), tag=f"t{tl}lo{half}_{kb}")

    x0t = sb(s01, "x0t", [128, DCH * NT], F32)
    r_x0t = R("x0t")
    castb = sb(s01, "castb", [128, DCH * NT], BF16)
    r_cast = R("castb")
    cvtA = sb(s01, "cvtA", [128, DCH * NT], F32)
    r_cvtA = R("cvtA")
    xme = sb(s01, "xme", [128, ICH, NT + 2], BF16)
    r_xme = R("xme")
    cv1 = [sb(s01, "cv1_0", [128, 2, NT], BF16)] * 2
    r_cv1 = [R("cv1_0")] * 2
    svt = [sb(s01, "svt0", [128, 2, NT], BF16)] * 2
    r_svt = [R("svt0")] * 2
    xms = sb(s01, "xms", [128, ICH * NT], BF16)
    r_xms = R("xms")
    sgate = sb(s01, "sgate", [128, ICH * NT], BF16)
    r_sg = R("sgate")
    tmp1k = [sb(s01, "tmp1k0", [128, NT], F32)] * 2
    r_tmp1k = [R("tmp1k0")] * 2

    dt_sb = sb(s01, "dt_sb", [DSTATE, NT], F32)
    asc_sb = sb(s01, "asc_sb", [DSTATE, NT], F32)
    bm_sb = sb(s01, "bm_sb", [DSTATE, NT], F32)
    cm_sb = sb(s01, "cm_sb", [DSTATE, NT], F32)
    bsc_sb = sb(s01, "bsc_sb", [DSTATE, NT], F32)
    yst_sb = sb(s01, "yst_sb", [DSTATE, NT], F32)
    yv_sb = sb(s01, "yv_sb", [DSTATE, NT], F32)
    ybf_sb = sb(s01, "ybf_sb", [DSTATE, NT], BF16)
    ysq_sb = ybf_sb
    ynrm_sb = sb(s01, "ynrm_sb", [DSTATE, NT], BF16)
    state_sb = sb(s01, "state_sb", [DSTATE, 1], F32)
    r_dt = R("dt"); r_asc = R("asc"); r_bm = R("bm"); r_cm = R("cm")
    r_bsc = R("bsc"); r_yst = R("yst"); r_yv = R("yv"); r_ybf = R("ybf")
    r_ysq = r_ybf; r_ynrm = R("ynrm"); r_state = R("state")

    rows1 = sb(s01, "rows1", [1, 8 * 512], F32)
    ROW = {}
    r_row = {}
    for i, nm in enumerate(["s1", "s2", "var", "rstd", "af", "bf", "bh", "tmp"]):
        ROW[nm] = rows1[:, i * 512:(i + 1) * 512]
        r_row[nm] = R(f"row_{nm}")
    bct = {}
    r_bct = {}
    for i, nm in enumerate(["af", "bf", "ah", "bh", "x1"]):
        bct[nm] = sb(s01, f"bc_{nm}", [128, 512], F32)
        r_bct[nm] = R(f"bc_{nm}")

    wfull_sb = sb(s01, "wfull_sb", [128, 16], F32)
    lgt_sb = sb(s01, "lgt_sb", [128, 16], F32)
    m1_sb = sb(s01, "m1_sb", [128, 4], F32)
    m2_sb = sb(s01, "m2_sb", [128, 4], F32)
    tA_sb = sb(s01, "tA_sb", [128, 4], F32)
    tB_sb = sb(s01, "tB_sb", [128, 4], F32)
    es_sb = sb(s01, "es_sb", [128, 16], F32)
    msk_sb = sb(s01, "msk_sb", [128, 16], F32)
    r_wfull = R("wfull"); r_lgt = R("lgt"); r_top2 = R("top2")
    wrow_sb = rows1[:, 7 * 512:8 * 512]
    r_wrowsb = r_row["tmp"]

    def v3(ap2d, nch=DCH):
        return ap2d.rearrange("p (c f) -> p c f", f=NT)

    def bc3(nm, nch=DCH):
        return bct[nm][:].rearrange("p (o f) -> p o f", o=1).to_broadcast([128, nch, NT])

    def cols3(ap2d, nch, step=1, off=0):
        return ap2d[:, off::step].rearrange("p (c o) -> p c o", o=1)[:, 0:nch, :]

    def ln_rows(with_rms, tag):
        A = dict(ROW)
        em.step("dve", lambda v, A=A: v.tensor_scalar(out=A["s1"], in0=pb[4][0:1, :], scalar1=1.0 / D,
                                                      scalar2=None, op0=OP.mult),
                reads=(r_pb[4],), writes=(r_row["s1"],), tag=f"{tag}_mu")
        em.step("dve", lambda v, A=A: v.tensor_scalar(out=A["s2"], in0=pb[5][0:1, :], scalar1=1.0 / D,
                                                      scalar2=None, op0=OP.mult),
                reads=(r_pb[5],), writes=(r_row["s2"],), tag=f"{tag}_ms")
        em.step("dve", lambda v, A=A: v.tensor_tensor(out=A["tmp"], in0=A["s1"], in1=A["s1"], op=OP.mult),
                reads=(r_row["s1"],), writes=(r_row["tmp"],), tag=f"{tag}_mu2")
        em.step("dve", lambda v, A=A: v.tensor_tensor(out=A["var"], in0=A["s2"], in1=A["tmp"], op=OP.subtract),
                reads=(r_row["s2"], r_row["tmp"]), writes=(r_row["var"],), tag=f"{tag}_var")
        em.step("dve", lambda v, A=A: v.tensor_scalar(out=A["tmp"], in0=A["var"], scalar1=1e-5,
                                                      scalar2=None, op0=OP.add),
                reads=(r_row["var"],), writes=(r_row["tmp"],), tag=f"{tag}_ve")
        em.step("dve", lambda v, A=A: v.reciprocal(out=A["tmp"], in_=A["tmp"]),
                reads=(r_row["tmp"],), writes=(r_row["tmp"],), tag=f"{tag}_rv")
        em.step("act", lambda s, A=A: s.activation(out=A["rstd"], in_=A["tmp"], func=AF.Sqrt),
                reads=(r_row["tmp"],), writes=(r_row["rstd"],), tag=f"{tag}_sd")
        if with_rms:
            em.step("dve", lambda v, A=A: v.tensor_tensor(out=A["af"], in0=A["var"], in1=A["tmp"], op=OP.mult),
                    reads=(r_row["var"], r_row["tmp"]), writes=(r_row["af"],), tag=f"{tag}_q")
            em.step("dve", lambda v, A=A: v.tensor_scalar(out=A["af"], in0=A["af"], scalar1=1e-6,
                                                          scalar2=None, op0=OP.add),
                    reads=(r_row["af"],), writes=(r_row["af"],), tag=f"{tag}_qe")
            em.step("dve", lambda v, A=A: v.reciprocal(out=A["af"], in_=A["af"]),
                    reads=(r_row["af"],), writes=(r_row["af"],), tag=f"{tag}_rq")
            em.step("act", lambda s, A=A: s.activation(out=A["af"], in_=A["af"], func=AF.Sqrt),
                    reads=(r_row["af"],), writes=(r_row["af"],), tag=f"{tag}_r2")
            em.step("dve", lambda v, A=A: v.tensor_tensor(out=A["af"], in0=A["af"], in1=A["rstd"], op=OP.mult),
                    reads=(r_row["af"], r_row["rstd"]), writes=(r_row["af"],), tag=f"{tag}_af")
            em.step("dve", lambda v, A=A: v.tensor_tensor(out=A["bf"], in0=A["s1"], in1=A["af"], op=OP.mult),
                    reads=(r_row["s1"], r_row["af"]), writes=(r_row["bf"],), tag=f"{tag}_bf")
        em.step("dve", lambda v, A=A: v.tensor_tensor(out=A["bh"], in0=A["s1"], in1=A["rstd"], op=OP.mult),
                reads=(r_row["s1"], r_row["rstd"]), writes=(r_row["bh"],), tag=f"{tag}_bh")

    def bcast(rowap, rowres, bcname, tag, nrows=128):
        em.step("pe", lambda t: t.matmul(pb[6][0:nrows, :], onesrow_f[0:1, 0:nrows], rowap,
                                         start=True, stop=True),
                reads=(rowres, r_const), writes=(r_pb[6],), tag=f"{tag}m")
        em.step("act", lambda s: s.copy(out=bct[bcname][0:nrows, :], in_=pb[6][0:nrows, :]),
                reads=(r_pb[6],), writes=(r_bct[bcname],), tag=f"{tag}e")

    def stats_pass(src_ap, src_res, tag, rms_only=False):
        if not rms_only:
            em.step("dve", lambda v: v.tensor_copy(out=castb[:], in_=src_ap),
                    reads=(src_res,), writes=(r_cast,), tag=f"{tag}c")
            for d in range(DCH):
                em.step("pe", lambda t, d=d: t.matmul(pb[4][0:1, :], onescol_b[:], castb[:, d * NT:(d + 1) * NT],
                                                      start=(d == 0), stop=(d == DCH - 1)),
                        reads=(r_cast, r_const), writes=(r_pb[4],), accum=(d > 0), tag=f"{tag}1{d}")
        em.step("act", lambda s: s.activation(out=castb[:], in_=src_ap, func=AF.Square),
                reads=(src_res,), writes=(r_cast,), tag=f"{tag}q")
        for d in range(DCH):
            em.step("pe", lambda t, d=d: t.matmul(pb[5][0:1, :], onescol_b[:], castb[:, d * NT:(d + 1) * NT],
                                                  start=(d == 0), stop=(d == DCH - 1)),
                    reads=(r_cast, r_const), writes=(r_pb[5],), accum=(d > 0), tag=f"{tag}2{d}")

    for tl in range(NTILES):
        sfx = f"t{tl}"
        em.dma("sp", lambda e, tl=tl: e.dma_start(out=x0t[:], in_=x0fm[tl]),
            reads=(r_x0fm[tl],), writes=(r_x0t,), tag=f"{sfx}ldx")
        stats_pass(x0t[:], r_x0t, f"{sfx}s")
        ln_rows(True, f"{sfx}ln1")
        bcast(ROW["af"], r_row["af"], "af", f"{sfx}baf")
        bcast(ROW["bf"], r_row["bf"], "bf", f"{sfx}bbf")
        bcast(ROW["rstd"], r_row["rstd"], "ah", f"{sfx}bah")
        bcast(ROW["bh"], r_row["bh"], "bh", f"{sfx}bbh")
        em.step("dve", lambda v: v.tensor_tensor(out=v3(cvtA[:]), in0=v3(x0t[:]), in1=bc3("af"), op=OP.mult),
                reads=(r_x0t, r_bct["af"]), writes=(r_cvtA,), tag=f"{sfx}h1")
        em.step("dve", lambda v: v.tensor_tensor(out=v3(cvtA[:]), in0=v3(cvtA[:]), in1=bc3("bf"), op=OP.subtract),
                reads=(r_cvtA, r_bct["bf"]), writes=(r_cvtA,), tag=f"{sfx}h2")
        em.step("dve", lambda v: v.tensor_tensor(out=v3(castb[:]), in0=v3(cvtA[:]),
                                                 in1=cols3(n1w_sb[:], DCH).to_broadcast([128, DCH, NT]), op=OP.mult),
                reads=(r_cvtA, r_smw), writes=(r_cast,), tag=f"{sfx}h3")
        if tl % 4 == 0:
            em.step("dve", lambda v: v.memset(xme[:, :, 0:2], 0.0),
                    reads=(r_xme,), writes=(r_xme,), tag=f"{sfx}cz")
        else:
            em.step("dve", lambda v: v.tensor_copy(out=xme[:, :, 0:2], in_=xme[:, :, NT:NT + 2]),
                    reads=(r_xme,), writes=(r_xme,), tag=f"{sfx}cc")
        for blk in range(8):
            buf = blk % 2
            w1_load(tl, blk, buf)
            for fl in range(4):
                f = blk * 4 + fl
                bank = f % 4

                def mmg(t, fl=fl, buf=buf, bank=bank, f=f):
                    for k in range(DCH):
                        t.matmul(pb[bank], w1blk[buf][:, k * 512 + fl * 128:k * 512 + (fl + 1) * 128],
                                 castb[:, k * NT:(k + 1) * NT], start=(k == 0), stop=False)
                    return t.matmul(pb[bank], bib_row[0:1, f * 128:(f + 1) * 128], onesrow_b[:],
                                    start=False, stop=True)
                em.step("pe", mmg, reads=(r_w1blk[buf], r_cast, r_smw, r_const),
                        writes=(r_pb[bank],), tag=f"{sfx}ip{f}")
                if f % 2 == 1:
                    pair = pA if bank == 1 else pB
                    rpair = (r_pb[0], r_pb[1]) if bank == 1 else (r_pb[2], r_pb[3])
                    if f < ICH:
                        em.step("act", lambda s, f=f, pair=pair: s.copy(
                            out=xme[:, f - 1:f + 1, 2:NT + 2], in_=pair[:, :]),
                            reads=rpair + (r_xme,), writes=(r_xme,), tag=f"{sfx}ie{f}")
                        c0 = f - 1
                        pi = (f // 2) % 2

                        def cva(v, c0=c0, pi=pi):
                            inst = None
                            for j in range(2):
                                inst = v.tensor_scalar(out=cv1[pi][:, j, :], in0=xme[:, c0 + j, 0:NT],
                                                       scalar1=convw_f[:, 3 * (c0 + j):3 * (c0 + j) + 1],
                                                       scalar2=None, op0=OP.mult)
                            return inst
                        em.step("dve", cva, reads=(r_xme, r_smw), writes=(r_cv1[pi],), tag=f"{sfx}v1{c0}")

                        def cvb(v, c0=c0, pi=pi):
                            inst = None
                            for j in range(2):
                                inst = v.scalar_tensor_tensor(out=cv1[pi][:, j, :], in0=xme[:, c0 + j, 1:NT + 1],
                                                              scalar=convw_b[:, 3 * (c0 + j) + 1:3 * (c0 + j) + 2],
                                                              in1=cv1[pi][:, j, :], op0=OP.mult, op1=OP.add)
                            return inst
                        em.step("dve", cvb, reads=(r_xme, r_cv1[pi], r_smw), writes=(r_cv1[pi],), tag=f"{sfx}v2{c0}")

                        def cvc(v, c0=c0, pi=pi):
                            inst = None
                            for j in range(2):
                                inst = v.scalar_tensor_tensor(out=cv1[pi][:, j, :], in0=xme[:, c0 + j, 2:NT + 2],
                                                              scalar=convw_b[:, 3 * (c0 + j) + 2:3 * (c0 + j) + 3],
                                                              in1=cv1[pi][:, j, :], op0=OP.mult, op1=OP.add)
                            return inst
                        em.step("dve", cvc, reads=(r_xme, r_cv1[pi], r_smw), writes=(r_cv1[pi],), tag=f"{sfx}v3{c0}")
                        em.step("dve", lambda v, c0=c0, pi=pi: v.tensor_tensor(
                            out=cv1[pi][:], in0=cv1[pi][:],
                            in1=convb_sb[:, c0:c0 + 2].rearrange("p (c o) -> p c o", o=1).to_broadcast([128, 2, NT]),
                            op=OP.add),
                            reads=(r_cv1[pi], r_smw), writes=(r_cv1[pi],), tag=f"{sfx}v4{c0}")
                        em.step("act", lambda s, pi=pi: s.activation(out=svt[pi][:], in_=cv1[pi][:], func=AF.Sigmoid),
                                reads=(r_cv1[pi],), writes=(r_svt[pi],), tag=f"{sfx}v5{c0}")
                        em.step("dve", lambda v, c0=c0, pi=pi: v.tensor_tensor(
                            out=xms[:, c0 * NT:(c0 + 2) * NT],
                            in0=cv1[pi][:].rearrange("p c f -> p (c f)"),
                            in1=svt[pi][:].rearrange("p c f -> p (c f)"), op=OP.mult),
                            reads=(r_cv1[pi], r_svt[pi]), writes=(r_xms,), tag=f"{sfx}v6{c0}")
                    else:
                        c = f - ICH
                        em.step("act", lambda s, c=c, pair=pair: s.activation(
                            out=sgate[:, (c - 1) * NT:(c + 1) * NT], in_=pair[:, :], func=AF.Sigmoid),
                            reads=rpair, writes=(r_sg,), tag=f"{sfx}ig{c}")
        for (nm, wsb, bank) in [("dt", dtw_sb, 0), ("bm", bpw_sb, 1), ("cm", cpw_sb, 2)]:
            def mmg(t, wsb=wsb, bank=bank):
                inst = None
                for k in range(ICH):
                    inst = t.matmul(pb[bank][0:DSTATE, :], wsb[k][:], xms[:, k * NT:(k + 1) * NT],
                                    start=(k == 0), stop=(k == ICH - 1))
                return inst
            em.step("pe", mmg, reads=(r_xms, r_smw), writes=(r_pb[bank],), tag=f"{sfx}p{nm}")
        em.step("act", lambda s: s.activation(out=dt_sb[:], in_=pb[0][0:DSTATE, :], func=AF.Sigmoid,
                                              bias=dtb_sb[:], scale=1.0),
                reads=(r_pb[0], r_smw), writes=(r_dt,), tag=f"{sfx}dt")
        em.step("act", lambda s: s.activation(out=asc_sb[:], in_=pb[0][0:DSTATE, :], func=AF.Sigmoid,
                                              bias=ndtb_sb[:], scale=-1.0),
                reads=(r_pb[0], r_smw), writes=(r_asc,), tag=f"{sfx}as")
        em.step("act", lambda s: s.activation(out=bm_sb[:], in_=pb[1][0:DSTATE, :], func=AF.Identity,
                                              bias=bpb_sb[:], scale=1.0),
                reads=(r_pb[1], r_smw), writes=(r_bm,), tag=f"{sfx}bm")
        em.step("act", lambda s: s.activation(out=cm_sb[:], in_=pb[2][0:DSTATE, :], func=AF.Identity,
                                              bias=cpb_sb[:], scale=1.0),
                reads=(r_pb[2], r_smw), writes=(r_cm,), tag=f"{sfx}cm")
        if tl % 4 == 0:
            em.step("dve", lambda v: v.memset(state_sb[:], 0.0),
                    reads=(r_state,), writes=(r_state,), tag=f"{sfx}z")
        em.step("dve", lambda v: v.tensor_tensor(out=bsc_sb[:], in0=dt_sb[:], in1=bm_sb[:], op=OP.mult),
                reads=(r_dt, r_bm), writes=(r_bsc,), tag=f"{sfx}bs")
        em.step("dve", lambda v: v.tensor_tensor_scan(out=yst_sb[:], data0=asc_sb[:], data1=bsc_sb[:],
                                                      initial=state_sb[:], op0=OP.mult, op1=OP.add),
                reads=(r_asc, r_bsc, r_state), writes=(r_yst,), tag=f"{sfx}sc")
        em.step("dve", lambda v: v.tensor_copy(out=state_sb[:], in_=yst_sb[:, NT - 1:NT]),
                reads=(r_yst,), writes=(r_state,), tag=f"{sfx}su")
        em.step("dve", lambda v: v.tensor_tensor(out=yv_sb[:], in0=yst_sb[:], in1=cm_sb[:], op=OP.mult),
                reads=(r_yst, r_cm), writes=(r_yv,), tag=f"{sfx}y")
        em.step("dve", lambda v: v.tensor_copy(out=ybf_sb[:], in_=yv_sb[:]),
                reads=(r_yv,), writes=(r_ybf,), tag=f"{sfx}yc")
        em.step("act", lambda s: s.activation(out=ysq_sb[:], in_=yv_sb[:], func=AF.Square),
                reads=(r_yv,), writes=(r_ysq,), tag=f"{sfx}yq")
        em.step("pe", lambda t: t.matmul(pb[4][0:1, :], onescol_b[0:DSTATE, :], ybf_sb[:], start=True, stop=True),
                reads=(r_ybf, r_const), writes=(r_pb[4],), tag=f"{sfx}y1")
        em.step("pe", lambda t: t.matmul(pb[5][0:1, :], onescol_b[0:DSTATE, :], ysq_sb[:], start=True, stop=True),
                reads=(r_ysq, r_const), writes=(r_pb[5],), tag=f"{sfx}y2")
        em.step("dve", lambda v, rr=ROW["s1"]: v.tensor_scalar(out=rr, in0=pb[4][0:1, :], scalar1=1.0 / DSTATE,
                                                               scalar2=None, op0=OP.mult),
                reads=(r_pb[4],), writes=(r_row["s1"],), tag=f"{sfx}ym")
        em.step("dve", lambda v, rr=ROW["s2"]: v.tensor_scalar(out=rr, in0=pb[5][0:1, :], scalar1=1.0 / DSTATE,
                                                               scalar2=None, op0=OP.mult),
                reads=(r_pb[5],), writes=(r_row["s2"],), tag=f"{sfx}ys")
        em.step("dve", lambda v, ra=ROW["tmp"], rb=ROW["s1"]: v.tensor_tensor(out=ra, in0=rb, in1=rb, op=OP.mult),
                reads=(r_row["s1"],), writes=(r_row["tmp"],), tag=f"{sfx}ym2")
        em.step("dve", lambda v, ra=ROW["var"], rb=ROW["s2"], rc=ROW["tmp"]: v.tensor_tensor(out=ra, in0=rb, in1=rc, op=OP.subtract),
                reads=(r_row["s2"], r_row["tmp"]), writes=(r_row["var"],), tag=f"{sfx}yv")
        em.step("dve", lambda v, ra=ROW["tmp"], rb=ROW["var"]: v.tensor_scalar(out=ra, in0=rb, scalar1=1e-5,
                                                                               scalar2=None, op0=OP.add),
                reads=(r_row["var"],), writes=(r_row["tmp"],), tag=f"{sfx}ye")
        em.step("dve", lambda v, ra=ROW["tmp"]: v.reciprocal(out=ra, in_=ra),
                reads=(r_row["tmp"],), writes=(r_row["tmp"],), tag=f"{sfx}yr")
        em.step("act", lambda s, ra=ROW["rstd"], rb=ROW["tmp"]: s.activation(out=ra, in_=rb, func=AF.Sqrt),
                reads=(r_row["tmp"],), writes=(r_row["rstd"],), tag=f"{sfx}yd")
        bcast(ROW["s1"], r_row["s1"], "x1", f"{sfx}bm1", nrows=DSTATE)
        em.step("dve", lambda v: v.tensor_tensor(out=yv_sb[:], in0=yv_sb[:], in1=bct["x1"][0:DSTATE, :], op=OP.subtract),
                reads=(r_yv, r_bct["x1"]), writes=(r_yv,), tag=f"{sfx}n1")
        bcast(ROW["rstd"], r_row["rstd"], "x1", f"{sfx}bm2", nrows=DSTATE)
        em.step("dve", lambda v: v.tensor_tensor(out=ynrm_sb[:], in0=yv_sb[:], in1=bct["x1"][0:DSTATE, :], op=OP.mult),
                reads=(r_yv, r_bct["x1"]), writes=(r_ynrm,), tag=f"{sfx}n2")
        for c in range(ICH):
            bank = c % 4

            def mmg(t, c=c, bank=bank):
                t.matmul(pb[bank], s2iw_sb[:, c * 128:(c + 1) * 128], ynrm_sb[:], start=True, stop=False)
                return t.matmul(pb[bank], s2ib_row[0:1, c * 128:(c + 1) * 128], onesrow_b[:],
                                start=False, stop=True)
            em.step("pe", mmg, reads=(r_ynrm, r_smw, r_const), writes=(r_pb[bank],), tag=f"{sfx}s2i{c}")
            em.step("dve", lambda v, c=c, bank=bank: v.scalar_tensor_tensor(
                out=tmp1k[c % 2][:], in0=xms[:, c * NT:(c + 1) * NT], scalar=dpar_b[:, c:c + 1],
                in1=pb[bank], op0=OP.mult, op1=OP.add),
                reads=(r_xms, r_pb[bank], r_smw), writes=(r_tmp1k[c % 2],), tag=f"{sfx}t3{c}")
            em.step("dve", lambda v, c=c: v.tensor_tensor(out=xms[:, c * NT:(c + 1) * NT], in0=tmp1k[c % 2][:],
                                                          in1=sgate[:, c * NT:(c + 1) * NT], op=OP.mult),
                    reads=(r_tmp1k[c % 2], r_sg), writes=(r_xms,), tag=f"{sfx}t5{c}")
        for half in range(2):
            for kb in range(4):
                buf = kb % 2
                outw_load(tl, half, kb, buf)
                for kl in range(4):
                    k = kb * 4 + kl

                    def mmg(t, kl=kl, buf=buf, half=half, k=k):
                        inst = None
                        for dd in range(4):
                            d = half * 4 + dd
                            inst = t.matmul(pb[dd], outw_blk[buf][:, kl * D + d * 128:kl * D + (d + 1) * 128],
                                            xms[:, k * NT:(k + 1) * NT], start=(k == 0), stop=False)
                        return inst
                    em.step("pe", mmg, reads=(r_owb[buf], r_xms),
                            writes=tuple(r_pb[0:4]), accum=(k > 0), tag=f"{sfx}op{half}_{k}")

            def mmb(t, half=half):
                inst = None
                for dd in range(4):
                    d = half * 4 + dd
                    inst = t.matmul(pb[dd], outb_row[0:1, d * 128:(d + 1) * 128], onesrow_b[:],
                                    start=False, stop=True)
                return inst
            em.step("pe", mmb, reads=(r_smw, r_const), writes=tuple(r_pb[0:4]), accum=True, tag=f"{sfx}ob{half}")
            for pr in range(2):
                pair = pA if pr == 0 else pB
                rpair = (r_pb[0], r_pb[1]) if pr == 0 else (r_pb[2], r_pb[3])
                em.step("act", lambda s, half=half, pr=pr, pair=pair: s.copy(
                    out=cvtA[:, (half * 4 + pr * 2) * NT:(half * 4 + pr * 2 + 2) * NT], in_=pair[:, :]),
                    reads=rpair, writes=(r_cvtA,), tag=f"{sfx}oe{half}{pr}")
        em.step("dve", lambda v: v.tensor_tensor(out=v3(x0t[:]), in0=v3(x0t[:]), in1=bc3("ah"), op=OP.mult),
                reads=(r_x0t, r_bct["ah"]), writes=(r_x0t,), tag=f"{sfx}a1")
        em.step("dve", lambda v: v.tensor_tensor(out=x0t[:], in0=x0t[:], in1=cvtA[:], op=OP.add),
                reads=(r_x0t, r_cvtA), writes=(r_x0t,), tag=f"{sfx}a2")
        em.step("dve", lambda v: v.tensor_tensor(out=v3(x0t[:]), in0=v3(x0t[:]), in1=bc3("bh"), op=OP.subtract),
                reads=(r_x0t, r_bct["bh"]), writes=(r_x0t,), tag=f"{sfx}a3")
        em.dma("sp", lambda e, tl=tl: e.dma_start(out=accbuf[tl], in_=x0t[:]),
            reads=(r_x0t,), writes=(r_accb[tl],), tag=f"{sfx}sa")
        stats_pass(x0t[:], r_x0t, f"{sfx}r", rms_only=True)
        em.step("dve", lambda v, ra=ROW["tmp"]: v.tensor_scalar(out=ra, in0=pb[5][0:1, :], scalar1=1.0 / D,
                                                                scalar2=1e-6, op0=OP.mult, op1=OP.add),
                reads=(r_pb[5],), writes=(r_row["tmp"],), tag=f"{sfx}gm")
        em.step("dve", lambda v, ra=ROW["tmp"]: v.reciprocal(out=ra, in_=ra),
                reads=(r_row["tmp"],), writes=(r_row["tmp"],), tag=f"{sfx}gr")
        em.step("act", lambda s, ra=ROW["tmp"]: s.activation(out=ra, in_=ra, func=AF.Sqrt),
                reads=(r_row["tmp"],), writes=(r_row["tmp"],), tag=f"{sfx}g2")
        bcast(ROW["tmp"], r_row["tmp"], "x1", f"{sfx}bg")
        em.step("dve", lambda v: v.tensor_tensor(out=v3(x0t[:]), in0=v3(x0t[:]), in1=bc3("x1"), op=OP.mult),
                reads=(r_x0t, r_bct["x1"]), writes=(r_x0t,), tag=f"{sfx}ga")
        em.step("dve", lambda v: v.tensor_tensor(out=v3(x0t[:]), in0=v3(x0t[:]),
                                                 in1=cols3(n2w_sb[:], DCH).to_broadcast([128, DCH, NT]), op=OP.mult),
                reads=(r_x0t, r_smw), writes=(r_x0t,), tag=f"{sfx}gb")
        em.step("dve", lambda v: v.tensor_copy(out=castb[:], in_=x0t[:]),
                reads=(r_x0t,), writes=(r_cast,), tag=f"{sfx}gc")
        em.dma("sp", lambda e, tl=tl: e.dma_start(out=gbuf[tl], in_=castb[:]),
            reads=(r_cast,), writes=(r_gbuf[tl],), tag=f"{sfx}sg")
        for sub in range(4):
            def mmg(t, sub=sub):
                inst = None
                for k in range(DCH):
                    inst = t.matmul(pb[4][:, sub * 4:(sub + 1) * 4],
                                    x0t[:, k * NT + sub * 128:k * NT + (sub + 1) * 128], gatewf_sb[k][:],
                                    start=(k == 0), stop=(k == DCH - 1))
                return inst
            em.step("pe", mmg, reads=(r_x0t, r_smw), writes=(r_pb[4],),
                    accum=(sub > 0), tag=f"{sfx}lg{sub}")

        def lgev(v):
            inst = None
            for sub in range(4):
                inst = v.tensor_tensor(out=lgt_sb[:, sub * 4:(sub + 1) * 4],
                                       in0=pb[4][:, sub * 4:(sub + 1) * 4], in1=gbrep_sb[:], op=OP.add)
            return inst
        em.step("dve", lgev, reads=(r_pb[4], r_smw), writes=(r_lgt,), tag=f"{sfx}le")
        l0 = lgt_sb[:, 0::4]; l1 = lgt_sb[:, 1::4]; l2 = lgt_sb[:, 2::4]; l3 = lgt_sb[:, 3::4]
        em.step("dve", lambda v: v.tensor_tensor(out=tA_sb[:], in0=l0, in1=l1, op=OP.max),
                reads=(r_lgt,), writes=(r_top2,), tag=f"{sfx}p1")
        em.step("dve", lambda v: v.tensor_tensor(out=tB_sb[:], in0=l2, in1=l3, op=OP.max),
                reads=(r_lgt, r_top2), writes=(r_top2,), tag=f"{sfx}p2")
        em.step("dve", lambda v: v.tensor_tensor(out=m1_sb[:], in0=tA_sb[:], in1=tB_sb[:], op=OP.max),
                reads=(r_top2,), writes=(r_top2,), tag=f"{sfx}p3")
        em.step("dve", lambda v: v.tensor_tensor(out=m2_sb[:], in0=tA_sb[:], in1=tB_sb[:], op=OP.min),
                reads=(r_top2,), writes=(r_top2,), tag=f"{sfx}p4")
        em.step("dve", lambda v: v.tensor_tensor(out=tA_sb[:], in0=l0, in1=l1, op=OP.min),
                reads=(r_lgt, r_top2), writes=(r_top2,), tag=f"{sfx}p5")
        em.step("dve", lambda v: v.tensor_tensor(out=tB_sb[:], in0=l2, in1=l3, op=OP.min),
                reads=(r_lgt, r_top2), writes=(r_top2,), tag=f"{sfx}p6")
        em.step("dve", lambda v: v.tensor_tensor(out=tA_sb[:], in0=tA_sb[:], in1=tB_sb[:], op=OP.max),
                reads=(r_top2,), writes=(r_top2,), tag=f"{sfx}p7")
        em.step("dve", lambda v: v.tensor_tensor(out=m2_sb[:], in0=m2_sb[:], in1=tA_sb[:], op=OP.max),
                reads=(r_top2,), writes=(r_top2,), tag=f"{sfx}p8")
        em.step("dve", lambda v: v.tensor_tensor(out=tA_sb[:], in0=m1_sb[:], in1=m2_sb[:], op=OP.add),
                reads=(r_top2,), writes=(r_top2,), tag=f"{sfx}p9")

        def zst(v):
            inst = None
            for sub in range(4):
                inst = v.scalar_tensor_tensor(out=es_sb[:, sub * 4:(sub + 1) * 4],
                                              in0=lgt_sb[:, sub * 4:(sub + 1) * 4], scalar=2.0,
                                              in1=tA_sb[:, sub:sub + 1].to_broadcast([128, 4]),
                                              op0=OP.mult, op1=OP.subtract)
            return inst
        em.step("dve", zst, reads=(r_lgt, r_top2), writes=(r_top2,), tag=f"{sfx}pz")
        em.step("act", lambda s: s.activation(out=es_sb[:], in_=es_sb[:], func=AF.Sigmoid),
                reads=(r_top2,), writes=(r_top2,), tag=f"{sfx}px")

        def maskst(v):
            inst = None
            for sub in range(4):
                inst = v.tensor_scalar(out=msk_sb[:, sub * 4:(sub + 1) * 4],
                                       in0=lgt_sb[:, sub * 4:(sub + 1) * 4],
                                       scalar1=m2_sb[:, sub:sub + 1], scalar2=None, op0=OP.is_ge)
            return inst
        em.step("dve", maskst, reads=(r_lgt, r_top2), writes=(r_top2,), tag=f"{sfx}pm")

        def wfst(v):
            return v.tensor_tensor(out=wfull_sb[:], in0=es_sb[:], in1=msk_sb[:], op=OP.mult)
        em.step("dve", wfst, reads=(r_top2,), writes=(r_wfull,), tag=f"{sfx}pf")
        for e in range(NEXP):
            def wtr(t, e=e):
                inst = None
                for sub in range(4):
                    col = sub * 4 + e
                    inst = t.matmul(pb[6][0:1, sub * 128:(sub + 1) * 128], wfull_sb[:, col:col + 1],
                                    identf[:], start=True, stop=True)
                return inst
            em.step("pe", wtr, reads=(r_wfull, r_const), writes=(r_pb[6],), tag=f"{sfx}wt{e}")
            em.step("act", lambda s: s.copy(out=wrow_sb, in_=pb[6][0:1, :]),
                    reads=(r_pb[6],), writes=(r_wrowsb,), tag=f"{sfx}we{e}")
            em.dma("sp", lambda g, e=e, tl=tl: g.dma_start(
                out=wrows[e:e + 1, tl * NT:(tl + 1) * NT], in_=wrow_sb),
                reads=(r_wrowsb,), writes=(r_wrows,), tag=f"{sfx}ws{e}")

    s01.close()
    em.barrier("b1")

    s2 = ExitStack()
    ew1_sb = [sb(s2, f"ew1_{k}", [128, HID], BF16) for k in range(DCH)]
    r_ew1 = [R(f"ew1_{k}") for k in range(DCH)]
    ew2_sb = [sb(s2, f"ew2_{k}", [128, 4 * D], BF16) for k in range(HCH // 4)]
    r_ew2 = [R(f"ew2_{k}") for k in range(HCH // 4)]
    g2_sb = [sb(s2, f"g2_{i}", [128, DCH * NT], BF16) for i in range(2)]
    r_g2 = [R(f"g2_{i}") for i in range(2)]
    hid_sb = [sb(s2, f"hid_{f}", [128, NT], BF16) for f in range(HCH)]
    r_hid = [R(f"hid_{f}") for f in range(HCH)]
    bcwe_sb = [sb(s2, "bcwe0", [128, NT], F32)] * 2
    r_bcwe = [R("bcwe0")] * 2
    ctr_sb = [sb(s2, f"ctr{i}", [128, NT], F32) for i in range(2)]
    r_ctr = [R(f"ctr{i}") for i in range(2)]

    for e in range(NEXP):
        for k in range(DCH):
            em.dma("pool", lambda g, e=e, k=k: g.dma_start(out=ew1_sb[k][:], in_=ew1[e, k * 128:(k + 1) * 128, :]),
                   writes=(r_ew1[k],), tag=f"e{e}w1{k}")
        for kb in range(HCH // 4):
            em.dma("pool", lambda g, e=e, kb=kb: g.dma_start(
                out=ew2_sb[kb][:], in_=ew2[e, kb]),
                writes=(r_ew2[kb],), tag=f"e{e}w2{kb}")
        for tl in range(NTILES):
            sfx = f"e{e}t{tl}"
            gb = tl % 2
            em.dma("sp", lambda g, tl=tl, gb=gb: g.dma_start(out=g2_sb[gb][:], in_=gbuf[tl]),
                reads=(r_gbuf[tl],), writes=(r_g2[gb],), tag=f"{sfx}lg")
            em.dma("act", lambda g, e=e, tl=tl, gb=gb: g.dma_start(
                out=bcwe_sb[gb][:], in_=wrows[e:e + 1, tl * NT:(tl + 1) * NT].to_broadcast([128, NT])),
                reads=(r_wrows,), writes=(r_bcwe[gb],), tag=f"{sfx}bw")
            for f in range(HCH):
                bank = f % 4

                def mmg(t, f=f, bank=bank, gb=gb):
                    inst = None
                    for k in range(DCH):
                        inst = t.matmul(pb[bank][:, :], ew1_sb[k][:, f * 128:(f + 1) * 128],
                                        g2_sb[gb][:, k * NT:(k + 1) * NT],
                                        start=(k == 0), stop=(k == DCH - 1))
                    return inst
                em.step("pe", mmg, reads=(r_g2[gb],) + tuple(r_ew1), writes=(r_pb[bank],), tag=f"{sfx}h{f}")
                em.step("act", lambda s, e=e, f=f, bank=bank: s.activation(
                    out=hid_sb[f][:], in_=pb[bank][:, :], func=AF.Gelu,
                    bias=eb1_sb[:, e * 32 + f:e * 32 + f + 1], scale=1.0),
                    reads=(r_pb[bank], r_smw), writes=(r_hid[f],), tag=f"{sfx}g{f}")
            for d in range(DCH):
                bank = 4 + (d % 2)

                def mmg2(t, d=d, bank=bank):
                    inst = None
                    for k in range(HCH):
                        inst = t.matmul(pb[bank][:, :], ew2_sb[k // 4][:, (k % 4) * D + d * 128:(k % 4) * D + (d + 1) * 128],
                                        hid_sb[k][:], start=(k == 0), stop=(k == HCH - 1))
                    return inst
                em.step("pe", mmg2, reads=tuple(r_hid) + tuple(r_ew2), writes=(r_pb[bank],), tag=f"{sfx}o{d}")
                em.step("dve", lambda v, e=e, d=d, bank=bank, gb=gb: v.scalar_tensor_tensor(
                    out=ctr_sb[d % 2][:], in0=pb[bank][:, :], scalar=eb2_sb[:, e * 8 + d:e * 8 + d + 1],
                    in1=bcwe_sb[gb][:], op0=OP.add, op1=OP.mult),
                    reads=(r_pb[bank], r_smw, r_bcwe[gb]), writes=(r_ctr[d % 2],), tag=f"{sfx}v{d}")
                em.dma("pool", lambda g, d=d, tl=tl: g.dma_start(
                    out=accbuf[tl, :, d * NT:(d + 1) * NT], in_=ctr_sb[d % 2][:],
                    accum_op=OP.add),
                    reads=(r_ctr[d % 2],), writes=(r_accb[tl],), tag=f"{sfx}x{d}")
            if 1 <= tl <= 6:
                emit_wconv(1, f"s2{e}_{tl}")

    s2.close()
    em.barrier("b2")

    sr = ExitStack()
    scores_sb = sb(sr, "scores_sb", [128, T], F32)
    r_scores = R("scores")
    mA_sb = sb(sr, "mA", [128, T], F32)
    mB_sb = sb(sr, "mB", [128, T], F32)
    mC_sb = sb(sr, "mC", [128, T], F32)
    r_route = R("route")
    lo_sb = sb(sr, "lo", [48, 1], I32)
    hi_sb = sb(sr, "hi", [48, 1], I32)
    mid_sb = sb(sr, "mid", [48, 1], I32)
    cnt_sb = sb(sr, "cnt", [48, 1], F32)
    cond_sb = sb(sr, "cond", [48, 1], I32)
    rrem_sb = sb(sr, "rrem", [48, 1], F32)

    s3 = ExitStack()
    xb_q = [sb(s3, f"xb3_{q}", [128, DCH * NT], F32) for q in range(2)]
    x0h_q = [sb(s3, f"x0h3_{q}", [128, DCH * NT], F32) for q in range(2)]
    h2b_q = [sb(s3, f"h2b3_{q}", [128, DCH * NT], BF16) for q in range(2)]
    expt_q = [sb(s3, f"expt3_{q}", [128, NT], F32) for q in range(2)]
    rows_q = [sb(s3, f"rows3_{q}", [1, 6 * 512], F32) for q in range(2)]
    bct_q = [{nm: sb(s3, f"bc3{q}_{nm}", [128, 512], F32) for nm in ("a3", "b3", "a4", "b4")}
             for q in range(2)]
    cast3 = sb(s3, "cast3", [128, DCH * NT], BF16)
    tm_sb = sb(s3, "tm3", [128, D], F32)
    tmb_sb = sb(s3, "tmb3", [128, D], BF16)
    sct_sb = sb(s3, "sct", [128, FE], F32)
    r_xb3 = [R(f"xb3_{q}") for q in range(2)]
    r_x0h = [R(f"x0h3_{q}") for q in range(2)]
    r_h2b3 = [R(f"h2b3_{q}") for q in range(2)]
    r_expt3 = [R(f"expt3_{q}") for q in range(2)]
    r_c3 = R("cast3")
    r_tm = R("tm3"); r_tmb = R("tmb3"); r_sct = R("sct")
    ROW3 = []
    r_row3 = []
    bctR3 = []
    for q in range(2):
        RW = {}
        rr = {}
        for i, nm in enumerate(["s1", "s2", "var", "rstd", "bh", "tmp"]):
            RW[nm] = rows_q[q][:, i * 512:(i + 1) * 512]
            rr[nm] = R(f"r3{q}_{nm}")
        ROW3.append(RW)
        r_row3.append(rr)
        bctR3.append({nm: R(f"b3{q}_{nm}") for nm in ("a3", "b3", "a4", "b4")})
    QB = [(4, 5), (2, 3)]

    def stats3h(q, src_ap, src_res, tag, part):
        ba, bb = QB[q]
        if part == 0:
            em.step("dve", lambda v: v.tensor_copy(out=cast3[:], in_=src_ap),
                    reads=(src_res,), writes=(r_c3,), tag=f"{tag}c")
            for d in range(DCH):
                em.step("pe", lambda t, d=d, ba=ba: t.matmul(pb[ba][0:1, :], onescol_b[:],
                                                             cast3[:, d * NT:(d + 1) * NT],
                                                             start=(d == 0), stop=(d == DCH - 1)),
                        reads=(r_c3, r_const), writes=(r_pb[ba],), accum=(d > 0), tag=f"{tag}1{d}")
        else:
            em.step("act", lambda s: s.activation(out=cast3[:], in_=src_ap, func=AF.Square),
                    reads=(src_res,), writes=(r_c3,), tag=f"{tag}q")
            for d in range(DCH):
                em.step("pe", lambda t, d=d, bb=bb: t.matmul(pb[bb][0:1, :], onescol_b[:],
                                                             cast3[:, d * NT:(d + 1) * NT],
                                                             start=(d == 0), stop=(d == DCH - 1)),
                        reads=(r_c3, r_const), writes=(r_pb[bb],), accum=(d > 0), tag=f"{tag}2{d}")

    def ln3(q, tag):
        ba, bb = QB[q]
        RW = ROW3[q]
        rr = r_row3[q]
        em.step("dve", lambda v: v.tensor_scalar(out=RW["s1"], in0=pb[ba][0:1, :], scalar1=1.0 / D,
                                                 scalar2=None, op0=OP.mult),
                reads=(r_pb[ba],), writes=(rr["s1"],), tag=f"{tag}_mu")
        em.step("dve", lambda v: v.tensor_scalar(out=RW["s2"], in0=pb[bb][0:1, :], scalar1=1.0 / D,
                                                 scalar2=None, op0=OP.mult),
                reads=(r_pb[bb],), writes=(rr["s2"],), tag=f"{tag}_ms")
        em.step("dve", lambda v: v.tensor_tensor(out=RW["tmp"], in0=RW["s1"], in1=RW["s1"], op=OP.mult),
                reads=(rr["s1"],), writes=(rr["tmp"],), tag=f"{tag}_mu2")
        em.step("dve", lambda v: v.tensor_tensor(out=RW["var"], in0=RW["s2"], in1=RW["tmp"], op=OP.subtract),
                reads=(rr["s2"], rr["tmp"]), writes=(rr["var"],), tag=f"{tag}_var")
        em.step("dve", lambda v: v.tensor_scalar(out=RW["tmp"], in0=RW["var"], scalar1=1e-5,
                                                 scalar2=None, op0=OP.add),
                reads=(rr["var"],), writes=(rr["tmp"],), tag=f"{tag}_ve")
        em.step("dve", lambda v: v.reciprocal(out=RW["tmp"], in_=RW["tmp"]),
                reads=(rr["tmp"],), writes=(rr["tmp"],), tag=f"{tag}_rv")
        em.step("act", lambda s: s.activation(out=RW["rstd"], in_=RW["tmp"], func=AF.Sqrt),
                reads=(rr["tmp"],), writes=(rr["rstd"],), tag=f"{tag}_sd")
        em.step("dve", lambda v: v.tensor_tensor(out=RW["bh"], in0=RW["s1"], in1=RW["rstd"], op=OP.mult),
                reads=(rr["s1"], rr["rstd"]), writes=(rr["bh"],), tag=f"{tag}_bh")

    def bcast3(q, slot, dstnm, tag):
        RW = ROW3[q]
        rr = r_row3[q]
        em.step("pe", lambda t: t.matmul(pb[6][0:128, :], onesrow_f[0:1, 0:128], RW[slot],
                                         start=True, stop=True),
                reads=(rr[slot], r_const), writes=(r_pb[6],), tag=f"{tag}m")
        em.step("act", lambda s: s.copy(out=bct_q[q][dstnm][:], in_=pb[6][0:128, :]),
                reads=(r_pb[6],), writes=(bctR3[q][dstnm],), tag=f"{tag}e")

    def s3_tile(tl):
        q = tl % 2
        seq = tl // 4
        base = 32 * seq
        ba, bb = QB[q]
        sfx = f"s3t{tl}"
        xb = xb_q[q]
        x0h = x0h_q[q]
        h2b = h2b_q[q]
        expt = expt_q[q]
        rxb = r_xb3[q]
        rx0 = r_x0h[q]
        rh2 = r_h2b3[q]
        rexp = r_expt3[q]
        RW = ROW3[q]
        rr = r_row3[q]
        v3l = lambda ap: ap.rearrange("p (c f) -> p c f", f=NT)
        bcv = lambda nm: bct_q[q][nm][:].rearrange("p (o f) -> p o f", o=1).to_broadcast([128, DCH, NT])
        em.dma("sp", lambda e, tl=tl, xb=xb: e.dma_start(out=xb[:], in_=accbuf[tl]),
               reads=(r_accb[tl],), writes=(rxb,), tag=f"{sfx}lb")
        em.dma("act", lambda e, tl=tl, x0h=x0h: e.dma_start(out=x0h[:], in_=x0fm[tl]),
               reads=(r_x0fm[tl],), writes=(rx0,), tag=f"{sfx}l0")
        yield
        stats3h(q, xb[:], rxb, f"{sfx}s", 0)
        yield
        stats3h(q, xb[:], rxb, f"{sfx}s", 1)
        yield
        ln3(q, f"{sfx}l3")
        yield
        bcast3(q, "rstd", "a3", f"{sfx}b3")
        bcast3(q, "bh", "b3", f"{sfx}c3")
        yield
        em.step("dve", lambda v: v.tensor_tensor(out=v3l(xb[:]), in0=v3l(xb[:]), in1=bcv("a3"), op=OP.mult),
                reads=(rxb, bctR3[q]["a3"]), writes=(rxb,), tag=f"{sfx}xa")
        em.step("dve", lambda v: v.tensor_tensor(out=v3l(xb[:]), in0=v3l(xb[:]), in1=bcv("b3"), op=OP.subtract),
                reads=(rxb, bctR3[q]["b3"]), writes=(rxb,), tag=f"{sfx}xb")
        em.step("dve", lambda v: v.tensor_tensor(out=xb[:], in0=xb[:], in1=x0h[:], op=OP.add),
                reads=(rxb, rx0), writes=(rxb,), tag=f"{sfx}xc")
        yield
        stats3h(q, xb[:], rxb, f"{sfx}u", 0)
        yield
        stats3h(q, xb[:], rxb, f"{sfx}u", 1)
        yield
        ln3(q, f"{sfx}l4")
        yield
        bcast3(q, "rstd", "a4", f"{sfx}b4")
        bcast3(q, "bh", "b4", f"{sfx}c4")
        yield
        em.step("dve", lambda v: v.tensor_tensor(out=v3l(x0h[:]), in0=v3l(xb[:]), in1=bcv("a4"), op=OP.mult),
                reads=(rxb, bctR3[q]["a4"]), writes=(rx0,), tag=f"{sfx}ha")
        em.step("dve", lambda v: v.tensor_tensor(out=v3l(x0h[:]), in0=v3l(x0h[:]), in1=bcv("b4"), op=OP.subtract),
                reads=(rx0, bctR3[q]["b4"]), writes=(rx0,), tag=f"{sfx}hb")
        em.step("dve", lambda v: v.tensor_copy(out=h2b[:], in_=x0h[:]),
                reads=(rx0,), writes=(rh2,), tag=f"{sfx}hc")
        em.dma("sp", lambda e, tl=tl, h2b=h2b: e.dma_start(out=h2buf[tl], in_=h2b[:]),
               reads=(rh2,), writes=(r_h2buf[tl],), tag=f"{sfx}hs")
        yield
        for sub in range(4):
            for bank in range(2):
                def trg(t, sub=sub, bank=bank, xb=xb):
                    inst = None
                    for reg in range(4):
                        d = reg * 2 + bank
                        inst = t.transpose(pb[bank][:, reg * 128:(reg + 1) * 128],
                                           xb[:, d * NT + sub * 128:d * NT + (sub + 1) * 128], identf[:])
                    return inst
                em.step("pe", trg, reads=(rxb, r_const), writes=(r_pb[bank],), tag=f"{sfx}tx{sub}_{bank}")

                def ev(s, b=bank):
                    inst = None
                    for reg in range(4):
                        d = reg * 2 + b
                        inst = s.copy(out=tm_sb[:, d * 128:(d + 1) * 128], in_=pb[b][:, reg * 128:(reg + 1) * 128])
                    return inst
                em.step("act", ev, reads=(r_pb[bank],), writes=(r_tm,), tag=f"{sfx}te{sub}_{bank}")
            em.dma("sp", lambda e, tl=tl, sub=sub: e.dma_start(
                out=x1tm[tl * NT + sub * 128:tl * NT + (sub + 1) * 128, :], in_=tm_sb[:]),
                reads=(r_tm,), writes=(r_x1tm[tl],), tag=f"{sfx}ts{sub}")
        yield
        for sub in range(4):
            def trg2(t, sub=sub, h2b=h2b):
                inst = None
                for d in range(DCH):
                    inst = t.transpose(pbb[:, d * 128:(d + 1) * 128],
                                       h2b[:, d * NT + sub * 128:d * NT + (sub + 1) * 128], identb[:])
                return inst
            em.step("pe", trg2, reads=(rh2, r_const), writes=(r_pbb,), tag=f"{sfx}ty{sub}")
            em.step("act", lambda s: s.copy(out=tmb_sb[:], in_=pbb),
                    reads=(r_pbb,), writes=(r_tmb,), tag=f"{sfx}tz{sub}")
            em.dma("sp", lambda e, tl=tl, sub=sub, seq=seq: e.dma_start(
                out=h2tm[seq][((tl % 4) * 4 + sub) * 128:((tl % 4) * 4 + sub + 1) * 128, :], in_=tmb_sb[:]),
                reads=(r_tmb,), writes=(r_h2tm[tl],), tag=f"{sfx}tw{sub}")
        yield
        def mmg3(t, base=base, x0h=x0h, ba=ba):
            inst = None
            for k in range(DCH):
                inst = t.matmul(pb[ba][base:base + FE, :], fgwf_sb[k][:], x0h[:, k * NT:(k + 1) * NT],
                                start=(k == 0), stop=(k == DCH - 1))
            return inst
        em.step("pe", mmg3, reads=(rx0, r_smw), writes=(r_pb[ba],), tag=f"{sfx}fl")
        em.step("act", lambda s, base=base, ba=ba, expt=expt: s.activation(
            out=expt[base:base + FE, :], in_=pb[ba][base:base + FE, :], func=AF.Exp),
                reads=(r_pb[ba],), writes=(rexp,), tag=f"{sfx}fx")
        em.step("pe", lambda t, seq=seq, bb=bb, expt=expt: t.matmul(
            pb[bb][0:1, :], sel_seq[seq][0:48, :], expt[0:48, :], start=True, stop=True),
                reads=(rexp, r_const), writes=(r_pb[bb],), tag=f"{sfx}fs")
        em.step("dve", lambda v, bb=bb: v.tensor_copy(out=RW["tmp"], in_=pb[bb][0:1, :]),
                reads=(r_pb[bb],), writes=(rr["tmp"],), tag=f"{sfx}fe")
        em.step("dve", lambda v: v.reciprocal(out=RW["tmp"], in_=RW["tmp"]),
                reads=(rr["tmp"],), writes=(rr["tmp"],), tag=f"{sfx}fr")
        em.step("pe", lambda t, base=base, bb=bb: t.matmul(pb[bb][base:base + FE, :], onesrow_f[0:1, 0:FE],
                                                           RW["tmp"], start=True, stop=True),
                reads=(rr["tmp"], r_const), writes=(r_pb[bb],), tag=f"{sfx}fb")
        em.step("dve", lambda v, tl=tl, base=base, bb=bb, expt=expt: v.tensor_tensor(
            out=scores_sb[base:base + FE, (tl % 4) * NT:((tl % 4) + 1) * NT],
            in0=expt[base:base + FE, :], in1=pb[bb][base:base + FE, :], op=OP.mult),
            reads=(rexp, r_pb[bb]), writes=(r_scores,), tag=f"{sfx}fc")
        emit_wconv(4, f"s3{tl}")
        yield

    em.step("dve", lambda v: v.memset(scores_sb[:], 0.0), writes=(r_scores,), tag="s3z")
    s3_active = deque()
    s3_next = [0]

    def s3_pump():
        if s3_next[0] < NTILES:
            s3_active.append(s3_tile(s3_next[0]))
            s3_next[0] += 1

    s3_pump()
    while s3_active:
        g = s3_active.popleft()
        try:
            next(g)
            s3_active.append(g)
        except StopIteration:
            s3_pump()

    for k in range(16):
        em.step("pe", lambda t, k=k: t.transpose(pb[6][:, 0:128], scores_sb[:, k * 128:(k + 1) * 128], identf[:]),
                reads=(r_scores, r_const), writes=(r_pb[6],), tag=f"st{k}")
        for seq in range(B_CORE):
            em.step("act", lambda s, seq=seq: s.copy(out=sct_sb[:], in_=pb[6][:, 32 * seq:32 * seq + FE]),
                    reads=(r_pb[6],), writes=(r_sct,), tag=f"st{k}e{seq}")
            em.dma("sp", lambda e, k=k, seq=seq: e.dma_start(
                out=scoreT[seq][k * 128:(k + 1) * 128, :], in_=sct_sb[:]),
                reads=(r_sct,), writes=(r_scoreT[seq],), tag=f"st{k}s{seq}")

    s3.close()
    em.barrier("b3")

    s4a = ExitStack()
    wsg_sb = [sb(s4a, f"wsg{k}", [128, FSH], BF16) for k in range(DCH)]
    wsu_sb = [sb(s4a, f"wsu{k}", [128, FSH], BF16) for k in range(DCH)]
    wsd_sb = [sb(s4a, f"wsd{k}", [128, 4 * D], BF16) for k in range(FSH // 512)]
    r_wsg = R("wsg"); r_wsu = R("wsu"); r_wsd = R("wsd")
    for k in range(DCH):
        em.dma("pool", lambda g, k=k: g.dma_start(out=wsg_sb[k][:], in_=wsgT[k * 128:(k + 1) * 128, :]),
               writes=(r_wsg,), tag=f"4ag{k}")
        em.dma("pool", lambda g, k=k: g.dma_start(out=wsu_sb[k][:], in_=wsuT[k * 128:(k + 1) * 128, :]),
               writes=(r_wsu,), tag=f"4au{k}")
    for kb in range(FSH // 512):
        em.dma("pool", lambda g, kb=kb: g.dma_start(out=wsd_sb[kb][:], in_=wsdT[kb]),
               writes=(r_wsd,), tag=f"4ad{kb}")
    h24 = sb(s4a, "h24", [128, DCH * NT], BF16)
    r_h24 = R("h24")
    hsh_sb = [sb(s4a, f"hsh{f}", [128, NT], BF16) for f in range(FSH // 128)]
    r_hsh = [R(f"hsh{f}") for f in range(FSH // 128)]
    ush_sb = sb(s4a, "ush", [128, NT], BF16)
    r_ush = R("ush")
    sho_sb = [sb(s4a, f"sho{d}", [128, NT], F32) for d in range(DCH)]
    r_sho = [R(f"sho{d}") for d in range(DCH)]
    tm4_sb = sb(s4a, "tm4", [128, D], F32)
    r_tm4 = R("tm4")

    for tl in range(NTILES):
        seq = tl // 4
        sfx = f"s4s{tl}"
        em.dma("sp", lambda e, tl=tl: e.dma_start(out=h24[:], in_=h2buf[tl]),
               reads=(r_h2buf[tl],), writes=(r_h24,), tag=f"{sfx}l")
        for f in range(FSH // 128):
            b0 = f % 2
            def mmg(t, f=f, b0=b0):
                inst = None
                for k in range(DCH):
                    inst = t.matmul(pb[b0][:, :], wsg_sb[k][:, f * 128:(f + 1) * 128],
                                    h24[:, k * NT:(k + 1) * NT],
                                    start=(k == 0), stop=(k == DCH - 1))
                return inst
            em.step("pe", mmg, reads=(r_h24, r_wsg), writes=(r_pb[b0],), tag=f"{sfx}g{f}")

            def mmg2(t, f=f, b0=b0):
                inst = None
                for k in range(DCH):
                    inst = t.matmul(pb[2 + b0][:, :], wsu_sb[k][:, f * 128:(f + 1) * 128],
                                    h24[:, k * NT:(k + 1) * NT],
                                    start=(k == 0), stop=(k == DCH - 1))
                return inst
            em.step("pe", mmg2, reads=(r_h24, r_wsu), writes=(r_pb[2 + b0],), tag=f"{sfx}u{f}")
            em.step("act", lambda s, f=f, b0=b0: s.activation(out=hsh_sb[f][:], in_=pb[b0][:, :], func=AF.Gelu),
                    reads=(r_pb[b0],), writes=(r_hsh[f],), tag=f"{sfx}e{f}")
            em.step("dve", lambda v, f=f, b0=b0: v.tensor_copy(out=ush_sb[:], in_=pb[2 + b0][:, :]),
                    reads=(r_pb[2 + b0],), writes=(r_ush,), tag=f"{sfx}v{f}")
            em.step("dve", lambda v, f=f: v.tensor_tensor(out=hsh_sb[f][:], in0=hsh_sb[f][:], in1=ush_sb[:], op=OP.mult),
                    reads=(r_hsh[f], r_ush), writes=(r_hsh[f],), tag=f"{sfx}m{f}")
        for d in range(DCH):
            bank = 4 + (d % 2)

            def mmg3(t, d=d, bank=bank):
                inst = None
                for k in range(FSH // 128):
                    inst = t.matmul(pb[bank][:, :], wsd_sb[k // 4][:, (k % 4) * D + d * 128:(k % 4) * D + (d + 1) * 128],
                                    hsh_sb[k][:], start=(k == 0), stop=(k == FSH // 128 - 1))
                return inst
            em.step("pe", mmg3, reads=tuple(r_hsh) + (r_wsd,), writes=(r_pb[bank],), tag=f"{sfx}d{d}")
            em.step("act", lambda s, d=d, bank=bank: s.copy(out=sho_sb[d][:], in_=pb[bank][:, :]),
                    reads=(r_pb[bank],), writes=(r_sho[d],), tag=f"{sfx}c{d}")
        for sub in range(4):
            for bank in range(2):
                def trg(t, sub=sub, bank=bank):
                    inst = None
                    for reg in range(4):
                        d = reg * 2 + bank
                        inst = t.transpose(pb[bank][:, reg * 128:(reg + 1) * 128],
                                           sho_sb[d][:, sub * 128:(sub + 1) * 128], identf[:])
                    return inst
                em.step("pe", trg, reads=tuple(r_sho) + (r_const,), writes=(r_pb[bank],), tag=f"{sfx}t{sub}_{bank}")

                def ev(s, b=bank):
                    inst = None
                    for reg in range(4):
                        d = reg * 2 + b
                        inst = s.copy(out=tm4_sb[:, d * 128:(d + 1) * 128], in_=pb[b][:, reg * 128:(reg + 1) * 128])
                    return inst
                em.step("act", ev, reads=(r_pb[bank],), writes=(r_tm4,), tag=f"{sfx}w{sub}_{bank}")
            em.dma("sp", lambda e, tl=tl, sub=sub, seq=seq: e.dma_start(
                out=acc2[seq][(tl % 4) * NT + sub * 128:(tl % 4) * NT + (sub + 1) * 128, :], in_=tm4_sb[:]),
                reads=(r_tm4,), writes=(r_acc2[seq],), tag=f"{sfx}s{sub}")
        emit_wconv(4, f"s4a{tl}")

    em.step("dve", lambda v: v.memset(mB_sb[:], 0.0), writes=(r_route,), tag="r_z0")
    em.step("dve", lambda v: v.memset(lo_sb[:], 0), reads=(r_scores, r_route), writes=(r_route,), tag="r_l")
    em.step("dve", lambda v: v.memset(hi_sb[:], 0x3F800001), writes=(r_route,), tag="r_h")
    for it in range(31):
        em.step("dve", lambda v: v.tensor_tensor(out=mid_sb[:], in0=lo_sb[:], in1=hi_sb[:], op=OP.add),
                reads=(r_route,), writes=(r_route,), tag=f"r_m{it}")
        em.step("dve", lambda v: v.tensor_scalar(out=mid_sb[:], in0=mid_sb[:], scalar1=shift1[0:48, :],
                                                 scalar2=None, op0=OP.arith_shift_right),
                reads=(r_route, r_const), writes=(r_route,), tag=f"r_s{it}")
        em.step("dve", lambda v: v.tensor_scalar(out=mA_sb[0:48, :], in0=scores_sb[0:48, :],
                                                 scalar1=mid_sb[:].bitcast(F32), scalar2=None, op0=OP.is_ge),
                reads=(r_route, r_scores), writes=(r_route,), tag=f"r_c{it}")
        em.step("dve", lambda v: v.reduce_sum(out=cnt_sb[:], in_=mA_sb[0:48, :], axis=AX.X),
                reads=(r_route,), writes=(r_route,), tag=f"r_n{it}")
        em.step("dve", lambda v: v.tensor_scalar(out=cond_sb[:], in0=cnt_sb[:], scalar1=float(FCAP),
                                                 scalar2=None, op0=OP.is_ge),
                reads=(r_route,), writes=(r_route,), tag=f"r_d{it}")
        em.step("dve", lambda v: v.select(out=lo_sb[:], mask=cond_sb[:], on_true=mid_sb[:], on_false=lo_sb[:]),
                reads=(r_route,), writes=(r_route,), tag=f"r_a{it}")
        em.step("dve", lambda v: v.select(out=hi_sb[:], mask=cond_sb[:], on_true=hi_sb[:], on_false=mid_sb[:]),
                reads=(r_route,), writes=(r_route,), tag=f"r_b{it}")
    em.step("dve", lambda v: v.tensor_scalar(out=mA_sb[0:48, :], in0=scores_sb[0:48, :],
                                             scalar1=lo_sb[:].bitcast(F32), scalar2=None, op0=OP.is_gt),
            reads=(r_route, r_scores), writes=(r_route,), tag="r_gt")
    em.step("dve", lambda v: v.tensor_scalar(out=mB_sb[0:48, :], in0=scores_sb[0:48, :],
                                             scalar1=lo_sb[:].bitcast(F32), scalar2=None, op0=OP.is_equal),
            reads=(r_route, r_scores), writes=(r_route,), tag="r_eq")
    em.step("dve", lambda v: v.reduce_sum(out=cnt_sb[:], in_=mA_sb[0:48, :], axis=AX.X),
            reads=(r_route,), writes=(r_route,), tag="r_ng")
    em.step("dve", lambda v: v.tensor_scalar(out=rrem_sb[:], in0=cnt_sb[:], scalar1=-1.0, scalar2=float(FCAP),
                                             op0=OP.mult, op1=OP.add),
            reads=(r_route,), writes=(r_route,), tag="r_rr")
    em.step("dve", lambda v: v.tensor_tensor_scan(out=mC_sb[0:48, :], data0=mB_sb[0:48, :], data1=mB_sb[0:48, :],
                                                  initial=0.0, op0=OP.add, op1=OP.bypass),
            reads=(r_route,), writes=(r_route,), tag="r_cs")
    em.step("dve", lambda v: v.tensor_scalar(out=mC_sb[0:48, :], in0=mC_sb[0:48, :], scalar1=rrem_sb[:],
                                             scalar2=None, op0=OP.is_le),
            reads=(r_route,), writes=(r_route,), tag="r_le")
    em.step("dve", lambda v: v.tensor_tensor(out=mB_sb[0:48, :], in0=mB_sb[0:48, :], in1=mC_sb[0:48, :], op=OP.mult),
            reads=(r_route,), writes=(r_route,), tag="r_tm")
    em.step("dve", lambda v: v.tensor_tensor(out=mA_sb[0:48, :], in0=mA_sb[0:48, :], in1=mB_sb[0:48, :], op=OP.add),
            reads=(r_route,), writes=(r_route,), tag="r_M")
    em.step("dve", lambda v: v.tensor_tensor_scan(out=mC_sb[0:48, :], data0=mA_sb[0:48, :], data1=mA_sb[0:48, :],
                                                  initial=0.0, op0=OP.add, op1=OP.bypass),
            reads=(r_route,), writes=(r_route,), tag="r_ps")
    em.step("dve", lambda v: v.tensor_tensor(out=mB_sb[0:48, :], in0=mC_sb[0:48, :], in1=mA_sb[0:48, :], op=OP.mult),
            reads=(r_route,), writes=(r_route,), tag="r_mp")
    for k in range(16):
        em.step("pe", lambda t, k=k: t.transpose(pb[6][:, 0:128], mB_sb[:, k * 128:(k + 1) * 128], identf[:]),
                reads=(r_route, r_const), writes=(r_pb[6],), tag=f"r_t{k}")
        for seq in range(B_CORE):
            em.step("act", lambda s, seq=seq, k=k: s.copy(out=mpt_sb[seq][k][:], in_=pb[6][:, 32 * seq:32 * seq + FE]),
                    reads=(r_pb[6],), writes=(r_mpt[seq],), tag=f"r_e{k}_{seq}")

    oht_sb = [[sb(s4a, f"oht_{k}", [128, FCAP], BF16) for k in range(16)]] * B_CORE
    r_oht = [R("oht")] * B_CORE
    idxr_sb = sb(s4a, "idxr", [1, 2 * FCAP], F32)
    r_idxr = R("idxr")
    idxc4_sb = sb(s4a, "idxc4", [128, 4], F32)
    r_idxc4 = R("idxc4")
    for e in range(FE):
        sfx = f"ix{e}"
        for seq in range(B_CORE):
            def ohtst(v, e=e, seq=seq):
                inst = None
                for k in range(16):
                    inst = v.tensor_scalar(out=oht_sb[seq][k][:], in0=iota256[:],
                                           scalar1=mpt_sb[seq][k][:, e:e + 1],
                                           scalar2=None, op0=OP.is_equal)
                return inst
            em.step("dve", ohtst, reads=(r_mpt[seq], r_const), writes=(r_oht[seq],), tag=f"{sfx}o{seq}")

            def ixa(t, seq=seq):
                inst = None
                for k in range(16):
                    inst = t.matmul(pb[6][0:1, 0:FCAP], iota128_b[:], oht_sb[seq][k][:],
                                    start=(k == 0), stop=(k == 15))
                for k in range(16):
                    inst = t.matmul(pb[6][0:1, 256:256 + FCAP], chunkc_b[:, k:k + 1], oht_sb[seq][k][:],
                                    start=(k == 0), stop=(k == 15))
                return inst
            em.step("pe", ixa, reads=(r_oht[seq], r_const), writes=(r_pb[6],), tag=f"{sfx}i{seq}")
            em.step("dve", lambda v, seq=seq: v.tensor_scalar(out=idxr_sb[:, seq * FCAP:(seq + 1) * FCAP],
                                                              in0=pb[6][0:1, 256:512], scalar1=128.0,
                                                              scalar2=None, op0=OP.mult),
                    reads=(r_pb[6],), writes=(r_idxr,), tag=f"{sfx}s{seq}")
            em.step("dve", lambda v, seq=seq: v.tensor_tensor(out=idxr_sb[:, seq * FCAP:(seq + 1) * FCAP],
                                                              in0=idxr_sb[:, seq * FCAP:(seq + 1) * FCAP],
                                                              in1=pb[6][0:1, 0:256], op=OP.add),
                    reads=(r_idxr, r_pb[6]), writes=(r_idxr,), tag=f"{sfx}c{seq}")

        def idxtr(t):
            inst = None
            for j in range(4):
                inst = t.matmul(pb[6][0:128, 128 + j:129 + j], idxr_sb[0:1, j * 128:(j + 1) * 128],
                                onesrow_f[0:1, 0:1], start=True, stop=True)
            return inst
        em.step("pe", idxtr, reads=(r_idxr, r_const), writes=(r_pb[6],), tag=f"{sfx}t")
        em.step("act", lambda s: s.copy(out=idxc4_sb[:], in_=pb[6][0:128, 128:132]),
                reads=(r_pb[6],), writes=(r_idxc4,), tag=f"{sfx}e")
        em.step("dve", lambda v, e=e: v.tensor_copy(out=idxi_all[e][:], in_=idxc4_sb[:]),
                reads=(r_idxc4,), writes=(r_idxall[e],), tag=f"{sfx}x")
        for j in range(4):
            seq = j // 2
            em.dma("pool", lambda g, e=e, j=j, seq=seq: g.indirect_dma_start(
                out=wcol_all[e][:, j:j + 1], out_offset=None, in_=scoreT[seq][:],
                in_offset=IndirectOffsetOnAxis(idxi_all[e][:, j:j + 1], 0), element_offset=e),
                reads=(r_idxall[e], r_scoreT[seq]), writes=(r_idxall[e],), tag=f"{sfx}w{j}")

    s4a.close()
    sr.close()
    em.barrier("b4")

    s4b = ExitStack()
    wg_sb = [[sb(s4b, f"wg{b}_{k}", [128, FEXP], BF16) for k in range(DCH)] for b in range(2)]
    r_wg = [R(f"wg{b}") for b in range(2)]
    wu_sb = [sb(s4b, f"wu{k}", [128, FEXP], BF16) for k in range(DCH)]
    wd_sb = [sb(s4b, f"wd{k}", [128, 4 * D], BF16) for k in range(FEXP // 512)]
    r_wu = R("wu"); r_wd = R("wd")
    xtt_sb = [sb(s4b, f"xtt{i}", [128, D], BF16) for i in range(4)]
    r_xtt = [R(f"xtt{i}") for i in range(4)]
    xtk_sb = [sb(s4b, f"xtk{k}", [128, 2 * FCAP], BF16) for k in range(DCH)]
    r_xtk = [R(f"xtk{k}") for k in range(DCH)]
    gf_sb = [sb(s4b, f"gf{f}", [128, 2 * FCAP], BF16) for f in range(FEXP // 128)]
    r_gf = [R(f"gf{f}") for f in range(FEXP // 128)]
    uf_sb = [sb(s4b, f"uf{i}", [128, 2 * FCAP], BF16) for i in range(2)]
    r_uf = [R(f"uf{i}") for i in range(2)]
    eo_sb = [sb(s4b, f"eo{i}", [128, D], F32) for i in range(2)]
    r_eo = [R(f"eo{i}") for i in range(2)]

    def ld_wg(e):
        b = e % 2
        for k in range(DCH):
            em.dma(("sp", "act")[k % 2], lambda g, e=e, k=k, b=b: g.dma_start(
                out=wg_sb[b][k][:], in_=wgbf[e, k * 128:(k + 1) * 128, :]),
                reads=(r_wcast[e],), writes=(r_wg[b],), tag=f"f{e}a{k}")

    def ld_wu(e):
        for k in range(DCH):
            em.dma(("sp", "act")[k % 2], lambda g, e=e, k=k: g.dma_start(
                out=wu_sb[k][:], in_=wubf[e, k * 128:(k + 1) * 128, :]),
                reads=(r_wcast[e],), writes=(r_wu,), tag=f"f{e}b{k}")

    def ld_wd(e):
        for kb in range(FEXP // 512):
            em.dma(("sp", "act")[kb % 2], lambda g, e=e, kb=kb: g.dma_start(
                out=wd_sb[kb][:], in_=wdbf[e, kb]),
                reads=(r_wcast[e],), writes=(r_wd,), tag=f"f{e}c{kb}")

    def ld_gather(e):
        for j in range(4):
            seq = j // 2
            em.dma("pool", lambda g, e=e, j=j, seq=seq: g.indirect_dma_start(
                out=xtt_sb[j][:], out_offset=None, in_=h2tm[seq][:],
                in_offset=IndirectOffsetOnAxis(idxi_all[e][:, j:j + 1], 0)),
                reads=(r_idxall[e], r_h2tm[seq * 4], r_h2tm[seq * 4 + 1], r_h2tm[seq * 4 + 2], r_h2tm[seq * 4 + 3]),
                writes=(r_xtt[j],), tag=f"f{e}gx{j}")

    ld_wg(0)
    ld_wu(0)
    ld_wd(0)
    ld_gather(0)

    for e in range(FE):
        sfx = f"f{e}"
        b = e % 2
        for j in range(4):
            def trg(t, j=j):
                inst = None
                for d in range(DCH):
                    inst = t.transpose(pbb[:, d * 128:(d + 1) * 128],
                                       xtt_sb[j][:, d * 128:(d + 1) * 128], identb[:])
                return inst
            em.step("pe", trg, reads=(r_xtt[j], r_const), writes=(r_pb[7],), tag=f"{sfx}gt{j}")

            def gev(s, j=j):
                inst = None
                for d in range(DCH):
                    inst = s.copy(out=xtk_sb[d][:, j * 128:(j + 1) * 128], in_=pbb[:, d * 128:(d + 1) * 128])
                return inst
            em.step("act", gev, reads=(r_pb[7],), writes=tuple(r_xtk), tag=f"{sfx}ge{j}")
        for f in range(FEXP // 128):
            bank = f % 2

            def gm2(t, f=f, bank=bank, b=b):
                inst = None
                for k in range(DCH):
                    inst = t.matmul(pb[bank][:, :], wg_sb[b][k][:, f * 128:(f + 1) * 128], xtk_sb[k][:],
                                    start=(k == 0), stop=(k == DCH - 1))
                return inst
            em.step("pe", gm2, reads=tuple(r_xtk) + (r_wg[b],), writes=(r_pb[bank],), tag=f"{sfx}g{f}")
            em.step("act", lambda s, f=f, bank=bank: s.activation(out=gf_sb[f][:], in_=pb[bank][:, :],
                                                                  func=AF.Gelu),
                    reads=(r_pb[bank],), writes=(r_gf[f],), tag=f"{sfx}ga{f}")
        if e + 1 < FE:
            ld_wg(e + 1)
        for f in range(FEXP // 128):
            bank = 2 + f % 2

            def gm3(t, f=f, bank=bank):
                inst = None
                for k in range(DCH):
                    inst = t.matmul(pb[bank][:, :], wu_sb[k][:, f * 128:(f + 1) * 128], xtk_sb[k][:],
                                    start=(k == 0), stop=(k == DCH - 1))
                return inst
            em.step("pe", gm3, reads=tuple(r_xtk) + (r_wu,), writes=(r_pb[bank],), tag=f"{sfx}u{f}")
            em.step("dve", lambda v, f=f, bank=bank: v.tensor_copy(out=uf_sb[f % 2][:], in_=pb[bank][:, :]),
                    reads=(r_pb[bank],), writes=(r_uf[f % 2],), tag=f"{sfx}ue{f}")
            em.step("dve", lambda v, f=f: v.tensor_tensor(out=gf_sb[f][:], in0=gf_sb[f][:], in1=uf_sb[f % 2][:], op=OP.mult),
                    reads=(r_gf[f], r_uf[f % 2]), writes=(r_gf[f],), tag=f"{sfx}hh{f}")
        if e + 1 < FE:
            ld_wu(e + 1)
            ld_gather(e + 1)
        for j in range(4):
            seq = j // 2
            for dh in range(2):
                bank = 4 + dh

                def gm4(t, j=j, dh=dh, bank=bank):
                    inst = None
                    for k in range(FEXP // 128):
                        inst = t.matmul(pb[bank][:, :], gf_sb[k][:, j * 128:(j + 1) * 128],
                                        wd_sb[k // 4][:, (k % 4) * D + dh * 512:(k % 4) * D + (dh + 1) * 512],
                                        start=(k == 0), stop=(k == FEXP // 128 - 1))
                    return inst
                em.step("pe", gm4, reads=tuple(r_gf) + (r_wd,), writes=(r_pb[bank],), tag=f"{sfx}d{j}{dh}")

            def eoev(v, j=j, e=e):
                v.tensor_scalar(out=eo_sb[j % 2][:, 0:512], in0=pb[4][:, :], scalar1=wcol_all[e][:, j:j + 1],
                                scalar2=None, op0=OP.mult)
                return v.tensor_scalar(out=eo_sb[j % 2][:, 512:1024], in0=pb[5][:, :], scalar1=wcol_all[e][:, j:j + 1],
                                       scalar2=None, op0=OP.mult)
            em.step("dve", eoev, reads=(r_pb[4], r_pb[5], r_idxall[e]), writes=(r_eo[j % 2],), tag=f"{sfx}v{j}")
            em.dma("pool", lambda g, j=j, seq=seq, e=e: g.indirect_dma_start(
                out=acc2[seq][:], out_offset=IndirectOffsetOnAxis(idxi_all[e][:, j:j + 1], 0),
                in_=eo_sb[j % 2][:], in_offset=None, compute_op=OP.add),
                reads=(r_eo[j % 2], r_idxall[e]), writes=(r_acc2[seq],), tag=f"{sfx}z{j}")
        if e + 1 < FE:
            ld_wd(e + 1)
        if e < 8:
            emit_wconv(1, f"4b{e}")

    assert wconv_pos[0] == len(wconv_chunks), wconv_pos
    s4b.close()
    em.barrier("b5")

    s5 = ExitStack()
    NB5 = 4
    ff5 = [sb(s5, f"ff5_{i}", [128, D], F32) for i in range(NB5)]
    x15 = [sb(s5, f"x15_{i}", [128, D], F32) for i in range(NB5)]
    sq5 = [sb(s5, f"sq5_{i}", [128, D], F32) for i in range(NB5)]
    o5 = [sb(s5, f"o5_{i}", [128, D], F32) for i in range(NB5)]
    mu5 = [sb(s5, f"mu5_{i}", [128, 1], F32) for i in range(NB5)]
    vr5 = [sb(s5, f"vr5_{i}", [128, 1], F32) for i in range(NB5)]
    r_ff5 = [R(f"ff5_{i}") for i in range(NB5)]
    r_x15 = [R(f"x15_{i}") for i in range(NB5)]
    r_sq5 = [R(f"sq5_{i}") for i in range(NB5)]
    r_o5 = [R(f"o5_{i}") for i in range(NB5)]
    r_mu5 = [R(f"mu5_{i}") for i in range(NB5)]
    r_vr5 = [R(f"vr5_{i}") for i in range(NB5)]

    for ch in range(NTOK // 128):
        L = ch % NB5
        E = "dve"
        seq = ch // 16
        sfx = f"s5c{ch}"
        em.dma("sp", lambda e, ch=ch, seq=seq, L=L: e.dma_start(
            out=ff5[L][:], in_=acc2[seq][(ch % 16) * 128:((ch % 16) + 1) * 128, :]),
            reads=(r_acc2[seq],), writes=(r_ff5[L],), tag=f"{sfx}a")
        em.dma("act", lambda e, ch=ch, L=L: e.dma_start(out=x15[L][:], in_=x1tm[ch * 128:(ch + 1) * 128, :]),
               reads=(r_x1tm[ch // 4],), writes=(r_x15[L],), tag=f"{sfx}b")
        em.step("dve", lambda v, L=L: v.reduce_sum(out=mu5[L][:], in_=ff5[L][:], axis=AX.X),
                reads=(r_ff5[L],), writes=(r_mu5[L],), tag=f"{sfx}c")
        em.step(E, lambda v, L=L: v.tensor_scalar(out=mu5[L][:], in0=mu5[L][:], scalar1=1.0 / D,
                                                  scalar2=None, op0=OP.mult),
                reads=(r_mu5[L],), writes=(r_mu5[L],), tag=f"{sfx}d")
        em.step(E, lambda v, L=L: v.tensor_scalar(out=o5[L][:], in0=ff5[L][:], scalar1=mu5[L][:],
                                                  scalar2=None, op0=OP.subtract),
                reads=(r_ff5[L], r_mu5[L]), writes=(r_o5[L],), tag=f"{sfx}e")
        em.step("act", lambda s, L=L: s.activation(out=sq5[L][:], in_=o5[L][:], func=AF.Square),
                reads=(r_o5[L],), writes=(r_sq5[L],), tag=f"{sfx}f")
        em.step("dve", lambda v, L=L: v.reduce_sum(out=vr5[L][:], in_=sq5[L][:], axis=AX.X),
                reads=(r_sq5[L],), writes=(r_vr5[L],), tag=f"{sfx}g")
        em.step(E, lambda v, L=L: v.tensor_scalar(out=vr5[L][:], in0=vr5[L][:], scalar1=1.0 / D,
                                                  scalar2=1e-5, op0=OP.mult, op1=OP.add),
                reads=(r_vr5[L],), writes=(r_vr5[L],), tag=f"{sfx}h")
        em.step("dve", lambda v, L=L: v.reciprocal(out=vr5[L][:], in_=vr5[L][:]),
                reads=(r_vr5[L],), writes=(r_vr5[L],), tag=f"{sfx}i")
        em.step("act", lambda s, L=L: s.activation(out=vr5[L][:], in_=vr5[L][:], func=AF.Sqrt),
                reads=(r_vr5[L],), writes=(r_vr5[L],), tag=f"{sfx}j")
        em.step(E, lambda v, L=L: v.scalar_tensor_tensor(out=o5[L][:], in0=o5[L][:], scalar=vr5[L][:],
                                                         in1=x15[L][:], op0=OP.mult, op1=OP.add),
                reads=(r_o5[L], r_vr5[L], r_x15[L]), writes=(r_o5[L],), tag=f"{sfx}k")
        em.dma("pool", lambda e, ch=ch, L=L: e.dma_start(out=out[ch * 128:(ch + 1) * 128, :], in_=o5[L][:]),
               reads=(r_o5[L],), writes=(), tag=f"{sfx}n")

    s5.close()
    em.barrier("bfin")

    sems = {s: pers.enter_context(nc.semaphore(s)) for s in em.sem_names}

    def run_section(key, eng):
        for (e, waits, fn, sem, amt, tag) in em.steps:
            if e != key:
                continue
            for (s, v) in waits:
                eng.wait_ge(sems[s], v)
            if fn is not None:
                inst = fn(eng)
                if sem is not None and inst is not None:
                    inst.then_inc(sems[sem], amt)

    with nc.Block() as block:
        @block.tensor
        def _(t):
            run_section("pe", t)

        @block.scalar
        def _(s):
            run_section("act", s)

        @block.vector
        def _(v):
            run_section("dve", v)

        @block.gpsimd
        def _(g):
            run_section("pool", g)

        @block.sync
        def _(sy):
            run_section("sp", sy)

    pers.close()
    return nc


def kernel(**inputs):
    ap = {k: np.asarray(v) for k, v in inputs.items()}
    x = ap["L0_attn_x"]
    C = np.ascontiguousarray

    def colmajor(v, nch):
        return C(v.reshape(nch, 128).T)

    shared = {
        "w_inproj": C(ap["L0_attn_in_proj_w"].reshape(8, 128, 8, 512).transpose(2, 1, 0, 3).reshape(8, 128, 4096)),
        "b_inproj": C(ap["L0_attn_in_proj_b"].reshape(1, 4096)),
        "convw": C(ap["L0_attn_conv_w"].reshape(16, 128, 3).transpose(1, 0, 2).reshape(128, 48)),
        "convb": colmajor(ap["L0_attn_conv_b"], 16),
        "dtw": C(ap["L0_attn_dt_w"]),
        "bpw": C(ap["L0_attn_bp_w"]),
        "cpw": C(ap["L0_attn_cp_w"]),
        "dtb": C(ap["L0_attn_dt_b"].reshape(64, 1)),
        "bpb": C(ap["L0_attn_bp_b"].reshape(64, 1)),
        "cpb": C(ap["L0_attn_cp_b"].reshape(64, 1)),
        "s2iw": C(ap["L0_attn_s2i_w"]),
        "s2ib": C(ap["L0_attn_s2i_b"].reshape(1, 2048)),
        "dpar": colmajor(ap["L0_attn_D_param"], 16),
        "outw": C(ap["L0_attn_out_w"].reshape(4, 4, 128, 1024).transpose(0, 2, 1, 3).reshape(4, 128, 4096)),
        "outb": C(ap["L0_attn_out_b"].reshape(1, 1024)),
        "n1w": colmajor(ap["L0_attn_norm1_w"], 8),
        "n2w": colmajor(ap["L0_attn_norm2_w"], 8),
        "gatew": C(ap["L0_attn_gate_w"]),
        "gateb": C(ap["L0_attn_gate_b"].reshape(1, 4)),
        "ew1": C(ap["L0_attn_e_w1"]),
        "eb1": C(ap["L0_attn_e_b1"].reshape(4, 32, 128).transpose(2, 0, 1).reshape(128, 128)),
        "ew2": C(ap["L0_attn_e_w2"].reshape(4, 8, 4, 128, 1024).transpose(0, 1, 3, 2, 4).reshape(4, 8, 128, 4096)),
        "eb2": C(ap["L0_attn_e_b2"].reshape(4, 8, 128).transpose(2, 0, 1).reshape(128, 32)),
        "fgw": C(ap["L0_ffn_gate_w"].T),
        "wgT": C(ap["L0_ffn_Wg"].transpose(0, 2, 1)),
        "wuT": C(ap["L0_ffn_Wu"].transpose(0, 2, 1)),
        "wdT": C(ap["L0_ffn_Wd"].transpose(0, 2, 1).reshape(16, 4, 4, 128, 1024).transpose(0, 1, 3, 2, 4).reshape(16, 4, 128, 4096)),
        "wsgT": C(ap["L0_ffn_Ws_g"].T),
        "wsuT": C(ap["L0_ffn_Ws_u"].T),
        "wsdT": C(np.ascontiguousarray(ap["L0_ffn_Ws_d"].T).reshape(4, 4, 128, 1024).transpose(0, 2, 1, 3).reshape(4, 128, 4096)),
    }
    nc = build_nc()
    in_maps = []
    for c in range(NCORES):
        m = dict(shared)
        m["x0fm"] = C(x[2 * c:2 * c + 2].reshape(NTILES, NT, DCH, 128).transpose(0, 3, 2, 1)
                      .reshape(NTILES, 128, DCH * NT))
        in_maps.append(m)
    res = run_bass_kernel_spmd(nc, in_maps, core_ids=list(range(NCORES)))
    outs = [res.results[c]["out"].reshape(2, T, D) for c in range(NCORES)]
    return np.concatenate(outs, axis=0).astype(np.float32)


# version 22
# speedup vs baseline: 1.0075x; 1.0075x over previous
import numpy as np
from collections import deque
from contextlib import ExitStack

import concourse.bass as bass
import concourse.mybir as mybir
from concourse.bass_utils import run_bass_kernel_spmd
from concourse.bass_core_rust import IndirectOffsetOnAxis

F32 = mybir.dt.float32
BF16 = mybir.dt.bfloat16
I32 = mybir.dt.int32
OP = mybir.AluOpType
AF = mybir.ActivationFunctionType
AX = mybir.AxisListType

NCORES = 8
B_CORE = 2
T = 2048
NTOK = B_CORE * T
D = 1024
INNER = 2048
DSTATE = 64
NT = 512
NTILES = NTOK // NT
DCH = D // 128
ICH = INNER // 128
HID = 4096
HCH = HID // 128
NEXP = 4
FE = 16
FCAP = 256
FEXP = 2048
FSH = 2048

ENGS = ("pe", "act", "dve", "pool", "sp")
QUEUES = ("sp", "act", "pool")
NQSEM = 4


class R:
    __slots__ = ("name", "w", "readers")

    def __init__(self, name):
        self.name = name
        self.w = None
        self.readers = []


class Em:
    def __init__(self):
        self.steps = []
        self.sem_names = list(ENGS) + [f"q{q}{i}" for q in QUEUES for i in range(NQSEM)]
        self.cnt = {s: 0 for s in self.sem_names}
        self.known = {e: {} for e in ENGS}
        self.qrot = {q: 0 for q in QUEUES}

    def _dep(self, eng, waits, dep):
        if dep is None:
            return
        sem, val = dep
        k = self.known[eng]
        if k.get(sem, 0) >= val:
            return
        k[sem] = val
        waits.append((sem, val))

    def _gather(self, eng, reads, writes, accum):
        waits = []
        for r in reads:
            self._dep(eng, waits, r.w)
        for r in writes:
            if r.w is not None:
                if r.w[0] != eng and not accum:
                    self._dep(eng, waits, r.w)
            for rd in r.readers:
                if rd[0] != eng and not accum:
                    self._dep(eng, waits, rd)
        return waits

    def _commit(self, reads, writes, sem, myval):
        for r in reads:
            r.readers.append((sem, myval))
        for r in writes:
            r.w = (sem, myval)
            r.readers = []

    def step(self, eng, fn, reads=(), writes=(), accum=False, tag=""):
        waits = self._gather(eng, reads, writes, accum)
        self.cnt[eng] += 1
        myval = self.cnt[eng]
        self._commit(reads, writes, eng, myval)
        self.steps.append((eng, waits, fn, eng, 1, tag))

    def dma(self, q, fn, reads=(), writes=(), tag=""):
        waits = self._gather(q, reads, writes, False)
        i = self.qrot[q]
        self.qrot[q] = (i + 1) % NQSEM
        sem = f"q{q}{i}"
        self.cnt[sem] += 16
        myval = self.cnt[sem]
        self._commit(reads, writes, sem, myval)
        self.steps.append((q, waits, fn, sem, 16, tag))

    def barrier(self, tag="bar"):
        snap = {s: v for s, v in self.cnt.items() if v > 0}
        for e in ENGS:
            waits = []
            for s, v in snap.items():
                if s == e:
                    continue
                if self.known[e].get(s, 0) < v:
                    self.known[e][s] = v
                    waits.append((s, v))
            self.steps.append((e, waits, None, None, 0, tag))


def build_nc():
    nc = bass.Bass("TRN2", target_bir_lowering=False)
    em = Em()
    pers = ExitStack()

    def din(name, shape, dtype=F32):
        return nc.dram_tensor(name, shape, dtype, kind="ExternalInput").ap()

    x0fm = din("x0fm", [NTILES, 128, DCH * NT])
    w_inproj = din("w_inproj", [8, 128, 8 * 512])
    b_inproj = din("b_inproj", [1, 2 * INNER])
    convw = din("convw", [128, 48])
    convb = din("convb", [128, 16])
    dtw = din("dtw", [INNER, DSTATE])
    bpw = din("bpw", [INNER, DSTATE])
    cpw = din("cpw", [INNER, DSTATE])
    dtb = din("dtb", [DSTATE, 1])
    bpb = din("bpb", [DSTATE, 1])
    cpb = din("cpb", [DSTATE, 1])
    s2iw = din("s2iw", [DSTATE, INNER])
    s2ib = din("s2ib", [1, INNER])
    dpar = din("dpar", [128, 16])
    outw = din("outw", [4, 128, 4 * D])
    outb = din("outb", [1, D])
    n1w = din("n1w", [128, 8])
    n2w = din("n2w", [128, 8])
    gatew = din("gatew", [D, NEXP])
    gateb = din("gateb", [1, NEXP])
    ew1 = din("ew1", [NEXP, D, HID])
    eb1 = din("eb1", [128, NEXP * 32])
    ew2 = din("ew2", [NEXP, 8, 128, 4 * D])
    eb2 = din("eb2", [128, NEXP * 8])
    fgw = din("fgw", [D, FE])
    wgT = din("wgT", [FE, D, FEXP])
    wuT = din("wuT", [FE, D, FEXP])
    wdT = din("wdT", [FE, 4, 128, 4 * D])
    wsgT = din("wsgT", [D, FSH])
    wsuT = din("wsuT", [D, FSH])
    wsdT = din("wsdT", [4, 128, 4 * D])
    out = nc.dram_tensor("out", [NTOK, D], F32, kind="ExternalOutput").ap()

    def scr(name, shape, dtype):
        return nc.dram_tensor(name, shape, dtype, kind="Internal").ap()

    w1bf = scr("w1bf", [8, 128, 8 * 512], BF16)
    owbf = scr("owbf", [4, 128, 4 * D], BF16)
    gbuf = scr("gbuf", [NTILES, 128, DCH * NT], BF16)
    accbuf = scr("accbuf", [NTILES, 128, DCH * NT], F32)
    x1tm = scr("x1tm", [NTOK, D], F32)
    h2buf = scr("h2buf", [NTILES, 128, DCH * NT], BF16)
    h2tm = [scr(f"h2tm_{s}", [T, D], BF16) for s in range(B_CORE)]
    acc2 = [scr(f"acc2_{s}", [T, D], F32) for s in range(B_CORE)]
    scoreT = [scr(f"scoreT_{s}", [T, FE], F32) for s in range(B_CORE)]
    wrows = scr("wrows", [NEXP, NTOK], F32)

    wgbf = scr("wgbf", [FE, D, FEXP], BF16)
    wubf = scr("wubf", [FE, D, FEXP], BF16)
    wdbf = scr("wdbf", [FE, 4, 128, 4 * D], BF16)
    r_wcast = [R(f"wcast{e}") for e in range(FE)]
    wconv_chunks = []
    for e in range(FE):
        for half in range(2):
            wconv_chunks.append((wgbf[e, half * 512:(half + 1) * 512, :],
                                 wgT[e, half * 512:(half + 1) * 512, :], e))
        for half in range(2):
            wconv_chunks.append((wubf[e, half * 512:(half + 1) * 512, :],
                                 wuT[e, half * 512:(half + 1) * 512, :], e))
        for half in range(2):
            wconv_chunks.append((wdbf[e, half * 2:(half + 1) * 2],
                                 wdT[e, half * 2:(half + 1) * 2], e))
    wconv_pos = [0]

    def emit_wconv(n, tagp):
        for i in range(n):
            if wconv_pos[0] >= len(wconv_chunks):
                return
            dst, src, ce = wconv_chunks[wconv_pos[0]]
            wconv_pos[0] += 1
            em.dma("pool", lambda g, dst=dst, src=src: g.dma_start(out=dst, in_=src),
                   writes=(r_wcast[ce],), tag=f"wcv{tagp}_{i}")

    r_x0fm = [R(f"x0fm{t}") for t in range(NTILES)]
    r_w1bf = R("w1bf"); r_owbf = R("owbf")
    r_gbuf = [R(f"gbuf{t}") for t in range(NTILES)]
    r_accb = [R(f"accb{t}") for t in range(NTILES)]
    r_x1tm = [R(f"x1tm{t}") for t in range(NTILES)]
    r_h2buf = [R(f"h2buf{t}") for t in range(NTILES)]
    r_h2tm = [R(f"h2tm{t}") for t in range(NTILES)]
    r_acc2 = [R(f"acc2_{s}") for s in range(B_CORE)]
    r_scoreT = [R(f"scoreT_{s}") for s in range(B_CORE)]
    r_wrows = R("wrows")

    pA = pers.enter_context(nc.psum_tensor("pA", [128, 1024], F32))
    pB = pers.enter_context(nc.psum_tensor("pB", [128, 1024], F32))
    p4 = pers.enter_context(nc.psum_tensor("p4", [128, 512], F32))
    p5 = pers.enter_context(nc.psum_tensor("p5", [128, 512], F32))
    p6 = pers.enter_context(nc.psum_tensor("p6", [128, 512], F32))
    p7 = pers.enter_context(nc.psum_tensor("p7", [128, 512], F32))
    pbb = p7[:].bitcast(BF16)
    pb = [pA[:, 0:512], pA[:, 512:1024], pB[:, 0:512], pB[:, 512:1024], p4[:], p5[:], p6[:], p7[:]]
    r_pb = [R("b0"), R("b1"), R("b2"), R("b3"), R("p4"), R("p5"), R("p6"), R("p7")]
    r_pbb = r_pb[7]

    def sb(stack, name, shape, dtype):
        return stack.enter_context(nc.sbuf_tensor(name, shape, dtype))

    onescol_b = sb(pers, "onescol_b", [128, 1], BF16)
    onesrow_b = sb(pers, "onesrow_b", [1, 512], BF16)
    onescol_f = sb(pers, "onescol_f", [128, 1], F32)
    onesrow_f = sb(pers, "onesrow_f", [1, 128], F32)
    identf = sb(pers, "identf", [128, 128], F32)
    identb = sb(pers, "identb", [128, 128], BF16)
    pidx_f = sb(pers, "pidx_f", [128, 1], F32)
    iota256 = sb(pers, "iota256", [128, 256], F32)
    iota128_b = sb(pers, "iota128_b", [128, 1], BF16)
    chunkc_b = sb(pers, "chunkc_b", [128, 16], BF16)
    shift1 = sb(pers, "shift1", [128, 1], I32)
    sel_seq = [sb(pers, f"sel{s}", [128, 1], F32) for s in range(B_CORE)]
    r_const = R("const")

    em.step("pool", lambda g: g.iota(pidx_f[:], pattern=[[0, 1]], base=0, channel_multiplier=1,
                                     allow_small_or_imprecise_dtypes=True), writes=(r_const,), tag="c0")
    em.step("pool", lambda g: g.iota(identf[:], pattern=[[1, 128]], base=0, channel_multiplier=0,
                                     allow_small_or_imprecise_dtypes=True), reads=(r_const,), writes=(r_const,), tag="c1")
    em.step("pool", lambda g: g.tensor_scalar(out=identf[:], in0=identf[:], scalar1=pidx_f[:],
                                              scalar2=1.0, op0=OP.is_equal, op1=OP.mult),
            reads=(r_const,), writes=(r_const,), tag="c2")
    em.step("pool", lambda g: g.iota(iota256[:], pattern=[[1, 256]], base=1, channel_multiplier=0,
                                     allow_small_or_imprecise_dtypes=True), reads=(r_const,), writes=(r_const,), tag="c3")
    em.step("pool", lambda g: g.iota(iota128_b[:], pattern=[[0, 1]], base=0, channel_multiplier=1,
                                     allow_small_or_imprecise_dtypes=True), reads=(r_const,), writes=(r_const,), tag="c4")
    em.step("pool", lambda g: g.iota(chunkc_b[:], pattern=[[1, 16]], base=0, channel_multiplier=0,
                                     allow_small_or_imprecise_dtypes=True), reads=(r_const,), writes=(r_const,), tag="c5")
    em.step("dve", lambda v: v.tensor_copy(out=identb[:], in_=identf[:]),
            reads=(r_const,), writes=(r_const,), tag="c6")
    em.step("dve", lambda v: v.memset(onescol_b[:], 1.0), reads=(r_const,), writes=(r_const,), tag="c7")
    em.step("dve", lambda v: v.memset(onescol_f[:], 1.0), reads=(r_const,), writes=(r_const,), tag="c8")
    em.step("dve", lambda v: v.memset(onesrow_f[:], 1.0), reads=(r_const,), writes=(r_const,), tag="c9")
    em.step("dve", lambda v: v.memset(onesrow_b[:], 1.0), reads=(r_const,), writes=(r_const,), tag="c9b")
    em.step("dve", lambda v: v.memset(shift1[:], 1), reads=(r_const,), writes=(r_const,), tag="c10")
    for s in range(B_CORE):
        em.step("dve", lambda v, s=s: v.memset(sel_seq[s][:], 0.0), reads=(r_const,), writes=(r_const,), tag=f"c11{s}")
        em.step("dve", lambda v, s=s: v.memset(sel_seq[s][32 * s:32 * s + 16, :], 1.0),
                reads=(r_const,), writes=(r_const,), tag=f"c12{s}")

    bib_row = sb(pers, "bib_row", [1, 2 * INNER], BF16)
    outb_row = sb(pers, "outb_row", [1, D], BF16)
    s2ib_row = sb(pers, "s2ib_row", [1, INNER], BF16)
    convw_b = sb(pers, "convw_b", [128, 48], BF16)
    convw_f = sb(pers, "convw_f", [128, 48], F32)
    convb_sb = sb(pers, "convb_sb", [128, 16], F32)
    dtb_sb = sb(pers, "dtb_sb", [DSTATE, 1], F32)
    ndtb_sb = sb(pers, "ndtb_sb", [DSTATE, 1], F32)
    bpb_sb = sb(pers, "bpb_sb", [DSTATE, 1], F32)
    cpb_sb = sb(pers, "cpb_sb", [DSTATE, 1], F32)
    dpar_b = sb(pers, "dpar_b", [128, 16], BF16)
    n1w_sb = sb(pers, "n1w_sb", [128, 8], F32)
    n2w_sb = sb(pers, "n2w_sb", [128, 8], F32)
    gatew_sb = [sb(pers, f"gatew_sb{k}", [128, NEXP], BF16) for k in range(DCH)]
    gatewf_sb = [sb(pers, f"gatewf_sb{k}", [128, NEXP], F32) for k in range(DCH)]
    gbrep_sb = sb(pers, "gbrep_sb", [128, NEXP], F32)
    eb1_sb = sb(pers, "eb1_sb", [128, NEXP * 32], F32)
    eb2_sb = sb(pers, "eb2_sb", [128, NEXP * 8], F32)
    fgw_sb = [sb(pers, f"fgw_sb{k}", [128, FE], BF16) for k in range(DCH)]
    fgwf_sb = [sb(pers, f"fgwf_sb{k}", [128, FE], F32) for k in range(DCH)]
    gateb_row = sb(pers, "gateb_row", [1, NEXP], F32)
    mpt_sb = [[sb(pers, f"mpt{s}_{k}", [128, FE], F32) for k in range(16)] for s in range(B_CORE)]
    idxi_all = [sb(pers, f"idxi{e}", [128, 4], I32) for e in range(FE)]
    wcol_all = [sb(pers, f"wcolp{e}", [128, 4], F32) for e in range(FE)]
    r_idxall = [R(f"idxall{e}") for e in range(FE)]
    r_smw = R("smw")
    r_mpt = [R(f"mpt{s}") for s in range(B_CORE)]

    def ldp(dst, src, tag):
        em.dma("sp", lambda e: e.dma_start(out=dst, in_=src), writes=(r_smw,), tag=tag)

    def ldc(dst, src, tag):
        em.dma("pool", lambda e: e.dma_start(out=dst, in_=src), writes=(r_smw,), tag=tag)

    ldc(bib_row[:], b_inproj[:], "ld_bi")
    ldc(outb_row[:], outb[:], "ld_ob")
    ldc(s2ib_row[:], s2ib[:], "ld_sb")
    ldc(convw_b[:], convw[:], "ld_convw")
    ldp(convw_f[:], convw[:], "ld_convwf")
    ldp(convb_sb[:], convb[:], "ld_convb")
    ldp(dtb_sb[:], dtb[:], "ld_dtb")
    ldp(bpb_sb[:], bpb[:], "ld_bpb")
    ldp(cpb_sb[:], cpb[:], "ld_cpb")
    ldc(dpar_b[:], dpar[:], "ld_dpar")
    ldp(n1w_sb[:], n1w[:], "ld_n1w")
    ldp(n2w_sb[:], n2w[:], "ld_n2w")
    for k in range(DCH):
        ldc(gatew_sb[k][:], gatew[k * 128:(k + 1) * 128, :], f"ld_gw{k}")
        ldc(fgw_sb[k][:], fgw[k * 128:(k + 1) * 128, :], f"ld_fgw{k}")
        ldp(gatewf_sb[k][:], gatew[k * 128:(k + 1) * 128, :], f"ld_gwf{k}")
        ldp(fgwf_sb[k][:], fgw[k * 128:(k + 1) * 128, :], f"ld_fgwf{k}")
    ldp(eb1_sb[:], eb1[:], "ld_eb1")
    ldp(eb2_sb[:], eb2[:], "ld_eb2")
    ldp(gateb_row[:], gateb[:], "ld_gb")
    em.step("pe", lambda t: t.matmul(pb[0][0:128, 0:NEXP], onesrow_f[:], gateb_row[:], start=True, stop=True),
            reads=(r_smw, r_const), writes=(r_pb[0],), tag="gbm")
    em.step("act", lambda s: s.copy(out=gbrep_sb[:], in_=pb[0][0:128, 0:NEXP]),
            reads=(r_pb[0],), writes=(r_smw,), tag="gbe")
    em.step("dve", lambda v: v.tensor_scalar(out=ndtb_sb[:], in0=dtb_sb[:], scalar1=-1.0, scalar2=None, op0=OP.mult),
            reads=(r_smw,), writes=(r_smw,), tag="ndtb")

    s01 = ExitStack()
    em.barrier("b0")

    dtw_sb = [sb(s01, f"dtw{k}", [128, DSTATE], BF16) for k in range(ICH)]
    bpw_sb = [sb(s01, f"bpw{k}", [128, DSTATE], BF16) for k in range(ICH)]
    cpw_sb = [sb(s01, f"cpw{k}", [128, DSTATE], BF16) for k in range(ICH)]
    s2iw_sb = sb(s01, "s2iw_sb", [DSTATE, INNER], BF16)
    for k in range(ICH):
        ldc(dtw_sb[k][:], dtw[k * 128:(k + 1) * 128, :], f"ld_dtw{k}")
        ldc(bpw_sb[k][:], bpw[k * 128:(k + 1) * 128, :], f"ld_bpw{k}")
        ldc(cpw_sb[k][:], cpw[k * 128:(k + 1) * 128, :], f"ld_cpw{k}")
    ldc(s2iw_sb[:], s2iw[:], "ld_s2iw")

    w1blk = [sb(s01, f"w1blk{i}", [128, 8 * 512], BF16) for i in range(2)]
    r_w1blk = [R(f"w1blk{i}") for i in range(2)]
    outw_blk = [sb(s01, f"owb{i}", [128, 4 * D], BF16) for i in range(2)]
    r_owb = [R(f"owb{i}") for i in range(2)]

    def w1_load(tl, blk, buf):
        if tl == 0:
            em.dma("pool", lambda e, blk=blk, buf=buf: e.dma_start(out=w1blk[buf][:], in_=w_inproj[blk]),
                   writes=(r_w1blk[buf],), tag=f"wc{blk}")
            em.dma("sp", lambda e, blk=blk, buf=buf: e.dma_start(out=w1bf[blk], in_=w1blk[buf][:]),
                   reads=(r_w1blk[buf],), writes=(r_w1bf,), tag=f"ws{blk}")
        else:
            em.dma("pool", lambda e, blk=blk, buf=buf: e.dma_start(
                out=w1blk[buf][:], in_=w1bf[blk]),
                reads=(r_w1bf,), writes=(r_w1blk[buf],), tag=f"t{tl}lw{blk}")

    def outw_load(tl, half, kb, buf):
        if tl == 0 and half == 0:
            em.dma("pool", lambda e, kb=kb, buf=buf: e.dma_start(out=outw_blk[buf][:], in_=outw[kb]),
                   writes=(r_owb[buf],), tag=f"oc{kb}")
            em.dma("sp", lambda e, kb=kb, buf=buf: e.dma_start(out=owbf[kb], in_=outw_blk[buf][:]),
                   reads=(r_owb[buf],), writes=(r_owbf,), tag=f"os{kb}")
        else:
            em.dma("pool", lambda e, kb=kb, buf=buf: e.dma_start(
                out=outw_blk[buf][:], in_=owbf[kb]),
                reads=(r_owbf,), writes=(r_owb[buf],), tag=f"t{tl}lo{half}_{kb}")

    x0t = sb(s01, "x0t", [128, DCH * NT], F32)
    r_x0t = R("x0t")
    castb = sb(s01, "castb", [128, DCH * NT], BF16)
    r_cast = R("castb")
    cvtA = sb(s01, "cvtA", [128, DCH * NT], F32)
    r_cvtA = R("cvtA")
    xme = sb(s01, "xme", [128, ICH, NT + 2], BF16)
    r_xme = R("xme")
    cv1 = [sb(s01, "cv1_0", [128, 2, NT], BF16)] * 2
    r_cv1 = [R("cv1_0")] * 2
    svt = [sb(s01, "svt0", [128, 2, NT], BF16)] * 2
    r_svt = [R("svt0")] * 2
    xms = sb(s01, "xms", [128, ICH * NT], BF16)
    r_xms = R("xms")
    sgate = sb(s01, "sgate", [128, ICH * NT], BF16)
    r_sg = R("sgate")
    tmp1k = [sb(s01, "tmp1k0", [128, NT], F32)] * 2
    r_tmp1k = [R("tmp1k0")] * 2

    dt_sb = sb(s01, "dt_sb", [DSTATE, NT], F32)
    asc_sb = sb(s01, "asc_sb", [DSTATE, NT], F32)
    bm_sb = sb(s01, "bm_sb", [DSTATE, NT], F32)
    cm_sb = sb(s01, "cm_sb", [DSTATE, NT], F32)
    bsc_sb = sb(s01, "bsc_sb", [DSTATE, NT], F32)
    yst_sb = sb(s01, "yst_sb", [DSTATE, NT], F32)
    yv_sb = sb(s01, "yv_sb", [DSTATE, NT], F32)
    ybf_sb = sb(s01, "ybf_sb", [DSTATE, NT], BF16)
    ysq_sb = ybf_sb
    ynrm_sb = sb(s01, "ynrm_sb", [DSTATE, NT], BF16)
    state_sb = sb(s01, "state_sb", [DSTATE, 1], F32)
    r_dt = R("dt"); r_asc = R("asc"); r_bm = R("bm"); r_cm = R("cm")
    r_bsc = R("bsc"); r_yst = R("yst"); r_yv = R("yv"); r_ybf = R("ybf")
    r_ysq = r_ybf; r_ynrm = R("ynrm"); r_state = R("state")

    rows1 = sb(s01, "rows1", [1, 8 * 512], F32)
    ROW = {}
    r_row = {}
    for i, nm in enumerate(["s1", "s2", "var", "rstd", "af", "bf", "bh", "tmp"]):
        ROW[nm] = rows1[:, i * 512:(i + 1) * 512]
        r_row[nm] = R(f"row_{nm}")
    bct = {}
    r_bct = {}
    for i, nm in enumerate(["af", "bf", "ah", "bh", "x1"]):
        bct[nm] = sb(s01, f"bc_{nm}", [128, 512], F32)
        r_bct[nm] = R(f"bc_{nm}")

    wfull_sb = sb(s01, "wfull_sb", [128, 16], F32)
    lgt_sb = sb(s01, "lgt_sb", [128, 16], F32)
    m1_sb = sb(s01, "m1_sb", [128, 4], F32)
    m2_sb = sb(s01, "m2_sb", [128, 4], F32)
    tA_sb = sb(s01, "tA_sb", [128, 4], F32)
    tB_sb = sb(s01, "tB_sb", [128, 4], F32)
    es_sb = sb(s01, "es_sb", [128, 16], F32)
    msk_sb = sb(s01, "msk_sb", [128, 16], F32)
    r_wfull = R("wfull"); r_lgt = R("lgt"); r_top2 = R("top2")
    wrow_sb = rows1[:, 7 * 512:8 * 512]
    r_wrowsb = r_row["tmp"]

    def v3(ap2d, nch=DCH):
        return ap2d.rearrange("p (c f) -> p c f", f=NT)

    def bc3(nm, nch=DCH):
        return bct[nm][:].rearrange("p (o f) -> p o f", o=1).to_broadcast([128, nch, NT])

    def cols3(ap2d, nch, step=1, off=0):
        return ap2d[:, off::step].rearrange("p (c o) -> p c o", o=1)[:, 0:nch, :]

    def ln_rows(with_rms, tag):
        A = dict(ROW)
        em.step("dve", lambda v, A=A: v.tensor_scalar(out=A["s1"], in0=pb[4][0:1, :], scalar1=1.0 / D,
                                                      scalar2=None, op0=OP.mult),
                reads=(r_pb[4],), writes=(r_row["s1"],), tag=f"{tag}_mu")
        em.step("dve", lambda v, A=A: v.tensor_scalar(out=A["s2"], in0=pb[5][0:1, :], scalar1=1.0 / D,
                                                      scalar2=None, op0=OP.mult),
                reads=(r_pb[5],), writes=(r_row["s2"],), tag=f"{tag}_ms")
        em.step("dve", lambda v, A=A: v.tensor_tensor(out=A["tmp"], in0=A["s1"], in1=A["s1"], op=OP.mult),
                reads=(r_row["s1"],), writes=(r_row["tmp"],), tag=f"{tag}_mu2")
        em.step("dve", lambda v, A=A: v.tensor_tensor(out=A["var"], in0=A["s2"], in1=A["tmp"], op=OP.subtract),
                reads=(r_row["s2"], r_row["tmp"]), writes=(r_row["var"],), tag=f"{tag}_var")
        em.step("dve", lambda v, A=A: v.tensor_scalar(out=A["tmp"], in0=A["var"], scalar1=1e-5,
                                                      scalar2=None, op0=OP.add),
                reads=(r_row["var"],), writes=(r_row["tmp"],), tag=f"{tag}_ve")
        em.step("dve", lambda v, A=A: v.reciprocal(out=A["tmp"], in_=A["tmp"]),
                reads=(r_row["tmp"],), writes=(r_row["tmp"],), tag=f"{tag}_rv")
        em.step("act", lambda s, A=A: s.activation(out=A["rstd"], in_=A["tmp"], func=AF.Sqrt),
                reads=(r_row["tmp"],), writes=(r_row["rstd"],), tag=f"{tag}_sd")
        if with_rms:
            em.step("dve", lambda v, A=A: v.tensor_tensor(out=A["af"], in0=A["var"], in1=A["tmp"], op=OP.mult),
                    reads=(r_row["var"], r_row["tmp"]), writes=(r_row["af"],), tag=f"{tag}_q")
            em.step("dve", lambda v, A=A: v.tensor_scalar(out=A["af"], in0=A["af"], scalar1=1e-6,
                                                          scalar2=None, op0=OP.add),
                    reads=(r_row["af"],), writes=(r_row["af"],), tag=f"{tag}_qe")
            em.step("dve", lambda v, A=A: v.reciprocal(out=A["af"], in_=A["af"]),
                    reads=(r_row["af"],), writes=(r_row["af"],), tag=f"{tag}_rq")
            em.step("act", lambda s, A=A: s.activation(out=A["af"], in_=A["af"], func=AF.Sqrt),
                    reads=(r_row["af"],), writes=(r_row["af"],), tag=f"{tag}_r2")
            em.step("dve", lambda v, A=A: v.tensor_tensor(out=A["af"], in0=A["af"], in1=A["rstd"], op=OP.mult),
                    reads=(r_row["af"], r_row["rstd"]), writes=(r_row["af"],), tag=f"{tag}_af")
            em.step("dve", lambda v, A=A: v.tensor_tensor(out=A["bf"], in0=A["s1"], in1=A["af"], op=OP.mult),
                    reads=(r_row["s1"], r_row["af"]), writes=(r_row["bf"],), tag=f"{tag}_bf")
        em.step("dve", lambda v, A=A: v.tensor_tensor(out=A["bh"], in0=A["s1"], in1=A["rstd"], op=OP.mult),
                reads=(r_row["s1"], r_row["rstd"]), writes=(r_row["bh"],), tag=f"{tag}_bh")

    def bcast(rowap, rowres, bcname, tag, nrows=128):
        em.step("pe", lambda t: t.matmul(pb[6][0:nrows, :], onesrow_f[0:1, 0:nrows], rowap,
                                         start=True, stop=True),
                reads=(rowres, r_const), writes=(r_pb[6],), tag=f"{tag}m")
        em.step("act", lambda s: s.copy(out=bct[bcname][0:nrows, :], in_=pb[6][0:nrows, :]),
                reads=(r_pb[6],), writes=(r_bct[bcname],), tag=f"{tag}e")

    def stats_pass(src_ap, src_res, tag, rms_only=False):
        if not rms_only:
            em.step("dve", lambda v: v.tensor_copy(out=castb[:], in_=src_ap),
                    reads=(src_res,), writes=(r_cast,), tag=f"{tag}c")
            for d in range(DCH):
                em.step("pe", lambda t, d=d: t.matmul(pb[4][0:1, :], onescol_b[:], castb[:, d * NT:(d + 1) * NT],
                                                      start=(d == 0), stop=(d == DCH - 1)),
                        reads=(r_cast, r_const), writes=(r_pb[4],), accum=(d > 0), tag=f"{tag}1{d}")
        em.step("act", lambda s: s.activation(out=castb[:], in_=src_ap, func=AF.Square),
                reads=(src_res,), writes=(r_cast,), tag=f"{tag}q")
        for d in range(DCH):
            em.step("pe", lambda t, d=d: t.matmul(pb[5][0:1, :], onescol_b[:], castb[:, d * NT:(d + 1) * NT],
                                                  start=(d == 0), stop=(d == DCH - 1)),
                    reads=(r_cast, r_const), writes=(r_pb[5],), accum=(d > 0), tag=f"{tag}2{d}")

    for tl in range(NTILES):
        sfx = f"t{tl}"
        em.dma("sp", lambda e, tl=tl: e.dma_start(out=x0t[:], in_=x0fm[tl]),
            reads=(r_x0fm[tl],), writes=(r_x0t,), tag=f"{sfx}ldx")
        stats_pass(x0t[:], r_x0t, f"{sfx}s")
        ln_rows(True, f"{sfx}ln1")
        bcast(ROW["af"], r_row["af"], "af", f"{sfx}baf")
        bcast(ROW["bf"], r_row["bf"], "bf", f"{sfx}bbf")
        bcast(ROW["rstd"], r_row["rstd"], "ah", f"{sfx}bah")
        bcast(ROW["bh"], r_row["bh"], "bh", f"{sfx}bbh")
        em.step("dve", lambda v: v.tensor_tensor(out=v3(cvtA[:]), in0=v3(x0t[:]), in1=bc3("af"), op=OP.mult),
                reads=(r_x0t, r_bct["af"]), writes=(r_cvtA,), tag=f"{sfx}h1")
        em.step("dve", lambda v: v.tensor_tensor(out=v3(cvtA[:]), in0=v3(cvtA[:]), in1=bc3("bf"), op=OP.subtract),
                reads=(r_cvtA, r_bct["bf"]), writes=(r_cvtA,), tag=f"{sfx}h2")
        em.step("dve", lambda v: v.tensor_tensor(out=v3(castb[:]), in0=v3(cvtA[:]),
                                                 in1=cols3(n1w_sb[:], DCH).to_broadcast([128, DCH, NT]), op=OP.mult),
                reads=(r_cvtA, r_smw), writes=(r_cast,), tag=f"{sfx}h3")
        if tl % 4 == 0:
            em.step("dve", lambda v: v.memset(xme[:, :, 0:2], 0.0),
                    reads=(r_xme,), writes=(r_xme,), tag=f"{sfx}cz")
        else:
            em.step("dve", lambda v: v.tensor_copy(out=xme[:, :, 0:2], in_=xme[:, :, NT:NT + 2]),
                    reads=(r_xme,), writes=(r_xme,), tag=f"{sfx}cc")
        for blk in range(8):
            buf = blk % 2
            w1_load(tl, blk, buf)
            for fl in range(4):
                f = blk * 4 + fl
                bank = f % 4

                def mmg(t, fl=fl, buf=buf, bank=bank, f=f):
                    for k in range(DCH):
                        t.matmul(pb[bank], w1blk[buf][:, k * 512 + fl * 128:k * 512 + (fl + 1) * 128],
                                 castb[:, k * NT:(k + 1) * NT], start=(k == 0), stop=False)
                    return t.matmul(pb[bank], bib_row[0:1, f * 128:(f + 1) * 128], onesrow_b[:],
                                    start=False, stop=True)
                em.step("pe", mmg, reads=(r_w1blk[buf], r_cast, r_smw, r_const),
                        writes=(r_pb[bank],), tag=f"{sfx}ip{f}")
                if f % 2 == 1:
                    pair = pA if bank == 1 else pB
                    rpair = (r_pb[0], r_pb[1]) if bank == 1 else (r_pb[2], r_pb[3])
                    if f < ICH:
                        em.step("act", lambda s, f=f, pair=pair: s.copy(
                            out=xme[:, f - 1:f + 1, 2:NT + 2], in_=pair[:, :]),
                            reads=rpair + (r_xme,), writes=(r_xme,), tag=f"{sfx}ie{f}")
                        c0 = f - 1
                        pi = (f // 2) % 2

                        def cva(v, c0=c0, pi=pi):
                            inst = None
                            for j in range(2):
                                inst = v.tensor_scalar(out=cv1[pi][:, j, :], in0=xme[:, c0 + j, 0:NT],
                                                       scalar1=convw_f[:, 3 * (c0 + j):3 * (c0 + j) + 1],
                                                       scalar2=None, op0=OP.mult)
                            return inst
                        em.step("dve", cva, reads=(r_xme, r_smw), writes=(r_cv1[pi],), tag=f"{sfx}v1{c0}")

                        def cvb(v, c0=c0, pi=pi):
                            inst = None
                            for j in range(2):
                                inst = v.scalar_tensor_tensor(out=cv1[pi][:, j, :], in0=xme[:, c0 + j, 1:NT + 1],
                                                              scalar=convw_b[:, 3 * (c0 + j) + 1:3 * (c0 + j) + 2],
                                                              in1=cv1[pi][:, j, :], op0=OP.mult, op1=OP.add)
                            return inst
                        em.step("dve", cvb, reads=(r_xme, r_cv1[pi], r_smw), writes=(r_cv1[pi],), tag=f"{sfx}v2{c0}")

                        def cvc(v, c0=c0, pi=pi):
                            inst = None
                            for j in range(2):
                                inst = v.scalar_tensor_tensor(out=cv1[pi][:, j, :], in0=xme[:, c0 + j, 2:NT + 2],
                                                              scalar=convw_b[:, 3 * (c0 + j) + 2:3 * (c0 + j) + 3],
                                                              in1=cv1[pi][:, j, :], op0=OP.mult, op1=OP.add)
                            return inst
                        em.step("dve", cvc, reads=(r_xme, r_cv1[pi], r_smw), writes=(r_cv1[pi],), tag=f"{sfx}v3{c0}")
                        em.step("dve", lambda v, c0=c0, pi=pi: v.tensor_tensor(
                            out=cv1[pi][:], in0=cv1[pi][:],
                            in1=convb_sb[:, c0:c0 + 2].rearrange("p (c o) -> p c o", o=1).to_broadcast([128, 2, NT]),
                            op=OP.add),
                            reads=(r_cv1[pi], r_smw), writes=(r_cv1[pi],), tag=f"{sfx}v4{c0}")
                        em.step("act", lambda s, pi=pi: s.activation(out=svt[pi][:], in_=cv1[pi][:], func=AF.Sigmoid),
                                reads=(r_cv1[pi],), writes=(r_svt[pi],), tag=f"{sfx}v5{c0}")
                        em.step("dve", lambda v, c0=c0, pi=pi: v.tensor_tensor(
                            out=xms[:, c0 * NT:(c0 + 2) * NT],
                            in0=cv1[pi][:].rearrange("p c f -> p (c f)"),
                            in1=svt[pi][:].rearrange("p c f -> p (c f)"), op=OP.mult),
                            reads=(r_cv1[pi], r_svt[pi]), writes=(r_xms,), tag=f"{sfx}v6{c0}")
                    else:
                        c = f - ICH
                        em.step("act", lambda s, c=c, pair=pair: s.activation(
                            out=sgate[:, (c - 1) * NT:(c + 1) * NT], in_=pair[:, :], func=AF.Sigmoid),
                            reads=rpair, writes=(r_sg,), tag=f"{sfx}ig{c}")
        for (nm, wsb, bank) in [("dt", dtw_sb, 0), ("bm", bpw_sb, 1), ("cm", cpw_sb, 2)]:
            def mmg(t, wsb=wsb, bank=bank):
                inst = None
                for k in range(ICH):
                    inst = t.matmul(pb[bank][0:DSTATE, :], wsb[k][:], xms[:, k * NT:(k + 1) * NT],
                                    start=(k == 0), stop=(k == ICH - 1))
                return inst
            em.step("pe", mmg, reads=(r_xms, r_smw), writes=(r_pb[bank],), tag=f"{sfx}p{nm}")
        em.step("act", lambda s: s.activation(out=dt_sb[:], in_=pb[0][0:DSTATE, :], func=AF.Sigmoid,
                                              bias=dtb_sb[:], scale=1.0),
                reads=(r_pb[0], r_smw), writes=(r_dt,), tag=f"{sfx}dt")
        em.step("act", lambda s: s.activation(out=asc_sb[:], in_=pb[0][0:DSTATE, :], func=AF.Sigmoid,
                                              bias=ndtb_sb[:], scale=-1.0),
                reads=(r_pb[0], r_smw), writes=(r_asc,), tag=f"{sfx}as")
        em.step("act", lambda s: s.activation(out=bm_sb[:], in_=pb[1][0:DSTATE, :], func=AF.Identity,
                                              bias=bpb_sb[:], scale=1.0),
                reads=(r_pb[1], r_smw), writes=(r_bm,), tag=f"{sfx}bm")
        em.step("act", lambda s: s.activation(out=cm_sb[:], in_=pb[2][0:DSTATE, :], func=AF.Identity,
                                              bias=cpb_sb[:], scale=1.0),
                reads=(r_pb[2], r_smw), writes=(r_cm,), tag=f"{sfx}cm")
        if tl % 4 == 0:
            em.step("dve", lambda v: v.memset(state_sb[:], 0.0),
                    reads=(r_state,), writes=(r_state,), tag=f"{sfx}z")
        em.step("dve", lambda v: v.tensor_tensor(out=bsc_sb[:], in0=dt_sb[:], in1=bm_sb[:], op=OP.mult),
                reads=(r_dt, r_bm), writes=(r_bsc,), tag=f"{sfx}bs")
        em.step("dve", lambda v: v.tensor_tensor_scan(out=yst_sb[:], data0=asc_sb[:], data1=bsc_sb[:],
                                                      initial=state_sb[:], op0=OP.mult, op1=OP.add),
                reads=(r_asc, r_bsc, r_state), writes=(r_yst,), tag=f"{sfx}sc")
        em.step("dve", lambda v: v.tensor_copy(out=state_sb[:], in_=yst_sb[:, NT - 1:NT]),
                reads=(r_yst,), writes=(r_state,), tag=f"{sfx}su")
        em.step("dve", lambda v: v.tensor_tensor(out=yv_sb[:], in0=yst_sb[:], in1=cm_sb[:], op=OP.mult),
                reads=(r_yst, r_cm), writes=(r_yv,), tag=f"{sfx}y")
        em.step("dve", lambda v: v.tensor_copy(out=ybf_sb[:], in_=yv_sb[:]),
                reads=(r_yv,), writes=(r_ybf,), tag=f"{sfx}yc")
        em.step("act", lambda s: s.activation(out=ysq_sb[:], in_=yv_sb[:], func=AF.Square),
                reads=(r_yv,), writes=(r_ysq,), tag=f"{sfx}yq")
        em.step("pe", lambda t: t.matmul(pb[4][0:1, :], onescol_b[0:DSTATE, :], ybf_sb[:], start=True, stop=True),
                reads=(r_ybf, r_const), writes=(r_pb[4],), tag=f"{sfx}y1")
        em.step("pe", lambda t: t.matmul(pb[5][0:1, :], onescol_b[0:DSTATE, :], ysq_sb[:], start=True, stop=True),
                reads=(r_ysq, r_const), writes=(r_pb[5],), tag=f"{sfx}y2")
        em.step("dve", lambda v, rr=ROW["s1"]: v.tensor_scalar(out=rr, in0=pb[4][0:1, :], scalar1=1.0 / DSTATE,
                                                               scalar2=None, op0=OP.mult),
                reads=(r_pb[4],), writes=(r_row["s1"],), tag=f"{sfx}ym")
        em.step("dve", lambda v, rr=ROW["s2"]: v.tensor_scalar(out=rr, in0=pb[5][0:1, :], scalar1=1.0 / DSTATE,
                                                               scalar2=None, op0=OP.mult),
                reads=(r_pb[5],), writes=(r_row["s2"],), tag=f"{sfx}ys")
        em.step("dve", lambda v, ra=ROW["tmp"], rb=ROW["s1"]: v.tensor_tensor(out=ra, in0=rb, in1=rb, op=OP.mult),
                reads=(r_row["s1"],), writes=(r_row["tmp"],), tag=f"{sfx}ym2")
        em.step("dve", lambda v, ra=ROW["var"], rb=ROW["s2"], rc=ROW["tmp"]: v.tensor_tensor(out=ra, in0=rb, in1=rc, op=OP.subtract),
                reads=(r_row["s2"], r_row["tmp"]), writes=(r_row["var"],), tag=f"{sfx}yv")
        em.step("dve", lambda v, ra=ROW["tmp"], rb=ROW["var"]: v.tensor_scalar(out=ra, in0=rb, scalar1=1e-5,
                                                                               scalar2=None, op0=OP.add),
                reads=(r_row["var"],), writes=(r_row["tmp"],), tag=f"{sfx}ye")
        em.step("dve", lambda v, ra=ROW["tmp"]: v.reciprocal(out=ra, in_=ra),
                reads=(r_row["tmp"],), writes=(r_row["tmp"],), tag=f"{sfx}yr")
        em.step("act", lambda s, ra=ROW["rstd"], rb=ROW["tmp"]: s.activation(out=ra, in_=rb, func=AF.Sqrt),
                reads=(r_row["tmp"],), writes=(r_row["rstd"],), tag=f"{sfx}yd")
        bcast(ROW["s1"], r_row["s1"], "x1", f"{sfx}bm1", nrows=DSTATE)
        em.step("dve", lambda v: v.tensor_tensor(out=yv_sb[:], in0=yv_sb[:], in1=bct["x1"][0:DSTATE, :], op=OP.subtract),
                reads=(r_yv, r_bct["x1"]), writes=(r_yv,), tag=f"{sfx}n1")
        bcast(ROW["rstd"], r_row["rstd"], "x1", f"{sfx}bm2", nrows=DSTATE)
        em.step("dve", lambda v: v.tensor_tensor(out=ynrm_sb[:], in0=yv_sb[:], in1=bct["x1"][0:DSTATE, :], op=OP.mult),
                reads=(r_yv, r_bct["x1"]), writes=(r_ynrm,), tag=f"{sfx}n2")
        for c in range(ICH):
            bank = c % 4

            def mmg(t, c=c, bank=bank):
                t.matmul(pb[bank], s2iw_sb[:, c * 128:(c + 1) * 128], ynrm_sb[:], start=True, stop=False)
                return t.matmul(pb[bank], s2ib_row[0:1, c * 128:(c + 1) * 128], onesrow_b[:],
                                start=False, stop=True)
            em.step("pe", mmg, reads=(r_ynrm, r_smw, r_const), writes=(r_pb[bank],), tag=f"{sfx}s2i{c}")
            em.step("dve", lambda v, c=c, bank=bank: v.scalar_tensor_tensor(
                out=tmp1k[c % 2][:], in0=xms[:, c * NT:(c + 1) * NT], scalar=dpar_b[:, c:c + 1],
                in1=pb[bank], op0=OP.mult, op1=OP.add),
                reads=(r_xms, r_pb[bank], r_smw), writes=(r_tmp1k[c % 2],), tag=f"{sfx}t3{c}")
            em.step("dve", lambda v, c=c: v.tensor_tensor(out=xms[:, c * NT:(c + 1) * NT], in0=tmp1k[c % 2][:],
                                                          in1=sgate[:, c * NT:(c + 1) * NT], op=OP.mult),
                    reads=(r_tmp1k[c % 2], r_sg), writes=(r_xms,), tag=f"{sfx}t5{c}")
        for half in range(2):
            for kb in range(4):
                buf = kb % 2
                outw_load(tl, half, kb, buf)
                for kl in range(4):
                    k = kb * 4 + kl

                    def mmg(t, kl=kl, buf=buf, half=half, k=k):
                        inst = None
                        for dd in range(4):
                            d = half * 4 + dd
                            inst = t.matmul(pb[dd], outw_blk[buf][:, kl * D + d * 128:kl * D + (d + 1) * 128],
                                            xms[:, k * NT:(k + 1) * NT], start=(k == 0), stop=False)
                        return inst
                    em.step("pe", mmg, reads=(r_owb[buf], r_xms),
                            writes=tuple(r_pb[0:4]), accum=(k > 0), tag=f"{sfx}op{half}_{k}")

            def mmb(t, half=half):
                inst = None
                for dd in range(4):
                    d = half * 4 + dd
                    inst = t.matmul(pb[dd], outb_row[0:1, d * 128:(d + 1) * 128], onesrow_b[:],
                                    start=False, stop=True)
                return inst
            em.step("pe", mmb, reads=(r_smw, r_const), writes=tuple(r_pb[0:4]), accum=True, tag=f"{sfx}ob{half}")
            for pr in range(2):
                pair = pA if pr == 0 else pB
                rpair = (r_pb[0], r_pb[1]) if pr == 0 else (r_pb[2], r_pb[3])
                em.step("act", lambda s, half=half, pr=pr, pair=pair: s.copy(
                    out=cvtA[:, (half * 4 + pr * 2) * NT:(half * 4 + pr * 2 + 2) * NT], in_=pair[:, :]),
                    reads=rpair, writes=(r_cvtA,), tag=f"{sfx}oe{half}{pr}")
        em.step("dve", lambda v: v.tensor_tensor(out=v3(x0t[:]), in0=v3(x0t[:]), in1=bc3("ah"), op=OP.mult),
                reads=(r_x0t, r_bct["ah"]), writes=(r_x0t,), tag=f"{sfx}a1")
        em.step("dve", lambda v: v.tensor_tensor(out=x0t[:], in0=x0t[:], in1=cvtA[:], op=OP.add),
                reads=(r_x0t, r_cvtA), writes=(r_x0t,), tag=f"{sfx}a2")
        em.step("dve", lambda v: v.tensor_tensor(out=v3(x0t[:]), in0=v3(x0t[:]), in1=bc3("bh"), op=OP.subtract),
                reads=(r_x0t, r_bct["bh"]), writes=(r_x0t,), tag=f"{sfx}a3")
        em.dma("sp", lambda e, tl=tl: e.dma_start(out=accbuf[tl], in_=x0t[:]),
            reads=(r_x0t,), writes=(r_accb[tl],), tag=f"{sfx}sa")
        stats_pass(x0t[:], r_x0t, f"{sfx}r", rms_only=True)
        em.step("dve", lambda v, ra=ROW["tmp"]: v.tensor_scalar(out=ra, in0=pb[5][0:1, :], scalar1=1.0 / D,
                                                                scalar2=1e-6, op0=OP.mult, op1=OP.add),
                reads=(r_pb[5],), writes=(r_row["tmp"],), tag=f"{sfx}gm")
        em.step("dve", lambda v, ra=ROW["tmp"]: v.reciprocal(out=ra, in_=ra),
                reads=(r_row["tmp"],), writes=(r_row["tmp"],), tag=f"{sfx}gr")
        em.step("act", lambda s, ra=ROW["tmp"]: s.activation(out=ra, in_=ra, func=AF.Sqrt),
                reads=(r_row["tmp"],), writes=(r_row["tmp"],), tag=f"{sfx}g2")
        bcast(ROW["tmp"], r_row["tmp"], "x1", f"{sfx}bg")
        em.step("dve", lambda v: v.tensor_tensor(out=v3(x0t[:]), in0=v3(x0t[:]), in1=bc3("x1"), op=OP.mult),
                reads=(r_x0t, r_bct["x1"]), writes=(r_x0t,), tag=f"{sfx}ga")
        em.step("dve", lambda v: v.tensor_tensor(out=v3(x0t[:]), in0=v3(x0t[:]),
                                                 in1=cols3(n2w_sb[:], DCH).to_broadcast([128, DCH, NT]), op=OP.mult),
                reads=(r_x0t, r_smw), writes=(r_x0t,), tag=f"{sfx}gb")
        em.step("dve", lambda v: v.tensor_copy(out=castb[:], in_=x0t[:]),
                reads=(r_x0t,), writes=(r_cast,), tag=f"{sfx}gc")
        em.dma("sp", lambda e, tl=tl: e.dma_start(out=gbuf[tl], in_=castb[:]),
            reads=(r_cast,), writes=(r_gbuf[tl],), tag=f"{sfx}sg")
        for sub in range(4):
            def mmg(t, sub=sub):
                inst = None
                for k in range(DCH):
                    inst = t.matmul(pb[4][:, sub * 4:(sub + 1) * 4],
                                    x0t[:, k * NT + sub * 128:k * NT + (sub + 1) * 128], gatewf_sb[k][:],
                                    start=(k == 0), stop=(k == DCH - 1))
                return inst
            em.step("pe", mmg, reads=(r_x0t, r_smw), writes=(r_pb[4],),
                    accum=(sub > 0), tag=f"{sfx}lg{sub}")

        def lgev(v):
            inst = None
            for sub in range(4):
                inst = v.tensor_tensor(out=lgt_sb[:, sub * 4:(sub + 1) * 4],
                                       in0=pb[4][:, sub * 4:(sub + 1) * 4], in1=gbrep_sb[:], op=OP.add)
            return inst
        em.step("dve", lgev, reads=(r_pb[4], r_smw), writes=(r_lgt,), tag=f"{sfx}le")
        l0 = lgt_sb[:, 0::4]; l1 = lgt_sb[:, 1::4]; l2 = lgt_sb[:, 2::4]; l3 = lgt_sb[:, 3::4]
        em.step("dve", lambda v: v.tensor_tensor(out=tA_sb[:], in0=l0, in1=l1, op=OP.max),
                reads=(r_lgt,), writes=(r_top2,), tag=f"{sfx}p1")
        em.step("dve", lambda v: v.tensor_tensor(out=tB_sb[:], in0=l2, in1=l3, op=OP.max),
                reads=(r_lgt, r_top2), writes=(r_top2,), tag=f"{sfx}p2")
        em.step("dve", lambda v: v.tensor_tensor(out=m1_sb[:], in0=tA_sb[:], in1=tB_sb[:], op=OP.max),
                reads=(r_top2,), writes=(r_top2,), tag=f"{sfx}p3")
        em.step("dve", lambda v: v.tensor_tensor(out=m2_sb[:], in0=tA_sb[:], in1=tB_sb[:], op=OP.min),
                reads=(r_top2,), writes=(r_top2,), tag=f"{sfx}p4")
        em.step("dve", lambda v: v.tensor_tensor(out=tA_sb[:], in0=l0, in1=l1, op=OP.min),
                reads=(r_lgt, r_top2), writes=(r_top2,), tag=f"{sfx}p5")
        em.step("dve", lambda v: v.tensor_tensor(out=tB_sb[:], in0=l2, in1=l3, op=OP.min),
                reads=(r_lgt, r_top2), writes=(r_top2,), tag=f"{sfx}p6")
        em.step("dve", lambda v: v.tensor_tensor(out=tA_sb[:], in0=tA_sb[:], in1=tB_sb[:], op=OP.max),
                reads=(r_top2,), writes=(r_top2,), tag=f"{sfx}p7")
        em.step("dve", lambda v: v.tensor_tensor(out=m2_sb[:], in0=m2_sb[:], in1=tA_sb[:], op=OP.max),
                reads=(r_top2,), writes=(r_top2,), tag=f"{sfx}p8")
        em.step("dve", lambda v: v.tensor_tensor(out=tA_sb[:], in0=m1_sb[:], in1=m2_sb[:], op=OP.add),
                reads=(r_top2,), writes=(r_top2,), tag=f"{sfx}p9")

        def zst(v):
            inst = None
            for sub in range(4):
                inst = v.scalar_tensor_tensor(out=es_sb[:, sub * 4:(sub + 1) * 4],
                                              in0=lgt_sb[:, sub * 4:(sub + 1) * 4], scalar=2.0,
                                              in1=tA_sb[:, sub:sub + 1].to_broadcast([128, 4]),
                                              op0=OP.mult, op1=OP.subtract)
            return inst
        em.step("dve", zst, reads=(r_lgt, r_top2), writes=(r_top2,), tag=f"{sfx}pz")
        em.step("act", lambda s: s.activation(out=es_sb[:], in_=es_sb[:], func=AF.Sigmoid),
                reads=(r_top2,), writes=(r_top2,), tag=f"{sfx}px")

        def maskst(v):
            inst = None
            for sub in range(4):
                inst = v.tensor_scalar(out=msk_sb[:, sub * 4:(sub + 1) * 4],
                                       in0=lgt_sb[:, sub * 4:(sub + 1) * 4],
                                       scalar1=m2_sb[:, sub:sub + 1], scalar2=None, op0=OP.is_ge)
            return inst
        em.step("dve", maskst, reads=(r_lgt, r_top2), writes=(r_top2,), tag=f"{sfx}pm")

        def wfst(v):
            return v.tensor_tensor(out=wfull_sb[:], in0=es_sb[:], in1=msk_sb[:], op=OP.mult)
        em.step("dve", wfst, reads=(r_top2,), writes=(r_wfull,), tag=f"{sfx}pf")
        for e in range(NEXP):
            def wtr(t, e=e):
                inst = None
                for sub in range(4):
                    col = sub * 4 + e
                    inst = t.matmul(pb[6][0:1, sub * 128:(sub + 1) * 128], wfull_sb[:, col:col + 1],
                                    identf[:], start=True, stop=True)
                return inst
            em.step("pe", wtr, reads=(r_wfull, r_const), writes=(r_pb[6],), tag=f"{sfx}wt{e}")
            em.step("act", lambda s: s.copy(out=wrow_sb, in_=pb[6][0:1, :]),
                    reads=(r_pb[6],), writes=(r_wrowsb,), tag=f"{sfx}we{e}")
            em.dma("sp", lambda g, e=e, tl=tl: g.dma_start(
                out=wrows[e:e + 1, tl * NT:(tl + 1) * NT], in_=wrow_sb),
                reads=(r_wrowsb,), writes=(r_wrows,), tag=f"{sfx}ws{e}")

    s01.close()
    em.barrier("b1")

    s2 = ExitStack()
    ew1_sb = [sb(s2, f"ew1_{k}", [128, HID], BF16) for k in range(DCH)]
    r_ew1 = [R(f"ew1_{k}") for k in range(DCH)]
    ew2_sb = [sb(s2, f"ew2_{k}", [128, 4 * D], BF16) for k in range(HCH // 4)]
    r_ew2 = [R(f"ew2_{k}") for k in range(HCH // 4)]
    g2_sb = [sb(s2, f"g2_{i}", [128, DCH * NT], BF16) for i in range(2)]
    r_g2 = [R(f"g2_{i}") for i in range(2)]
    hid_sb = [sb(s2, f"hid_{f}", [128, NT], BF16) for f in range(HCH)]
    r_hid = [R(f"hid_{f}") for f in range(HCH)]
    bcwe_sb = [sb(s2, "bcwe0", [128, NT], F32)] * 2
    r_bcwe = [R("bcwe0")] * 2
    ctr_sb = [sb(s2, f"ctr{i}", [128, NT], F32) for i in range(2)]
    r_ctr = [R(f"ctr{i}") for i in range(2)]

    for e in range(NEXP):
        for k in range(DCH):
            em.dma("pool", lambda g, e=e, k=k: g.dma_start(out=ew1_sb[k][:], in_=ew1[e, k * 128:(k + 1) * 128, :]),
                   writes=(r_ew1[k],), tag=f"e{e}w1{k}")
        for kb in range(HCH // 4):
            em.dma("pool", lambda g, e=e, kb=kb: g.dma_start(
                out=ew2_sb[kb][:], in_=ew2[e, kb]),
                writes=(r_ew2[kb],), tag=f"e{e}w2{kb}")
        for tl in range(NTILES):
            sfx = f"e{e}t{tl}"
            gb = tl % 2
            em.dma("sp", lambda g, tl=tl, gb=gb: g.dma_start(out=g2_sb[gb][:], in_=gbuf[tl]),
                reads=(r_gbuf[tl],), writes=(r_g2[gb],), tag=f"{sfx}lg")
            em.dma("act", lambda g, e=e, tl=tl, gb=gb: g.dma_start(
                out=bcwe_sb[gb][:], in_=wrows[e:e + 1, tl * NT:(tl + 1) * NT].to_broadcast([128, NT])),
                reads=(r_wrows,), writes=(r_bcwe[gb],), tag=f"{sfx}bw")
            for f in range(HCH):
                bank = f % 4

                def mmg(t, f=f, bank=bank, gb=gb):
                    inst = None
                    for k in range(DCH):
                        inst = t.matmul(pb[bank][:, :], ew1_sb[k][:, f * 128:(f + 1) * 128],
                                        g2_sb[gb][:, k * NT:(k + 1) * NT],
                                        start=(k == 0), stop=(k == DCH - 1))
                    return inst
                em.step("pe", mmg, reads=(r_g2[gb],) + tuple(r_ew1), writes=(r_pb[bank],), tag=f"{sfx}h{f}")
                em.step("act", lambda s, e=e, f=f, bank=bank: s.activation(
                    out=hid_sb[f][:], in_=pb[bank][:, :], func=AF.Gelu,
                    bias=eb1_sb[:, e * 32 + f:e * 32 + f + 1], scale=1.0),
                    reads=(r_pb[bank], r_smw), writes=(r_hid[f],), tag=f"{sfx}g{f}")
            for d in range(DCH):
                bank = 4 + (d % 2)

                def mmg2(t, d=d, bank=bank):
                    inst = None
                    for k in range(HCH):
                        inst = t.matmul(pb[bank][:, :], ew2_sb[k // 4][:, (k % 4) * D + d * 128:(k % 4) * D + (d + 1) * 128],
                                        hid_sb[k][:], start=(k == 0), stop=(k == HCH - 1))
                    return inst
                em.step("pe", mmg2, reads=tuple(r_hid) + tuple(r_ew2), writes=(r_pb[bank],), tag=f"{sfx}o{d}")
                em.step("dve", lambda v, e=e, d=d, bank=bank, gb=gb: v.scalar_tensor_tensor(
                    out=ctr_sb[d % 2][:], in0=pb[bank][:, :], scalar=eb2_sb[:, e * 8 + d:e * 8 + d + 1],
                    in1=bcwe_sb[gb][:], op0=OP.add, op1=OP.mult),
                    reads=(r_pb[bank], r_smw, r_bcwe[gb]), writes=(r_ctr[d % 2],), tag=f"{sfx}v{d}")
                em.dma("pool", lambda g, d=d, tl=tl: g.dma_start(
                    out=accbuf[tl, :, d * NT:(d + 1) * NT], in_=ctr_sb[d % 2][:],
                    accum_op=OP.add),
                    reads=(r_ctr[d % 2],), writes=(r_accb[tl],), tag=f"{sfx}x{d}")
            if 1 <= tl <= 6:
                emit_wconv(1, f"s2{e}_{tl}")

    s2.close()
    em.barrier("b2")

    sr = ExitStack()
    scores_sb = sb(sr, "scores_sb", [128, T], F32)
    r_scores = R("scores")
    mA_sb = sb(sr, "mA", [128, T], F32)
    mB_sb = sb(sr, "mB", [128, T], F32)
    mC_sb = sb(sr, "mC", [128, T], F32)
    r_route = R("route")
    lo_sb = sb(sr, "lo", [48, 1], I32)
    hi_sb = sb(sr, "hi", [48, 1], I32)
    mid_sb = sb(sr, "mid", [48, 1], I32)
    cnt_sb = sb(sr, "cnt", [48, 1], F32)
    cond_sb = sb(sr, "cond", [48, 1], I32)
    rrem_sb = sb(sr, "rrem", [48, 1], F32)

    s3 = ExitStack()
    xb_q = [sb(s3, f"xb3_{q}", [128, DCH * NT], F32) for q in range(2)]
    x0h_q = [sb(s3, f"x0h3_{q}", [128, DCH * NT], F32) for q in range(2)]
    h2b_q = [sb(s3, f"h2b3_{q}", [128, DCH * NT], BF16) for q in range(2)]
    expt_q = [sb(s3, f"expt3_{q}", [128, NT], F32) for q in range(2)]
    rows_q = [sb(s3, f"rows3_{q}", [1, 6 * 512], F32) for q in range(2)]
    bct_q = [{nm: sb(s3, f"bc3{q}_{nm}", [128, 512], F32) for nm in ("a3", "b3", "a4", "b4")}
             for q in range(2)]
    cast3 = sb(s3, "cast3", [128, DCH * NT], BF16)
    tm_sb = sb(s3, "tm3", [128, D], F32)
    tmb_sb = sb(s3, "tmb3", [128, D], BF16)
    sct_sb = sb(s3, "sct", [128, FE], F32)
    r_xb3 = [R(f"xb3_{q}") for q in range(2)]
    r_x0h = [R(f"x0h3_{q}") for q in range(2)]
    r_h2b3 = [R(f"h2b3_{q}") for q in range(2)]
    r_expt3 = [R(f"expt3_{q}") for q in range(2)]
    r_c3 = R("cast3")
    r_tm = R("tm3"); r_tmb = R("tmb3"); r_sct = R("sct")
    ROW3 = []
    r_row3 = []
    bctR3 = []
    for q in range(2):
        RW = {}
        rr = {}
        for i, nm in enumerate(["s1", "s2", "var", "rstd", "bh", "tmp"]):
            RW[nm] = rows_q[q][:, i * 512:(i + 1) * 512]
            rr[nm] = R(f"r3{q}_{nm}")
        ROW3.append(RW)
        r_row3.append(rr)
        bctR3.append({nm: R(f"b3{q}_{nm}") for nm in ("a3", "b3", "a4", "b4")})
    QB = [(4, 5), (2, 3)]

    def stats3h(q, src_ap, src_res, tag, part):
        ba, bb = QB[q]
        if part == 0:
            em.step("dve", lambda v: v.tensor_copy(out=cast3[:], in_=src_ap),
                    reads=(src_res,), writes=(r_c3,), tag=f"{tag}c")
            for d in range(DCH):
                em.step("pe", lambda t, d=d, ba=ba: t.matmul(pb[ba][0:1, :], onescol_b[:],
                                                             cast3[:, d * NT:(d + 1) * NT],
                                                             start=(d == 0), stop=(d == DCH - 1)),
                        reads=(r_c3, r_const), writes=(r_pb[ba],), accum=(d > 0), tag=f"{tag}1{d}")
        else:
            em.step("act", lambda s: s.activation(out=cast3[:], in_=src_ap, func=AF.Square),
                    reads=(src_res,), writes=(r_c3,), tag=f"{tag}q")
            for d in range(DCH):
                em.step("pe", lambda t, d=d, bb=bb: t.matmul(pb[bb][0:1, :], onescol_b[:],
                                                             cast3[:, d * NT:(d + 1) * NT],
                                                             start=(d == 0), stop=(d == DCH - 1)),
                        reads=(r_c3, r_const), writes=(r_pb[bb],), accum=(d > 0), tag=f"{tag}2{d}")

    def ln3(q, tag):
        ba, bb = QB[q]
        RW = ROW3[q]
        rr = r_row3[q]
        em.step("dve", lambda v: v.tensor_scalar(out=RW["s1"], in0=pb[ba][0:1, :], scalar1=1.0 / D,
                                                 scalar2=None, op0=OP.mult),
                reads=(r_pb[ba],), writes=(rr["s1"],), tag=f"{tag}_mu")
        em.step("dve", lambda v: v.tensor_scalar(out=RW["s2"], in0=pb[bb][0:1, :], scalar1=1.0 / D,
                                                 scalar2=None, op0=OP.mult),
                reads=(r_pb[bb],), writes=(rr["s2"],), tag=f"{tag}_ms")
        em.step("dve", lambda v: v.tensor_tensor(out=RW["tmp"], in0=RW["s1"], in1=RW["s1"], op=OP.mult),
                reads=(rr["s1"],), writes=(rr["tmp"],), tag=f"{tag}_mu2")
        em.step("dve", lambda v: v.tensor_tensor(out=RW["var"], in0=RW["s2"], in1=RW["tmp"], op=OP.subtract),
                reads=(rr["s2"], rr["tmp"]), writes=(rr["var"],), tag=f"{tag}_var")
        em.step("dve", lambda v: v.tensor_scalar(out=RW["tmp"], in0=RW["var"], scalar1=1e-5,
                                                 scalar2=None, op0=OP.add),
                reads=(rr["var"],), writes=(rr["tmp"],), tag=f"{tag}_ve")
        em.step("dve", lambda v: v.reciprocal(out=RW["tmp"], in_=RW["tmp"]),
                reads=(rr["tmp"],), writes=(rr["tmp"],), tag=f"{tag}_rv")
        em.step("act", lambda s: s.activation(out=RW["rstd"], in_=RW["tmp"], func=AF.Sqrt),
                reads=(rr["tmp"],), writes=(rr["rstd"],), tag=f"{tag}_sd")
        em.step("dve", lambda v: v.tensor_tensor(out=RW["bh"], in0=RW["s1"], in1=RW["rstd"], op=OP.mult),
                reads=(rr["s1"], rr["rstd"]), writes=(rr["bh"],), tag=f"{tag}_bh")

    def bcast3(q, slot, dstnm, tag):
        RW = ROW3[q]
        rr = r_row3[q]
        em.step("pe", lambda t: t.matmul(pb[6][0:128, :], onesrow_f[0:1, 0:128], RW[slot],
                                         start=True, stop=True),
                reads=(rr[slot], r_const), writes=(r_pb[6],), tag=f"{tag}m")
        em.step("act", lambda s: s.copy(out=bct_q[q][dstnm][:], in_=pb[6][0:128, :]),
                reads=(r_pb[6],), writes=(bctR3[q][dstnm],), tag=f"{tag}e")

    def s3_tile(tl):
        q = tl % 2
        seq = tl // 4
        base = 32 * seq
        ba, bb = QB[q]
        sfx = f"s3t{tl}"
        xb = xb_q[q]
        x0h = x0h_q[q]
        h2b = h2b_q[q]
        expt = expt_q[q]
        rxb = r_xb3[q]
        rx0 = r_x0h[q]
        rh2 = r_h2b3[q]
        rexp = r_expt3[q]
        RW = ROW3[q]
        rr = r_row3[q]
        v3l = lambda ap: ap.rearrange("p (c f) -> p c f", f=NT)
        bcv = lambda nm: bct_q[q][nm][:].rearrange("p (o f) -> p o f", o=1).to_broadcast([128, DCH, NT])
        em.dma("sp", lambda e, tl=tl, xb=xb: e.dma_start(out=xb[:], in_=accbuf[tl]),
               reads=(r_accb[tl],), writes=(rxb,), tag=f"{sfx}lb")
        em.dma("act", lambda e, tl=tl, x0h=x0h: e.dma_start(out=x0h[:], in_=x0fm[tl]),
               reads=(r_x0fm[tl],), writes=(rx0,), tag=f"{sfx}l0")
        yield 0
        stats3h(q, xb[:], rxb, f"{sfx}s", 0)
        yield 1
        stats3h(q, xb[:], rxb, f"{sfx}s", 1)
        yield 2
        ln3(q, f"{sfx}l3")
        yield 3
        bcast3(q, "rstd", "a3", f"{sfx}b3")
        bcast3(q, "bh", "b3", f"{sfx}c3")
        yield 4
        em.step("dve", lambda v: v.tensor_tensor(out=v3l(xb[:]), in0=v3l(xb[:]), in1=bcv("a3"), op=OP.mult),
                reads=(rxb, bctR3[q]["a3"]), writes=(rxb,), tag=f"{sfx}xa")
        em.step("dve", lambda v: v.tensor_tensor(out=v3l(xb[:]), in0=v3l(xb[:]), in1=bcv("b3"), op=OP.subtract),
                reads=(rxb, bctR3[q]["b3"]), writes=(rxb,), tag=f"{sfx}xb")
        em.step("dve", lambda v: v.tensor_tensor(out=xb[:], in0=xb[:], in1=x0h[:], op=OP.add),
                reads=(rxb, rx0), writes=(rxb,), tag=f"{sfx}xc")
        yield 5
        stats3h(q, xb[:], rxb, f"{sfx}u", 0)
        yield 6
        stats3h(q, xb[:], rxb, f"{sfx}u", 1)
        yield 7
        ln3(q, f"{sfx}l4")
        yield 8
        bcast3(q, "rstd", "a4", f"{sfx}b4")
        bcast3(q, "bh", "b4", f"{sfx}c4")
        yield 9
        em.step("dve", lambda v: v.tensor_tensor(out=v3l(x0h[:]), in0=v3l(xb[:]), in1=bcv("a4"), op=OP.mult),
                reads=(rxb, bctR3[q]["a4"]), writes=(rx0,), tag=f"{sfx}ha")
        em.step("dve", lambda v: v.tensor_tensor(out=v3l(x0h[:]), in0=v3l(x0h[:]), in1=bcv("b4"), op=OP.subtract),
                reads=(rx0, bctR3[q]["b4"]), writes=(rx0,), tag=f"{sfx}hb")
        em.step("dve", lambda v: v.tensor_copy(out=h2b[:], in_=x0h[:]),
                reads=(rx0,), writes=(rh2,), tag=f"{sfx}hc")
        em.dma("sp", lambda e, tl=tl, h2b=h2b: e.dma_start(out=h2buf[tl], in_=h2b[:]),
               reads=(rh2,), writes=(r_h2buf[tl],), tag=f"{sfx}hs")
        yield 10
        for sub in range(4):
            for bank in range(2):
                def trg(t, sub=sub, bank=bank, xb=xb):
                    inst = None
                    for reg in range(4):
                        d = reg * 2 + bank
                        inst = t.transpose(pb[bank][:, reg * 128:(reg + 1) * 128],
                                           xb[:, d * NT + sub * 128:d * NT + (sub + 1) * 128], identf[:])
                    return inst
                em.step("pe", trg, reads=(rxb, r_const), writes=(r_pb[bank],), tag=f"{sfx}tx{sub}_{bank}")

                def ev(s, b=bank):
                    inst = None
                    for reg in range(4):
                        d = reg * 2 + b
                        inst = s.copy(out=tm_sb[:, d * 128:(d + 1) * 128], in_=pb[b][:, reg * 128:(reg + 1) * 128])
                    return inst
                em.step("act", ev, reads=(r_pb[bank],), writes=(r_tm,), tag=f"{sfx}te{sub}_{bank}")
            em.dma("sp", lambda e, tl=tl, sub=sub: e.dma_start(
                out=x1tm[tl * NT + sub * 128:tl * NT + (sub + 1) * 128, :], in_=tm_sb[:]),
                reads=(r_tm,), writes=(r_x1tm[tl],), tag=f"{sfx}ts{sub}")
        yield 11
        for sub in range(4):
            def trg2(t, sub=sub, h2b=h2b):
                inst = None
                for d in range(DCH):
                    inst = t.transpose(pbb[:, d * 128:(d + 1) * 128],
                                       h2b[:, d * NT + sub * 128:d * NT + (sub + 1) * 128], identb[:])
                return inst
            em.step("pe", trg2, reads=(rh2, r_const), writes=(r_pbb,), tag=f"{sfx}ty{sub}")
            em.step("act", lambda s: s.copy(out=tmb_sb[:], in_=pbb),
                    reads=(r_pbb,), writes=(r_tmb,), tag=f"{sfx}tz{sub}")
            em.dma("sp", lambda e, tl=tl, sub=sub, seq=seq: e.dma_start(
                out=h2tm[seq][((tl % 4) * 4 + sub) * 128:((tl % 4) * 4 + sub + 1) * 128, :], in_=tmb_sb[:]),
                reads=(r_tmb,), writes=(r_h2tm[tl],), tag=f"{sfx}tw{sub}")
        yield 12
        def mmg3(t, base=base, x0h=x0h, ba=ba):
            inst = None
            for k in range(DCH):
                inst = t.matmul(pb[ba][base:base + FE, :], fgwf_sb[k][:], x0h[:, k * NT:(k + 1) * NT],
                                start=(k == 0), stop=(k == DCH - 1))
            return inst
        em.step("pe", mmg3, reads=(rx0, r_smw), writes=(r_pb[ba],), tag=f"{sfx}fl")
        em.step("act", lambda s, base=base, ba=ba, expt=expt: s.activation(
            out=expt[base:base + FE, :], in_=pb[ba][base:base + FE, :], func=AF.Exp),
                reads=(r_pb[ba],), writes=(rexp,), tag=f"{sfx}fx")
        em.step("pe", lambda t, seq=seq, bb=bb, expt=expt: t.matmul(
            pb[bb][0:1, :], sel_seq[seq][0:48, :], expt[0:48, :], start=True, stop=True),
                reads=(rexp, r_const), writes=(r_pb[bb],), tag=f"{sfx}fs")
        em.step("dve", lambda v, bb=bb: v.tensor_copy(out=RW["tmp"], in_=pb[bb][0:1, :]),
                reads=(r_pb[bb],), writes=(rr["tmp"],), tag=f"{sfx}fe")
        em.step("dve", lambda v: v.reciprocal(out=RW["tmp"], in_=RW["tmp"]),
                reads=(rr["tmp"],), writes=(rr["tmp"],), tag=f"{sfx}fr")
        em.step("pe", lambda t, base=base, bb=bb: t.matmul(pb[bb][base:base + FE, :], onesrow_f[0:1, 0:FE],
                                                           RW["tmp"], start=True, stop=True),
                reads=(rr["tmp"], r_const), writes=(r_pb[bb],), tag=f"{sfx}fb")
        em.step("dve", lambda v, tl=tl, base=base, bb=bb, expt=expt: v.tensor_tensor(
            out=scores_sb[base:base + FE, (tl % 4) * NT:((tl % 4) + 1) * NT],
            in0=expt[base:base + FE, :], in1=pb[bb][base:base + FE, :], op=OP.mult),
            reads=(rexp, r_pb[bb]), writes=(r_scores,), tag=f"{sfx}fc")
        emit_wconv(4, f"s3{tl}")
        yield 13

    em.step("dve", lambda v: v.memset(scores_sb[:], 0.0), writes=(r_scores,), tag="s3z")
    s3_active = deque()
    s3_next = [0]

    def s3_pump():
        if s3_next[0] < NTILES:
            s3_active.append(s3_tile(s3_next[0]))
            s3_next[0] += 1

    S3_SPLIT = 10
    s3_pump()
    s3_pump()
    while s3_active:
        g = s3_active.popleft()
        try:
            st = next(g)
            if st is not None and st >= S3_SPLIT:
                for _ in g:
                    pass
                s3_pump()
            else:
                s3_active.append(g)
        except StopIteration:
            s3_pump()

    for k in range(16):
        em.step("pe", lambda t, k=k: t.transpose(pb[6][:, 0:128], scores_sb[:, k * 128:(k + 1) * 128], identf[:]),
                reads=(r_scores, r_const), writes=(r_pb[6],), tag=f"st{k}")
        for seq in range(B_CORE):
            em.step("act", lambda s, seq=seq: s.copy(out=sct_sb[:], in_=pb[6][:, 32 * seq:32 * seq + FE]),
                    reads=(r_pb[6],), writes=(r_sct,), tag=f"st{k}e{seq}")
            em.dma("sp", lambda e, k=k, seq=seq: e.dma_start(
                out=scoreT[seq][k * 128:(k + 1) * 128, :], in_=sct_sb[:]),
                reads=(r_sct,), writes=(r_scoreT[seq],), tag=f"st{k}s{seq}")

    s3.close()
    em.barrier("b3")

    s4a = ExitStack()
    wsg_sb = [sb(s4a, f"wsg{k}", [128, FSH], BF16) for k in range(DCH)]
    wsu_sb = [sb(s4a, f"wsu{k}", [128, FSH], BF16) for k in range(DCH)]
    wsd_sb = [sb(s4a, f"wsd{k}", [128, 4 * D], BF16) for k in range(FSH // 512)]
    r_wsg = R("wsg"); r_wsu = R("wsu"); r_wsd = R("wsd")
    for k in range(DCH):
        em.dma("pool", lambda g, k=k: g.dma_start(out=wsg_sb[k][:], in_=wsgT[k * 128:(k + 1) * 128, :]),
               writes=(r_wsg,), tag=f"4ag{k}")
        em.dma("pool", lambda g, k=k: g.dma_start(out=wsu_sb[k][:], in_=wsuT[k * 128:(k + 1) * 128, :]),
               writes=(r_wsu,), tag=f"4au{k}")
    for kb in range(FSH // 512):
        em.dma("pool", lambda g, kb=kb: g.dma_start(out=wsd_sb[kb][:], in_=wsdT[kb]),
               writes=(r_wsd,), tag=f"4ad{kb}")
    h24 = sb(s4a, "h24", [128, DCH * NT], BF16)
    r_h24 = R("h24")
    hsh_sb = [sb(s4a, f"hsh{f}", [128, NT], BF16) for f in range(FSH // 128)]
    r_hsh = [R(f"hsh{f}") for f in range(FSH // 128)]
    ush_sb = sb(s4a, "ush", [128, NT], BF16)
    r_ush = R("ush")
    sho_sb = [sb(s4a, f"sho{d}", [128, NT], F32) for d in range(DCH)]
    r_sho = [R(f"sho{d}") for d in range(DCH)]
    tm4_sb = sb(s4a, "tm4", [128, D], F32)
    r_tm4 = R("tm4")

    for tl in range(NTILES):
        seq = tl // 4
        sfx = f"s4s{tl}"
        em.dma("sp", lambda e, tl=tl: e.dma_start(out=h24[:], in_=h2buf[tl]),
               reads=(r_h2buf[tl],), writes=(r_h24,), tag=f"{sfx}l")
        for f in range(FSH // 128):
            b0 = f % 2
            def mmg(t, f=f, b0=b0):
                inst = None
                for k in range(DCH):
                    inst = t.matmul(pb[b0][:, :], wsg_sb[k][:, f * 128:(f + 1) * 128],
                                    h24[:, k * NT:(k + 1) * NT],
                                    start=(k == 0), stop=(k == DCH - 1))
                return inst
            em.step("pe", mmg, reads=(r_h24, r_wsg), writes=(r_pb[b0],), tag=f"{sfx}g{f}")

            def mmg2(t, f=f, b0=b0):
                inst = None
                for k in range(DCH):
                    inst = t.matmul(pb[2 + b0][:, :], wsu_sb[k][:, f * 128:(f + 1) * 128],
                                    h24[:, k * NT:(k + 1) * NT],
                                    start=(k == 0), stop=(k == DCH - 1))
                return inst
            em.step("pe", mmg2, reads=(r_h24, r_wsu), writes=(r_pb[2 + b0],), tag=f"{sfx}u{f}")
            em.step("act", lambda s, f=f, b0=b0: s.activation(out=hsh_sb[f][:], in_=pb[b0][:, :], func=AF.Gelu),
                    reads=(r_pb[b0],), writes=(r_hsh[f],), tag=f"{sfx}e{f}")
            em.step("dve", lambda v, f=f, b0=b0: v.tensor_copy(out=ush_sb[:], in_=pb[2 + b0][:, :]),
                    reads=(r_pb[2 + b0],), writes=(r_ush,), tag=f"{sfx}v{f}")
            em.step("dve", lambda v, f=f: v.tensor_tensor(out=hsh_sb[f][:], in0=hsh_sb[f][:], in1=ush_sb[:], op=OP.mult),
                    reads=(r_hsh[f], r_ush), writes=(r_hsh[f],), tag=f"{sfx}m{f}")
        for d in range(DCH):
            bank = 4 + (d % 2)

            def mmg3(t, d=d, bank=bank):
                inst = None
                for k in range(FSH // 128):
                    inst = t.matmul(pb[bank][:, :], wsd_sb[k // 4][:, (k % 4) * D + d * 128:(k % 4) * D + (d + 1) * 128],
                                    hsh_sb[k][:], start=(k == 0), stop=(k == FSH // 128 - 1))
                return inst
            em.step("pe", mmg3, reads=tuple(r_hsh) + (r_wsd,), writes=(r_pb[bank],), tag=f"{sfx}d{d}")
            em.step("act", lambda s, d=d, bank=bank: s.copy(out=sho_sb[d][:], in_=pb[bank][:, :]),
                    reads=(r_pb[bank],), writes=(r_sho[d],), tag=f"{sfx}c{d}")
        for sub in range(4):
            for bank in range(2):
                def trg(t, sub=sub, bank=bank):
                    inst = None
                    for reg in range(4):
                        d = reg * 2 + bank
                        inst = t.transpose(pb[bank][:, reg * 128:(reg + 1) * 128],
                                           sho_sb[d][:, sub * 128:(sub + 1) * 128], identf[:])
                    return inst
                em.step("pe", trg, reads=tuple(r_sho) + (r_const,), writes=(r_pb[bank],), tag=f"{sfx}t{sub}_{bank}")

                def ev(s, b=bank):
                    inst = None
                    for reg in range(4):
                        d = reg * 2 + b
                        inst = s.copy(out=tm4_sb[:, d * 128:(d + 1) * 128], in_=pb[b][:, reg * 128:(reg + 1) * 128])
                    return inst
                em.step("act", ev, reads=(r_pb[bank],), writes=(r_tm4,), tag=f"{sfx}w{sub}_{bank}")
            em.dma("sp", lambda e, tl=tl, sub=sub, seq=seq: e.dma_start(
                out=acc2[seq][(tl % 4) * NT + sub * 128:(tl % 4) * NT + (sub + 1) * 128, :], in_=tm4_sb[:]),
                reads=(r_tm4,), writes=(r_acc2[seq],), tag=f"{sfx}s{sub}")
        emit_wconv(4, f"s4a{tl}")

    em.step("dve", lambda v: v.memset(mB_sb[:], 0.0), writes=(r_route,), tag="r_z0")
    em.step("dve", lambda v: v.memset(lo_sb[:], 0), reads=(r_scores, r_route), writes=(r_route,), tag="r_l")
    em.step("dve", lambda v: v.memset(hi_sb[:], 0x3F800001), writes=(r_route,), tag="r_h")
    for it in range(31):
        em.step("dve", lambda v: v.tensor_tensor(out=mid_sb[:], in0=lo_sb[:], in1=hi_sb[:], op=OP.add),
                reads=(r_route,), writes=(r_route,), tag=f"r_m{it}")
        em.step("dve", lambda v: v.tensor_scalar(out=mid_sb[:], in0=mid_sb[:], scalar1=shift1[0:48, :],
                                                 scalar2=None, op0=OP.arith_shift_right),
                reads=(r_route, r_const), writes=(r_route,), tag=f"r_s{it}")
        em.step("dve", lambda v: v.tensor_scalar(out=mA_sb[0:48, :], in0=scores_sb[0:48, :],
                                                 scalar1=mid_sb[:].bitcast(F32), scalar2=None, op0=OP.is_ge),
                reads=(r_route, r_scores), writes=(r_route,), tag=f"r_c{it}")
        em.step("dve", lambda v: v.reduce_sum(out=cnt_sb[:], in_=mA_sb[0:48, :], axis=AX.X),
                reads=(r_route,), writes=(r_route,), tag=f"r_n{it}")
        em.step("dve", lambda v: v.tensor_scalar(out=cond_sb[:], in0=cnt_sb[:], scalar1=float(FCAP),
                                                 scalar2=None, op0=OP.is_ge),
                reads=(r_route,), writes=(r_route,), tag=f"r_d{it}")
        em.step("dve", lambda v: v.select(out=lo_sb[:], mask=cond_sb[:], on_true=mid_sb[:], on_false=lo_sb[:]),
                reads=(r_route,), writes=(r_route,), tag=f"r_a{it}")
        em.step("dve", lambda v: v.select(out=hi_sb[:], mask=cond_sb[:], on_true=hi_sb[:], on_false=mid_sb[:]),
                reads=(r_route,), writes=(r_route,), tag=f"r_b{it}")
    em.step("dve", lambda v: v.tensor_scalar(out=mA_sb[0:48, :], in0=scores_sb[0:48, :],
                                             scalar1=lo_sb[:].bitcast(F32), scalar2=None, op0=OP.is_gt),
            reads=(r_route, r_scores), writes=(r_route,), tag="r_gt")
    em.step("dve", lambda v: v.tensor_scalar(out=mB_sb[0:48, :], in0=scores_sb[0:48, :],
                                             scalar1=lo_sb[:].bitcast(F32), scalar2=None, op0=OP.is_equal),
            reads=(r_route, r_scores), writes=(r_route,), tag="r_eq")
    em.step("dve", lambda v: v.reduce_sum(out=cnt_sb[:], in_=mA_sb[0:48, :], axis=AX.X),
            reads=(r_route,), writes=(r_route,), tag="r_ng")
    em.step("dve", lambda v: v.tensor_scalar(out=rrem_sb[:], in0=cnt_sb[:], scalar1=-1.0, scalar2=float(FCAP),
                                             op0=OP.mult, op1=OP.add),
            reads=(r_route,), writes=(r_route,), tag="r_rr")
    em.step("dve", lambda v: v.tensor_tensor_scan(out=mC_sb[0:48, :], data0=mB_sb[0:48, :], data1=mB_sb[0:48, :],
                                                  initial=0.0, op0=OP.add, op1=OP.bypass),
            reads=(r_route,), writes=(r_route,), tag="r_cs")
    em.step("dve", lambda v: v.tensor_scalar(out=mC_sb[0:48, :], in0=mC_sb[0:48, :], scalar1=rrem_sb[:],
                                             scalar2=None, op0=OP.is_le),
            reads=(r_route,), writes=(r_route,), tag="r_le")
    em.step("dve", lambda v: v.tensor_tensor(out=mB_sb[0:48, :], in0=mB_sb[0:48, :], in1=mC_sb[0:48, :], op=OP.mult),
            reads=(r_route,), writes=(r_route,), tag="r_tm")
    em.step("dve", lambda v: v.tensor_tensor(out=mA_sb[0:48, :], in0=mA_sb[0:48, :], in1=mB_sb[0:48, :], op=OP.add),
            reads=(r_route,), writes=(r_route,), tag="r_M")
    em.step("dve", lambda v: v.tensor_tensor_scan(out=mC_sb[0:48, :], data0=mA_sb[0:48, :], data1=mA_sb[0:48, :],
                                                  initial=0.0, op0=OP.add, op1=OP.bypass),
            reads=(r_route,), writes=(r_route,), tag="r_ps")
    em.step("dve", lambda v: v.tensor_tensor(out=mB_sb[0:48, :], in0=mC_sb[0:48, :], in1=mA_sb[0:48, :], op=OP.mult),
            reads=(r_route,), writes=(r_route,), tag="r_mp")
    for k in range(16):
        em.step("pe", lambda t, k=k: t.transpose(pb[6][:, 0:128], mB_sb[:, k * 128:(k + 1) * 128], identf[:]),
                reads=(r_route, r_const), writes=(r_pb[6],), tag=f"r_t{k}")
        for seq in range(B_CORE):
            em.step("act", lambda s, seq=seq, k=k: s.copy(out=mpt_sb[seq][k][:], in_=pb[6][:, 32 * seq:32 * seq + FE]),
                    reads=(r_pb[6],), writes=(r_mpt[seq],), tag=f"r_e{k}_{seq}")

    oht_sb = [[sb(s4a, f"oht_{k}", [128, FCAP], BF16) for k in range(16)]] * B_CORE
    r_oht = [R("oht")] * B_CORE
    idxr_sb = sb(s4a, "idxr", [1, 2 * FCAP], F32)
    r_idxr = R("idxr")
    idxc4_sb = sb(s4a, "idxc4", [128, 4], F32)
    r_idxc4 = R("idxc4")
    for e in range(FE):
        sfx = f"ix{e}"
        for seq in range(B_CORE):
            def ohtst(v, e=e, seq=seq):
                inst = None
                for k in range(16):
                    inst = v.tensor_scalar(out=oht_sb[seq][k][:], in0=iota256[:],
                                           scalar1=mpt_sb[seq][k][:, e:e + 1],
                                           scalar2=None, op0=OP.is_equal)
                return inst
            em.step("dve", ohtst, reads=(r_mpt[seq], r_const), writes=(r_oht[seq],), tag=f"{sfx}o{seq}")

            def ixa(t, seq=seq):
                inst = None
                for k in range(16):
                    inst = t.matmul(pb[6][0:1, 0:FCAP], iota128_b[:], oht_sb[seq][k][:],
                                    start=(k == 0), stop=(k == 15))
                for k in range(16):
                    inst = t.matmul(pb[6][0:1, 256:256 + FCAP], chunkc_b[:, k:k + 1], oht_sb[seq][k][:],
                                    start=(k == 0), stop=(k == 15))
                return inst
            em.step("pe", ixa, reads=(r_oht[seq], r_const), writes=(r_pb[6],), tag=f"{sfx}i{seq}")
            em.step("dve", lambda v, seq=seq: v.tensor_scalar(out=idxr_sb[:, seq * FCAP:(seq + 1) * FCAP],
                                                              in0=pb[6][0:1, 256:512], scalar1=128.0,
                                                              scalar2=None, op0=OP.mult),
                    reads=(r_pb[6],), writes=(r_idxr,), tag=f"{sfx}s{seq}")
            em.step("dve", lambda v, seq=seq: v.tensor_tensor(out=idxr_sb[:, seq * FCAP:(seq + 1) * FCAP],
                                                              in0=idxr_sb[:, seq * FCAP:(seq + 1) * FCAP],
                                                              in1=pb[6][0:1, 0:256], op=OP.add),
                    reads=(r_idxr, r_pb[6]), writes=(r_idxr,), tag=f"{sfx}c{seq}")

        def idxtr(t):
            inst = None
            for j in range(4):
                inst = t.matmul(pb[6][0:128, 128 + j:129 + j], idxr_sb[0:1, j * 128:(j + 1) * 128],
                                onesrow_f[0:1, 0:1], start=True, stop=True)
            return inst
        em.step("pe", idxtr, reads=(r_idxr, r_const), writes=(r_pb[6],), tag=f"{sfx}t")
        em.step("act", lambda s: s.copy(out=idxc4_sb[:], in_=pb[6][0:128, 128:132]),
                reads=(r_pb[6],), writes=(r_idxc4,), tag=f"{sfx}e")
        em.step("dve", lambda v, e=e: v.tensor_copy(out=idxi_all[e][:], in_=idxc4_sb[:]),
                reads=(r_idxc4,), writes=(r_idxall[e],), tag=f"{sfx}x")
        for j in range(4):
            seq = j // 2
            em.dma("pool", lambda g, e=e, j=j, seq=seq: g.indirect_dma_start(
                out=wcol_all[e][:, j:j + 1], out_offset=None, in_=scoreT[seq][:],
                in_offset=IndirectOffsetOnAxis(idxi_all[e][:, j:j + 1], 0), element_offset=e),
                reads=(r_idxall[e], r_scoreT[seq]), writes=(r_idxall[e],), tag=f"{sfx}w{j}")

    s4a.close()
    sr.close()
    em.barrier("b4")

    s4b = ExitStack()
    wg_sb = [[sb(s4b, f"wg{b}_{k}", [128, FEXP], BF16) for k in range(DCH)] for b in range(2)]
    r_wg = [R(f"wg{b}") for b in range(2)]
    wu_sb = [sb(s4b, f"wu{k}", [128, FEXP], BF16) for k in range(DCH)]
    wd_sb = [sb(s4b, f"wd{k}", [128, 4 * D], BF16) for k in range(FEXP // 512)]
    r_wu = R("wu"); r_wd = R("wd")
    xtt_sb = [sb(s4b, f"xtt{i}", [128, D], BF16) for i in range(4)]
    r_xtt = [R(f"xtt{i}") for i in range(4)]
    xtk_sb = [sb(s4b, f"xtk{k}", [128, 2 * FCAP], BF16) for k in range(DCH)]
    r_xtk = [R(f"xtk{k}") for k in range(DCH)]
    gf_sb = [sb(s4b, f"gf{f}", [128, 2 * FCAP], BF16) for f in range(FEXP // 128)]
    r_gf = [R(f"gf{f}") for f in range(FEXP // 128)]
    uf_sb = [sb(s4b, f"uf{i}", [128, 2 * FCAP], BF16) for i in range(2)]
    r_uf = [R(f"uf{i}") for i in range(2)]
    eo_sb = [sb(s4b, f"eo{i}", [128, D], F32) for i in range(2)]
    r_eo = [R(f"eo{i}") for i in range(2)]

    def ld_wg(e):
        b = e % 2
        for k in range(DCH):
            em.dma(("sp", "act")[k % 2], lambda g, e=e, k=k, b=b: g.dma_start(
                out=wg_sb[b][k][:], in_=wgbf[e, k * 128:(k + 1) * 128, :]),
                reads=(r_wcast[e],), writes=(r_wg[b],), tag=f"f{e}a{k}")

    def ld_wu(e):
        for k in range(DCH):
            em.dma(("sp", "act")[k % 2], lambda g, e=e, k=k: g.dma_start(
                out=wu_sb[k][:], in_=wubf[e, k * 128:(k + 1) * 128, :]),
                reads=(r_wcast[e],), writes=(r_wu,), tag=f"f{e}b{k}")

    def ld_wd(e):
        for kb in range(FEXP // 512):
            em.dma(("sp", "act")[kb % 2], lambda g, e=e, kb=kb: g.dma_start(
                out=wd_sb[kb][:], in_=wdbf[e, kb]),
                reads=(r_wcast[e],), writes=(r_wd,), tag=f"f{e}c{kb}")

    def ld_gather(e):
        for j in range(4):
            seq = j // 2
            em.dma("pool", lambda g, e=e, j=j, seq=seq: g.indirect_dma_start(
                out=xtt_sb[j][:], out_offset=None, in_=h2tm[seq][:],
                in_offset=IndirectOffsetOnAxis(idxi_all[e][:, j:j + 1], 0)),
                reads=(r_idxall[e], r_h2tm[seq * 4], r_h2tm[seq * 4 + 1], r_h2tm[seq * 4 + 2], r_h2tm[seq * 4 + 3]),
                writes=(r_xtt[j],), tag=f"f{e}gx{j}")

    ld_wg(0)
    ld_wu(0)
    ld_wd(0)
    ld_gather(0)

    for e in range(FE):
        sfx = f"f{e}"
        b = e % 2
        for j in range(4):
            def trg(t, j=j):
                inst = None
                for d in range(DCH):
                    inst = t.transpose(pbb[:, d * 128:(d + 1) * 128],
                                       xtt_sb[j][:, d * 128:(d + 1) * 128], identb[:])
                return inst
            em.step("pe", trg, reads=(r_xtt[j], r_const), writes=(r_pb[7],), tag=f"{sfx}gt{j}")

            def gev(s, j=j):
                inst = None
                for d in range(DCH):
                    inst = s.copy(out=xtk_sb[d][:, j * 128:(j + 1) * 128], in_=pbb[:, d * 128:(d + 1) * 128])
                return inst
            em.step("act", gev, reads=(r_pb[7],), writes=tuple(r_xtk), tag=f"{sfx}ge{j}")
        for f in range(FEXP // 128):
            bank = f % 2

            def gm2(t, f=f, bank=bank, b=b):
                inst = None
                for k in range(DCH):
                    inst = t.matmul(pb[bank][:, :], wg_sb[b][k][:, f * 128:(f + 1) * 128], xtk_sb[k][:],
                                    start=(k == 0), stop=(k == DCH - 1))
                return inst
            em.step("pe", gm2, reads=tuple(r_xtk) + (r_wg[b],), writes=(r_pb[bank],), tag=f"{sfx}g{f}")
            em.step("act", lambda s, f=f, bank=bank: s.activation(out=gf_sb[f][:], in_=pb[bank][:, :],
                                                                  func=AF.Gelu),
                    reads=(r_pb[bank],), writes=(r_gf[f],), tag=f"{sfx}ga{f}")
        if e + 1 < FE:
            ld_wg(e + 1)
        for f in range(FEXP // 128):
            bank = 2 + f % 2

            def gm3(t, f=f, bank=bank):
                inst = None
                for k in range(DCH):
                    inst = t.matmul(pb[bank][:, :], wu_sb[k][:, f * 128:(f + 1) * 128], xtk_sb[k][:],
                                    start=(k == 0), stop=(k == DCH - 1))
                return inst
            em.step("pe", gm3, reads=tuple(r_xtk) + (r_wu,), writes=(r_pb[bank],), tag=f"{sfx}u{f}")
            em.step("dve", lambda v, f=f, bank=bank: v.tensor_copy(out=uf_sb[f % 2][:], in_=pb[bank][:, :]),
                    reads=(r_pb[bank],), writes=(r_uf[f % 2],), tag=f"{sfx}ue{f}")
            em.step("dve", lambda v, f=f: v.tensor_tensor(out=gf_sb[f][:], in0=gf_sb[f][:], in1=uf_sb[f % 2][:], op=OP.mult),
                    reads=(r_gf[f], r_uf[f % 2]), writes=(r_gf[f],), tag=f"{sfx}hh{f}")
        if e + 1 < FE:
            ld_wu(e + 1)
            ld_gather(e + 1)
        for j in range(4):
            seq = j // 2
            for dh in range(2):
                bank = 4 + dh

                def gm4(t, j=j, dh=dh, bank=bank):
                    inst = None
                    for k in range(FEXP // 128):
                        inst = t.matmul(pb[bank][:, :], gf_sb[k][:, j * 128:(j + 1) * 128],
                                        wd_sb[k // 4][:, (k % 4) * D + dh * 512:(k % 4) * D + (dh + 1) * 512],
                                        start=(k == 0), stop=(k == FEXP // 128 - 1))
                    return inst
                em.step("pe", gm4, reads=tuple(r_gf) + (r_wd,), writes=(r_pb[bank],), tag=f"{sfx}d{j}{dh}")

            def eoev(v, j=j, e=e):
                v.tensor_scalar(out=eo_sb[j % 2][:, 0:512], in0=pb[4][:, :], scalar1=wcol_all[e][:, j:j + 1],
                                scalar2=None, op0=OP.mult)
                return v.tensor_scalar(out=eo_sb[j % 2][:, 512:1024], in0=pb[5][:, :], scalar1=wcol_all[e][:, j:j + 1],
                                       scalar2=None, op0=OP.mult)
            em.step("dve", eoev, reads=(r_pb[4], r_pb[5], r_idxall[e]), writes=(r_eo[j % 2],), tag=f"{sfx}v{j}")
            em.dma("pool", lambda g, j=j, seq=seq, e=e: g.indirect_dma_start(
                out=acc2[seq][:], out_offset=IndirectOffsetOnAxis(idxi_all[e][:, j:j + 1], 0),
                in_=eo_sb[j % 2][:], in_offset=None, compute_op=OP.add),
                reads=(r_eo[j % 2], r_idxall[e]), writes=(r_acc2[seq],), tag=f"{sfx}z{j}")
        if e + 1 < FE:
            ld_wd(e + 1)
        if e < 8:
            emit_wconv(1, f"4b{e}")

    assert wconv_pos[0] == len(wconv_chunks), wconv_pos
    s4b.close()
    em.barrier("b5")

    s5 = ExitStack()
    NB5 = 4
    ff5 = [sb(s5, f"ff5_{i}", [128, D], F32) for i in range(NB5)]
    x15 = [sb(s5, f"x15_{i}", [128, D], F32) for i in range(NB5)]
    sq5 = [sb(s5, f"sq5_{i}", [128, D], F32) for i in range(NB5)]
    o5 = [sb(s5, f"o5_{i}", [128, D], F32) for i in range(NB5)]
    mu5 = [sb(s5, f"mu5_{i}", [128, 1], F32) for i in range(NB5)]
    vr5 = [sb(s5, f"vr5_{i}", [128, 1], F32) for i in range(NB5)]
    r_ff5 = [R(f"ff5_{i}") for i in range(NB5)]
    r_x15 = [R(f"x15_{i}") for i in range(NB5)]
    r_sq5 = [R(f"sq5_{i}") for i in range(NB5)]
    r_o5 = [R(f"o5_{i}") for i in range(NB5)]
    r_mu5 = [R(f"mu5_{i}") for i in range(NB5)]
    r_vr5 = [R(f"vr5_{i}") for i in range(NB5)]

    for ch in range(NTOK // 128):
        L = ch % NB5
        E = "dve"
        seq = ch // 16
        sfx = f"s5c{ch}"
        em.dma("sp", lambda e, ch=ch, seq=seq, L=L: e.dma_start(
            out=ff5[L][:], in_=acc2[seq][(ch % 16) * 128:((ch % 16) + 1) * 128, :]),
            reads=(r_acc2[seq],), writes=(r_ff5[L],), tag=f"{sfx}a")
        em.dma("act", lambda e, ch=ch, L=L: e.dma_start(out=x15[L][:], in_=x1tm[ch * 128:(ch + 1) * 128, :]),
               reads=(r_x1tm[ch // 4],), writes=(r_x15[L],), tag=f"{sfx}b")
        em.step("dve", lambda v, L=L: v.reduce_sum(out=mu5[L][:], in_=ff5[L][:], axis=AX.X),
                reads=(r_ff5[L],), writes=(r_mu5[L],), tag=f"{sfx}c")
        em.step(E, lambda v, L=L: v.tensor_scalar(out=mu5[L][:], in0=mu5[L][:], scalar1=1.0 / D,
                                                  scalar2=None, op0=OP.mult),
                reads=(r_mu5[L],), writes=(r_mu5[L],), tag=f"{sfx}d")
        em.step(E, lambda v, L=L: v.tensor_scalar(out=o5[L][:], in0=ff5[L][:], scalar1=mu5[L][:],
                                                  scalar2=None, op0=OP.subtract),
                reads=(r_ff5[L], r_mu5[L]), writes=(r_o5[L],), tag=f"{sfx}e")
        em.step("act", lambda s, L=L: s.activation(out=sq5[L][:], in_=o5[L][:], func=AF.Square),
                reads=(r_o5[L],), writes=(r_sq5[L],), tag=f"{sfx}f")
        em.step("dve", lambda v, L=L: v.reduce_sum(out=vr5[L][:], in_=sq5[L][:], axis=AX.X),
                reads=(r_sq5[L],), writes=(r_vr5[L],), tag=f"{sfx}g")
        em.step(E, lambda v, L=L: v.tensor_scalar(out=vr5[L][:], in0=vr5[L][:], scalar1=1.0 / D,
                                                  scalar2=1e-5, op0=OP.mult, op1=OP.add),
                reads=(r_vr5[L],), writes=(r_vr5[L],), tag=f"{sfx}h")
        em.step("dve", lambda v, L=L: v.reciprocal(out=vr5[L][:], in_=vr5[L][:]),
                reads=(r_vr5[L],), writes=(r_vr5[L],), tag=f"{sfx}i")
        em.step("act", lambda s, L=L: s.activation(out=vr5[L][:], in_=vr5[L][:], func=AF.Sqrt),
                reads=(r_vr5[L],), writes=(r_vr5[L],), tag=f"{sfx}j")
        em.step(E, lambda v, L=L: v.scalar_tensor_tensor(out=o5[L][:], in0=o5[L][:], scalar=vr5[L][:],
                                                         in1=x15[L][:], op0=OP.mult, op1=OP.add),
                reads=(r_o5[L], r_vr5[L], r_x15[L]), writes=(r_o5[L],), tag=f"{sfx}k")
        em.dma("pool", lambda e, ch=ch, L=L: e.dma_start(out=out[ch * 128:(ch + 1) * 128, :], in_=o5[L][:]),
               reads=(r_o5[L],), writes=(), tag=f"{sfx}n")

    s5.close()
    em.barrier("bfin")

    sems = {s: pers.enter_context(nc.semaphore(s)) for s in em.sem_names}

    def run_section(key, eng):
        for (e, waits, fn, sem, amt, tag) in em.steps:
            if e != key:
                continue
            for (s, v) in waits:
                eng.wait_ge(sems[s], v)
            if fn is not None:
                inst = fn(eng)
                if sem is not None and inst is not None:
                    inst.then_inc(sems[sem], amt)

    with nc.Block() as block:
        @block.tensor
        def _(t):
            run_section("pe", t)

        @block.scalar
        def _(s):
            run_section("act", s)

        @block.vector
        def _(v):
            run_section("dve", v)

        @block.gpsimd
        def _(g):
            run_section("pool", g)

        @block.sync
        def _(sy):
            run_section("sp", sy)

    pers.close()
    return nc


def kernel(**inputs):
    ap = {k: np.asarray(v) for k, v in inputs.items()}
    x = ap["L0_attn_x"]
    C = np.ascontiguousarray

    def colmajor(v, nch):
        return C(v.reshape(nch, 128).T)

    shared = {
        "w_inproj": C(ap["L0_attn_in_proj_w"].reshape(8, 128, 8, 512).transpose(2, 1, 0, 3).reshape(8, 128, 4096)),
        "b_inproj": C(ap["L0_attn_in_proj_b"].reshape(1, 4096)),
        "convw": C(ap["L0_attn_conv_w"].reshape(16, 128, 3).transpose(1, 0, 2).reshape(128, 48)),
        "convb": colmajor(ap["L0_attn_conv_b"], 16),
        "dtw": C(ap["L0_attn_dt_w"]),
        "bpw": C(ap["L0_attn_bp_w"]),
        "cpw": C(ap["L0_attn_cp_w"]),
        "dtb": C(ap["L0_attn_dt_b"].reshape(64, 1)),
        "bpb": C(ap["L0_attn_bp_b"].reshape(64, 1)),
        "cpb": C(ap["L0_attn_cp_b"].reshape(64, 1)),
        "s2iw": C(ap["L0_attn_s2i_w"]),
        "s2ib": C(ap["L0_attn_s2i_b"].reshape(1, 2048)),
        "dpar": colmajor(ap["L0_attn_D_param"], 16),
        "outw": C(ap["L0_attn_out_w"].reshape(4, 4, 128, 1024).transpose(0, 2, 1, 3).reshape(4, 128, 4096)),
        "outb": C(ap["L0_attn_out_b"].reshape(1, 1024)),
        "n1w": colmajor(ap["L0_attn_norm1_w"], 8),
        "n2w": colmajor(ap["L0_attn_norm2_w"], 8),
        "gatew": C(ap["L0_attn_gate_w"]),
        "gateb": C(ap["L0_attn_gate_b"].reshape(1, 4)),
        "ew1": C(ap["L0_attn_e_w1"]),
        "eb1": C(ap["L0_attn_e_b1"].reshape(4, 32, 128).transpose(2, 0, 1).reshape(128, 128)),
        "ew2": C(ap["L0_attn_e_w2"].reshape(4, 8, 4, 128, 1024).transpose(0, 1, 3, 2, 4).reshape(4, 8, 128, 4096)),
        "eb2": C(ap["L0_attn_e_b2"].reshape(4, 8, 128).transpose(2, 0, 1).reshape(128, 32)),
        "fgw": C(ap["L0_ffn_gate_w"].T),
        "wgT": C(ap["L0_ffn_Wg"].transpose(0, 2, 1)),
        "wuT": C(ap["L0_ffn_Wu"].transpose(0, 2, 1)),
        "wdT": C(ap["L0_ffn_Wd"].transpose(0, 2, 1).reshape(16, 4, 4, 128, 1024).transpose(0, 1, 3, 2, 4).reshape(16, 4, 128, 4096)),
        "wsgT": C(ap["L0_ffn_Ws_g"].T),
        "wsuT": C(ap["L0_ffn_Ws_u"].T),
        "wsdT": C(np.ascontiguousarray(ap["L0_ffn_Ws_d"].T).reshape(4, 4, 128, 1024).transpose(0, 2, 1, 3).reshape(4, 128, 4096)),
    }
    nc = build_nc()
    in_maps = []
    for c in range(NCORES):
        m = dict(shared)
        m["x0fm"] = C(x[2 * c:2 * c + 2].reshape(NTILES, NT, DCH, 128).transpose(0, 3, 2, 1)
                      .reshape(NTILES, 128, DCH * NT))
        in_maps.append(m)
    res = run_bass_kernel_spmd(nc, in_maps, core_ids=list(range(NCORES)))
    outs = [res.results[c]["out"].reshape(2, T, D) for c in range(NCORES)]
    return np.concatenate(outs, axis=0).astype(np.float32)
